# version 05
# baseline (speedup 1.0000x reference)
import os
import math
import contextlib
import numpy as np
import ml_dtypes

import concourse.bass as bass
import concourse.mybir as mybir
from concourse.bass_utils import run_bass_kernel_spmd

F32 = mybir.dt.float32
BF16 = mybir.dt.bfloat16
I32 = mybir.dt.int32
F16 = mybir.dt.float16
U8 = mybir.dt.uint8
AF = mybir.ActivationFunctionType
ALU = mybir.AluOpType

D = 4096
T = 4096
NH = 32
NKV = 8
HD = 128
NC = 32
OWN = 1024
NBLK = 8
SCALE = 1.0 / math.sqrt(HD)
EXP_BIAS = -15.0
EPS = 1e-5

ENGS = ("sync", "gpsimd", "vector", "scalar", "tensor")

DEBUG = False


def _variant_tables(q):
    A, B = q, 7 - q
    own = list(range(4 * A, 4 * A + 4)) + list(range(4 * B, 4 * B + 4))
    others = [g for g in range(32) if g not in own]
    j0 = [sum(1 for o in own if o < g) for g in others]
    return own, others, j0


class Sem:
    def __init__(self, es, nc, name):
        self.h = es.enter_context(nc.semaphore(name))
        self.v = 0


class Prog:
    def __init__(self):
        self.ops = {e: [] for e in ENGS}

    def emit(self, eng, fn):
        self.ops[eng].append(("op", fn))

    def if_open(self, v):
        for e in ENGS:
            self.ops[e].append(("if", v))

    def if_close(self):
        for e in ENGS:
            self.ops[e].append(("endif", None))

    def replay(self, eng_name, eng):
        need_if = any(k == "if" for k, _ in self.ops[eng_name])
        vmod = None
        if need_if:
            pid = eng.to_reg(eng.partition_id())
            vmod = eng.alloc_register(f"vm_{eng_name}")
            eng.reg_mod(vmod, pid, 4)
        guards = []
        for kind, payload in self.ops[eng_name]:
            if kind == "op":
                payload(eng)
            elif kind == "if":
                g = eng.If_eq(vmod, payload)
                g.__enter__()
                guards.append(g)
            else:
                guards.pop().__exit__(None, None, None)
        eng.end_ifs()


def build_nc():
    nc = bass.Bass("TRN2", target_bir_lowering=False, debug=False)

    xT = nc.dram_tensor("xT", [D, T], F32, kind="ExternalInput").ap()
    wq = nc.dram_tensor("wq", [D, D], F32, kind="ExternalInput").ap()
    wk = nc.dram_tensor("wk", [D, 1024], F32, kind="ExternalInput").ap()
    wv = nc.dram_tensor("wv", [D, 1024], F32, kind="ExternalInput").ap()
    wo = nc.dram_tensor("wo", [D, D], F32, kind="ExternalInput").ap()
    pk = nc.dram_tensor("pk", [4096, 2048], I32, kind="ExternalInput").ap()
    sc = nc.dram_tensor("sc", [4096, 128], F16, kind="ExternalInput").ap()
    mk = nc.dram_tensor("mk", [4096, 128], U8, kind="ExternalInput").ap()
    bias_in = nc.dram_tensor("bias_in", [128, 32], F32, kind="ExternalInput").ap()
    t1_in = nc.dram_tensor("t1_in", [128, T], BF16, kind="ExternalInput").ap()
    t2_in = nc.dram_tensor("t2_in", [128, T], BF16, kind="ExternalInput").ap()
    tri_in = nc.dram_tensor("tri_in", [128, 128], BF16, kind="ExternalInput").ap()
    swp_in = nc.dram_tensor("swp_in", [128, 128], BF16, kind="ExternalInput").ap()

    out = nc.dram_tensor("o", [D, OWN], F32, kind="ExternalOutput").ap()

    k_scr = nc.dram_tensor("k_scr", [1024, T], BF16, kind="Internal").ap()
    v_scr = nc.dram_tensor("v_scr", [T, 1024], BF16, kind="Internal").ap()
    h_scr = nc.dram_tensor("h_scr", [D, T], BF16, kind="Internal").ap()
    w_scr = nc.dram_tensor("w_scr", [4096, 4096], BF16, kind="Internal").ap()

    dbg = {}
    if DEBUG:
        for nm, shp in [("d_kscr", [1024, T]), ("d_vscr", [T, 1024]),
                        ("d_hscr", [D, T]), ("d_wscr", [4096, 4096]),
                        ("d_q", [D, OWN]), ("d_ctx", [D, OWN]),
                        ("d_x2", [D, OWN]), ("d_h2", [D, OWN])]:
            dbg[nm] = nc.dram_tensor(nm, shp, BF16, kind="ExternalOutput").ap()

    es = contextlib.ExitStack()
    with es:
        def sbt(name, shape, dtype):
            return es.enter_context(nc.sbuf_tensor(name, shape, dtype))

        bufa_t = sbt("bufa", [128, NC * OWN], BF16)
        bufb_t = sbt("bufb", [128, NC * OWN], BF16)
        bufa = [bufa_t[:, i * OWN:(i + 1) * OWN] for i in range(NC)]
        bufb = [bufb_t[:, i * OWN:(i + 1) * OWN] for i in range(NC)]
        WBC = 16384
        wbig = sbt("wbig", [128, WBC], BF16)

        t1b = sbt("t1b", [128, OWN], BF16)
        t2b = sbt("t2b", [128, OWN], BF16)
        kout = [sbt(f"kout{i}", [128, 512], BF16) for i in range(2)]
        s1f = sbt("s1f", [128, 512], F32)
        x2t = [sbt(f"x2t{i}", [128, 512], BF16) for i in range(2)]
        m_rep = sbt("m_rep", [128, 512], F32)
        r_rep = sbt("r_rep", [128, 512], F32)
        mrow = sbt("mrow", [1, 512], F32)
        rrow = sbt("rrow", [1, 512], F32)
        p1s = sbt("p1s", [128, 512], BF16)
        p2s = sbt("p2s", [128, 512], BF16)
        psb = [sbt(f"psb{i}", [128, OWN], BF16) for i in range(2)]
        recf = [sbt(f"recf{i}", [128, 512], F32) for i in range(2)]
        outt = [sbt(f"outt{i}", [128, 512], F32) for i in range(2)]
        pkt = [sbt(f"pkt{i}", [128, 256], I32) for i in range(2)]
        qi32 = sbt("qi32", [128, 512], I32)
        wdt = [sbt(f"wdt{i}", [128, 512], BF16) for i in range(2)]
        sct = sbt("sct", [128, 128], F16)
        mkt = sbt("mkt", [128, 128], U8)
        seff = sbt("seff", [128, 128], F32)
        tri = sbt("tri_sb", [128, 128], BF16)
        swp = sbt("swp_sb", [128, 128], BF16)
        kraw = [sbt(f"kraw{i}", [128, 512], BF16) for i in range(2)]
        ones_bf = sbt("ones_bf", [128, 1], BF16)
        ones_f = sbt("ones_f", [1, 128], F32)
        b15 = sbt("b15", [128, 1], F32)
        beps = sbt("beps", [1, 1], F32)
        bias_sb = sbt("bias_sb", [128, 32], F32)
        den_sb = sbt("den_sb", [1, OWN], F32)

        ps1 = es.enter_context(nc.psum_tensor("ps1", [128, 512], F32))
        ps2 = es.enter_context(nc.psum_tensor("ps2", [128, 512], F32))
        ps3 = es.enter_context(nc.psum_tensor("ps3", [128, 1024], F32))
        ps4 = es.enter_context(nc.psum_tensor("ps4", [128, 1024], F32))
        psd = es.enter_context(nc.psum_tensor("psd", [1, 1024], F32))

        s_x = Sem(es, nc, "s_x")
        s_wt = Sem(es, nc, "s_wt")
        s_tb = Sem(es, nc, "s_tb")
        s_mm = Sem(es, nc, "s_mm")
        s_ct = Sem(es, nc, "s_ct")
        s_sq = Sem(es, nc, "s_sq")
        s_st = Sem(es, nc, "s_st")
        s_h1 = Sem(es, nc, "s_h1")
        s_ev = Sem(es, nc, "s_ev")
        s_ex = Sem(es, nc, "s_ex")
        s_mk = Sem(es, nc, "s_mk")
        s_spk = Sem(es, nc, "s_spk")
        s_sph = Sem(es, nc, "s_sph")
        s_kv = Sem(es, nc, "s_kv")
        s_dq = Sem(es, nc, "s_dq")
        s_out = Sem(es, nc, "s_out")
        s_ph = Sem(es, nc, "s_ph")
        s_wt2 = Sem(es, nc, "s_wt2")
        s_mm2 = Sem(es, nc, "s_mm2")
        s_ev2 = Sem(es, nc, "s_ev2")
        s_x2 = Sem(es, nc, "s_x2")
        s_sq2 = Sem(es, nc, "s_sq2")
        s_st2 = Sem(es, nc, "s_st2")
        s_h2 = Sem(es, nc, "s_h2")
        s_pk = Sem(es, nc, "s_pk")

        ALL_SEMS = [s_x, s_wt, s_tb, s_mm, s_ct, s_sq, s_st, s_h1, s_ev,
                    s_ex, s_mk, s_spk, s_sph, s_kv, s_dq, s_out, s_ph,
                    s_wt2, s_mm2, s_ev2, s_x2, s_sq2, s_st2, s_h2, s_pk]

        P = Prog()

        def drain_waits(eng, vals):
            for sem, v in vals:
                if v > 0:
                    eng.wait_ge(sem.h, v)

        def bar():
            tgt = s_ph.v + 5
            vals = [(sem, sem.v) for sem in ALL_SEMS if sem is not s_ph]
            for e in ENGS:
                def f(eng, tgt=tgt, vals=vals):
                    drain_waits(eng, vals)
                    eng.nop().then_inc(s_ph.h, 1)
                    eng.wait_ge(s_ph.h, tgt)
                P.emit(e, f)
            s_ph.v = tgt

        def xb(p, c):
            return bufa_t[:, c * OWN + p * 512: c * OWN + (p + 1) * 512]

        def xgrp_ap(p, g):
            return bass.AP(bufa_t[:].tensor, (g * 4) * OWN + p * 512,
                           [[NC * OWN, 128], [OWN, 4], [1, 512]])

        def f(e):
            e.dma_start(out=tri[:], in_=tri_in).then_inc(s_x.h, 16)
            e.dma_start(out=swp[:], in_=swp_in).then_inc(s_x.h, 16)
            e.dma_start(out=bias_sb[:], in_=bias_in).then_inc(s_x.h, 16)
            e.wait_ge(s_x.h, 48)
        P.emit("sync", f)
        s_x.v += 48

        def f(e):
            e.memset(ones_bf[:], 1.0)
            e.memset(ones_f[:], 1.0)
            e.memset(b15[:], EXP_BIAS)
            e.memset(beps[:], EPS).then_inc(s_st.h, 1)
        P.emit("vector", f)
        s_st.v += 1
        bar()

        def ln_stats(tiles, xsem, xval, smm, ssq, sst):
            sq0 = ssq.v
            mm0 = smm.v

            def f_sq(e, sq0=sq0, mm0=mm0):
                e.wait_ge(xsem.h, xval)
                for c in range(NC):
                    if c >= 2:
                        e.wait_ge(smm.h, mm0 + c - 1)
                    e.activation(x2t[c % 2][:], tiles(c), AF.Square).then_inc(ssq.h, 1)
            P.emit("scalar", f_sq)
            ssq.v += NC

            def f_mm(e, sq0=sq0, stv0=sst.v):
                e.wait_ge(xsem.h, xval)
                if stv0 > 0:
                    e.wait_ge(sst.h, stv0)
                for c in range(NC):
                    e.matmul(psd[:1, 0:512], ones_bf[:], tiles(c),
                             start=(c == 0), stop=(c == NC - 1))
                    e.wait_ge(ssq.h, sq0 + c + 1)
                    e.matmul(psd[:1, 512:1024], ones_bf[:], x2t[c % 2][:],
                             start=(c == 0), stop=(c == NC - 1)).then_inc(smm.h, 1)
            P.emit("tensor", f_mm)
            smm.v += NC

            def f_vs(e, mmv=smm.v):
                e.wait_ge(smm.h, mmv)
                e.tensor_scalar(mrow[:], psd[:1, 0:512], 1.0 / D, None, ALU.mult)
                e.tensor_scalar(rrow[:], psd[:1, 512:1024], 1.0 / D, None, ALU.mult)
                e.tensor_tensor(s1f[:1, :], mrow[:], mrow[:], ALU.mult)
                e.tensor_tensor(rrow[:], rrow[:], s1f[:1, :], ALU.subtract).then_inc(sst.h, 1)
            P.emit("vector", f_vs)
            sst.v += 1

            def f_sqrt(e, stv=sst.v):
                e.wait_ge(sst.h, stv)
                e.activation(rrow[:], rrow[:], AF.Sqrt, bias=beps[:]).then_inc(ssq.h, 1)
            P.emit("scalar", f_sqrt)
            ssq.v += 1

            def f_rec(e, sqv=ssq.v):
                e.wait_ge(ssq.h, sqv)
                e.reciprocal(rrow[:], rrow[:]).then_inc(sst.h, 1)
            P.emit("vector", f_rec)
            sst.v += 1

            def f_bc(e, stv=sst.v, evv=s_ev.v):
                e.wait_ge(sst.h, stv)
                if evv > 0:
                    e.wait_ge(s_ev.h, evv)
                e.matmul(ps1[:], ones_f[:], mrow[:], start=True, stop=True)
                e.matmul(ps2[:], ones_f[:], rrow[:], start=True, stop=True).then_inc(smm.h, 1)
            P.emit("tensor", f_bc)
            smm.v += 1

            def f_cp(e, mmv=smm.v):
                e.wait_ge(smm.h, mmv)
                e.copy(m_rep[:], ps1[:])
                e.copy(r_rep[:], ps2[:]).then_inc(ssq.h, 1)
            P.emit("scalar", f_cp)
            ssq.v += 1

        def wks(s, c):
            return wbig[:, s * 4096 + c * 128: s * 4096 + (c + 1) * 128]

        def wvs(c):
            return wbig[:, c * 512:(c + 1) * 512]

        def wqs(par, c):
            return wbig[:, par * 4096 + c * 128: par * 4096 + (c + 1) * 128]

        def pass_a(half):
            wt0 = s_wt.v

            def f_ldwk(e, half=half):
                for c in range(NC):
                    src = wk[c * 128:(c + 1) * 128,
                             half * 512:(half + 1) * 512].rearrange("p (s o) -> p s o", s=4)
                    dst = bass.AP(wbig[:].tensor, c * 128,
                                  [[WBC, 128], [4096, 4], [1, 128]])
                    e.dma_start(out=dst, in_=src).then_inc(s_wt.h, 16)
            P.emit("gpsimd", f_ldwk)
            s_wt.v += 16 * NC

            sph_end = {}
            for blk in range(NBLK):
                par = blk % 2

                if half == 0:
                    def f_ldx(e, blk=blk, par=par, spv=sph_end.get(blk - 2, 0)):
                        if spv > 0:
                            e.wait_ge(s_sph.h, spv)
                        for g in range(8):
                            src = xT.rearrange("(c p) t -> p c t", p=128)[
                                :, g * 4:(g + 1) * 4, blk * 512:(blk + 1) * 512]
                            e.dma_start(out=xgrp_ap(par, g), in_=src).then_inc(s_x.h, 16)
                    P.emit("gpsimd", f_ldx)
                    s_x.v += 16 * 8

                    ln_stats(lambda c, par=par: xb(par, c), s_x, s_x.v,
                             s_mm, s_sq, s_st)

                    def f_h1(e, par=par, sqv=s_sq.v):
                        e.wait_ge(s_sq.h, sqv)
                        for c in range(NC):
                            e.tensor_tensor(s1f[:], xb(par, c), m_rep[:], ALU.subtract)
                            e.tensor_tensor(xb(par, c), s1f[:], r_rep[:], ALU.mult).then_inc(s_h1.h, 1)
                    P.emit("vector", f_h1)
                    s_h1.v += NC

                    if blk < 2:
                        def f_own(e, par=par, blk=blk, h1v=s_h1.v):
                            for c in range(NC):
                                e.wait_ge(s_h1.h, h1v - NC + c + 1)
                                ins = e.copy(bufb[c][:, blk * 512:(blk + 1) * 512], xb(par, c))
                            ins.then_inc(s_sq.h, 1)
                        P.emit("scalar", f_own)
                        s_sq.v += 1
                else:
                    def f_ldh(e, blk=blk, par=par, mmv=s_mm.v, sphv=s_sph.v):
                        e.wait_ge(s_mm.h, mmv)
                        e.wait_ge(s_sph.h, sphv)
                        for g in range(8):
                            src = h_scr.rearrange("(c p) t -> p c t", p=128)[
                                :, g * 4:(g + 1) * 4, blk * 512:(blk + 1) * 512]
                            e.dma_start(out=xgrp_ap(par, g), in_=src).then_inc(s_x.h, 16)
                    P.emit("sync", f_ldh)
                    s_x.v += 16 * 8

                def f_ldt(e, blk=blk):
                    e.dma_start(out=t1b[:, 0:512],
                                in_=t1_in[:, blk * 512:(blk + 1) * 512]).then_inc(s_tb.h, 16)
                    e.dma_start(out=t2b[:, 0:512],
                                in_=t2_in[:, blk * 512:(blk + 1) * 512]).then_inc(s_tb.h, 16)
                P.emit("sync", f_ldt)
                s_tb.v += 32

                kmm0 = s_mm.v
                h1v = s_h1.v
                xv = s_x.v
                kev_base = s_ev.v
                ksq0 = s_sq.v

                def f_kmm(e, par=par, wt0=wt0, h1v=h1v, xv=xv,
                          ksq0=ksq0, half=half, kev_base=kev_base):
                    e.wait_ge(s_wt.h, wt0 + 16 * NC)
                    if half == 0:
                        e.wait_ge(s_h1.h, h1v)
                    else:
                        e.wait_ge(s_x.h, xv)
                    for s in range(4):
                        tgt = ksq0 + s - 1
                        if tgt > 0:
                            e.wait_ge(s_sq.h, tgt)
                        pspc = [ps3[:, 0:512], ps4[:, 0:512]][s % 2]
                        for c in range(NC):
                            ins = e.matmul(pspc, wks(s, c), xb(par, c),
                                           start=(c == 0), stop=(c == NC - 1))
                        ins.then_inc(s_mm.h, 1)
                    for s in range(4):
                        e.wait_ge(s_sq.h, ksq0 + s + 1)
                        if kev_base + s > 0:
                            e.wait_ge(s_ev.h, kev_base + s)
                        e.matmul(ps1[:], swp[:], kraw[s % 2][:],
                                 start=True, stop=True).then_inc(s_mm.h, 1)
                P.emit("tensor", f_kmm)
                s_mm.v += 8

                def f_kev(e, kmm0=kmm0, ksq0=ksq0, kev_base=kev_base):
                    for s in range(4):
                        pspc = [ps3[:, 0:512], ps4[:, 0:512]][s % 2]
                        e.wait_ge(s_mm.h, kmm0 + s + 1)
                        if kev_base + s - 1 > 0:
                            e.wait_ge(s_ev.h, kev_base + s - 1)
                        e.copy(kraw[s % 2][:], pspc).then_inc(s_sq.h, 1)
                P.emit("scalar", f_kev)
                s_sq.v += 4

                kev0 = s_ev.v

                def f_kro(e, kmm0=kmm0, tbv=s_tb.v, spk0=s_spk.v):
                    e.wait_ge(s_tb.h, tbv)
                    for s in range(4):
                        e.wait_ge(s_mm.h, kmm0 + 4 + s + 1)
                        tgt = spk0 + 16 * (s - 1)
                        if tgt > 0:
                            e.wait_ge(s_spk.h, tgt)
                        ko = kout[s % 2]
                        e.tensor_tensor(p1s[:], kraw[s % 2][:], t1b[:, 0:512], ALU.mult)
                        e.tensor_tensor(p2s[:], ps1[:], t2b[:, 0:512], ALU.mult)
                        e.tensor_tensor(ko[:], p1s[:], p2s[:], ALU.add).then_inc(s_ev.h, 1)
                P.emit("vector", f_kro)
                s_ev.v += 4

                def f_kst(e, kev0=kev0, blk=blk, half=half):
                    for s in range(4):
                        kvh = half * 4 + s
                        e.wait_ge(s_ev.h, kev0 + s + 1)
                        e.dma_start(out=k_scr[kvh * 128:(kvh + 1) * 128,
                                              blk * 512:(blk + 1) * 512],
                                    in_=kout[s % 2][:]).then_inc(s_spk.h, 16)
                P.emit("sync", f_kst)
                s_spk.v += 16 * 4

                if half == 0:
                    def f_sp(e, par=par, blk=blk, mmv=s_mm.v):
                        e.wait_ge(s_mm.h, mmv)
                        for g in range(8):
                            dst = h_scr.rearrange("(c p) t -> p c t", p=128)[
                                :, g * 4:(g + 1) * 4, blk * 512:(blk + 1) * 512]
                            e.dma_start(out=dst, in_=xgrp_ap(par, g)).then_inc(s_sph.h, 16)
                    P.emit("gpsimd", f_sp)
                    s_sph.v += 16 * 8
                    sph_end[blk] = s_sph.v

            bar()

        def pass_b(vh):
            def f_ldwv(e, vh=vh):
                for c in range(NC):
                    e.dma_start(out=wvs(c),
                                in_=wv[c * 128:(c + 1) * 128,
                                       vh * 512:(vh + 1) * 512]).then_inc(s_wt.h, 16)
            P.emit("gpsimd", f_ldwv)
            wvdone = s_wt.v + 16 * NC
            s_wt.v += 16 * NC

            for blk in range(NBLK):
                par = blk % 2

                def f_ldh(e, blk=blk, par=par, mmv=s_mm.v, sphv=s_sph.v):
                    e.wait_ge(s_mm.h, mmv)
                    e.wait_ge(s_sph.h, sphv)
                    for g in range(8):
                        src = h_scr.rearrange("(c p) t -> p c t", p=128)[
                            :, g * 4:(g + 1) * 4, blk * 512:(blk + 1) * 512]
                        e.dma_start(out=xgrp_ap(par, g), in_=src).then_inc(s_x.h, 16)
                P.emit("sync", f_ldh)
                s_x.v += 16 * 8

                vmm0 = s_mm.v

                def f_vmm(e, par=par, xv=s_x.v, wvdone=wvdone, ev0=s_ev.v):
                    e.wait_ge(s_wt.h, wvdone)
                    e.wait_ge(s_x.h, xv)
                    for tc in range(4):
                        tgt = ev0 + tc - 1
                        if tgt > 0:
                            e.wait_ge(s_ev.h, tgt)
                        pspc = [ps3[:, 0:512], ps4[:, 0:512]][tc % 2]
                        for c in range(NC):
                            ins = e.matmul(pspc, xb(par, c)[:, tc * 128:(tc + 1) * 128],
                                           wvs(c), start=(c == 0), stop=(c == NC - 1))
                        ins.then_inc(s_mm.h, 1)
                P.emit("tensor", f_vmm)
                s_mm.v += 4

                vev0 = s_ev.v

                def f_vev(e, vmm0=vmm0, spk0=s_spk.v):
                    for tc in range(4):
                        pspc = [ps3[:, 0:512], ps4[:, 0:512]][tc % 2]
                        e.wait_ge(s_mm.h, vmm0 + tc + 1)
                        tgt = spk0 + 16 * (tc - 1)
                        if tgt > 0:
                            e.wait_ge(s_spk.h, tgt)
                        e.tensor_copy(kout[tc % 2][:], pspc).then_inc(s_ev.h, 1)
                P.emit("vector", f_vev)
                s_ev.v += 4

                def f_vst(e, vev0=vev0, blk=blk, vh=vh):
                    for tc in range(4):
                        e.wait_ge(s_ev.h, vev0 + tc + 1)
                        e.dma_start(out=v_scr[blk * 512 + tc * 128: blk * 512 + (tc + 1) * 128,
                                              vh * 512:(vh + 1) * 512],
                                    in_=kout[tc % 2][:]).then_inc(s_spk.h, 16)
                P.emit("gpsimd", f_vst)
                s_spk.v += 16 * 4

            bar()

        def p3():
            def f_ldt(e):
                e.dma_start(out=t1b[:], in_=t1_in[:, 0:OWN]).then_inc(s_tb.h, 16)
                e.dma_start(out=t2b[:], in_=t2_in[:, 0:OWN]).then_inc(s_tb.h, 16)
            P.emit("sync", f_ldt)
            s_tb.v += 32

            qmm_end = {}
            for qh in range(NH):
                par = qh % 2

                def f_ldq(e, qh=qh, par=par, mmv=qmm_end.get(qh - 2, s_mm.v)):
                    e.wait_ge(s_mm.h, mmv)
                    for g in range(8):
                        src = wq.rearrange("(c p) o -> p c o", p=128)[
                            :, g * 4:(g + 1) * 4, qh * 128:(qh + 1) * 128]
                        dst = bass.AP(wbig[:].tensor, par * 4096 + (g * 4) * 128,
                                      [[WBC, 128], [128, 4], [1, 128]])
                        e.dma_start(out=dst, in_=src).then_inc(s_wt.h, 16)
                P.emit("gpsimd", f_ldq)
                s_wt.v += 16 * 8

                qmm0 = s_mm.v
                qsq0 = s_sq.v
                qev0 = s_ev.v

                def f_qmm(e, qh=qh, par=par, wtv=s_wt.v, qsq0=qsq0, qev0=qev0):
                    e.wait_ge(s_wt.h, wtv)
                    for hf in range(2):
                        tgt = qsq0 + hf - 1
                        if tgt > 0:
                            e.wait_ge(s_sq.h, tgt)
                        pspc = [ps3[:, 0:512], ps4[:, 0:512]][hf]
                        for c in range(NC):
                            ins = e.matmul(pspc, wqs(par, c),
                                           bufb[c][:, hf * 512:(hf + 1) * 512],
                                           start=(c == 0), stop=(c == NC - 1))
                        ins.then_inc(s_mm.h, 1)
                    for hf in range(2):
                        e.wait_ge(s_sq.h, qsq0 + hf + 1)
                        tgt = qev0 + hf
                        if tgt > 0:
                            e.wait_ge(s_ev.h, tgt)
                        e.matmul(ps1[:], swp[:], kraw[hf][:],
                                 start=True, stop=True).then_inc(s_mm.h, 1)
                P.emit("tensor", f_qmm)
                s_mm.v += 4
                qmm_end[qh] = s_mm.v

                def f_qev(e, qmm0=qmm0, qev0=qev0):
                    for hf in range(2):
                        pspc = [ps3[:, 0:512], ps4[:, 0:512]][hf]
                        e.wait_ge(s_mm.h, qmm0 + hf + 1)
                        tgt = qev0 + hf - 1
                        if tgt > 0:
                            e.wait_ge(s_ev.h, tgt)
                        e.copy(kraw[hf][:], pspc).then_inc(s_sq.h, 1)
                P.emit("scalar", f_qev)
                s_sq.v += 2

                def f_qro(e, qh=qh, qmm0=qmm0, tbv=s_tb.v):
                    e.wait_ge(s_tb.h, tbv)
                    for hf in range(2):
                        e.wait_ge(s_mm.h, qmm0 + 2 + hf + 1)
                        qt = bufa[qh][:, hf * 512:(hf + 1) * 512]
                        e.tensor_tensor(p1s[:], kraw[hf][:], t1b[:, hf * 512:(hf + 1) * 512], ALU.mult)
                        e.tensor_tensor(p2s[:], ps1[:], t2b[:, hf * 512:(hf + 1) * 512], ALU.mult)
                        e.tensor_tensor(qt[:], p1s[:], p2s[:], ALU.add).then_inc(s_ev.h, 1)
                P.emit("vector", f_qro)
                s_ev.v += 2

            bar()

        def kbuf(par):
            return wbig[:, par * 4096:(par + 1) * 4096]

        def vbuf(par):
            return wbig[:, 8192 + par * 4096: 8192 + (par + 1) * 4096]

        def p4_variant(q):
            own, others, j0s = _variant_tables(q)
            visits = [(l, l, True) for l in range(8)]
            visits += [(8 + t, j0, False) for t, j0 in enumerate(j0s) if j0 < 8]
            nvis = len(visits)
            lastA = max(vi for vi, (_, j0, _) in enumerate(visits) if j0 < 4)
            lastB = nvis - 1
            nmask_thru = []
            nm = 0
            for (_, _, dg) in visits:
                if dg:
                    nm += 1
                nmask_thru.append(nm)

            kvh_ct = {}
            for kvh in range(NKV):
                par = kvh % 2

                def f_ldkv(e, kvh=kvh, par=par, ctv=kvh_ct.get(kvh - 2, 0),
                           spkv=s_spk.v):
                    e.wait_ge(s_spk.h, spkv)
                    if kvh >= 2 and ctv > 0:
                        e.wait_ge(s_ct.h, ctv)
                    e.dma_start(out=kbuf(par),
                                in_=k_scr[kvh * 128:(kvh + 1) * 128, :]).then_inc(s_kv.h, 16)
                    for g in range(8):
                        src = v_scr.rearrange("(c p) o -> p c o", p=128)[
                            :, g * 4:(g + 1) * 4, kvh * 128:(kvh + 1) * 128]
                        dst = bass.AP(wbig[:].tensor, 8192 + par * 4096 + g * 512,
                                      [[WBC, 128], [128, 4], [1, 128]])
                        e.dma_start(out=dst, in_=src).then_inc(s_kv.h, 16)
                P.emit("sync", f_ldkv)
                s_kv.v += 16 * 9
                kv_now = s_kv.v

                for j2 in range(4):
                    qh = kvh * 4 + j2
                    smm0 = s_mm.v
                    sct0 = s_ct.v
                    sex0 = s_ex.v
                    smk0 = s_mk.v
                    sq_pre = s_sq.v
                    sev_pre = s_ev.v

                    def f_mm(e, par=par, qh=qh, visits=visits, smm0=smm0,
                             sct0=sct0, sex0=sex0, smk0=smk0,
                             nmask_thru=nmask_thru, lastA=lastA, lastB=lastB,
                             kv_now=kv_now, sq_pre=sq_pre, sev_pre=sev_pre,
                             nvis=nvis):
                        e.wait_ge(s_kv.h, kv_now)
                        if sq_pre > 0:
                            e.wait_ge(s_sq.h, sq_pre)
                        if sev_pre > 0:
                            e.wait_ge(s_ev.h, sev_pre)
                        started = {0: False, 512: False}

                        def ctx_of(vi):
                            lk_, j0_, dg_ = visits[vi]
                            e.wait_ge(s_ex.h, sex0 + vi + 1)
                            if dg_:
                                e.wait_ge(s_mk.h, smk0 + nmask_thru[vi])
                            vt = vbuf(par)[:, lk_ * 128:(lk_ + 1) * 128]
                            pt = psb[vi % 2]
                            c0 = j0_ * 128
                            ins = None
                            for cs in (0, 512):
                                cc = max(cs, c0)
                                if cc >= cs + 512:
                                    continue
                                bank = ps1 if cs == 0 else ps2
                                first = not started[cs]
                                lastv = lastA if cs == 0 else lastB
                                e.matmul(bank[:, cc - cs:512], vt, pt[:, cc:cs + 512],
                                         start=first, stop=(vi == lastv))
                                ins = e.matmul(psd[:1, cc:cs + 512], ones_bf[:],
                                               pt[:, cc:cs + 512],
                                               start=first, stop=(vi == lastv))
                                started[cs] = True
                            ins.then_inc(s_ct.h, 1)

                        for vi, (lk, j0, dg) in enumerate(visits):
                            if vi >= 2:
                                e.wait_ge(s_ex.h, sex0 + vi - 1)
                            sps = [ps3, ps4][vi % 2]
                            kt = kbuf(par)[:, lk * 128:(lk + 1) * 128]
                            c0 = j0 * 128
                            ins = None
                            for cs in (0, 512):
                                cc = max(cs, c0)
                                if cc >= cs + 512:
                                    continue
                                ins = e.matmul(sps[:, cc:cs + 512], kt,
                                               bufa[qh][:, cc:cs + 512],
                                               start=True, stop=True)
                            ins.then_inc(s_mm.h, 1)
                            if vi >= 1:
                                ctx_of(vi - 1)
                        ctx_of(nvis - 1)
                    P.emit("tensor", f_mm)
                    s_mm.v += nvis
                    s_ct.v += nvis

                    def f_exp(e, visits=visits, smm0=smm0, sct0=sct0, sex0=sex0):
                        for vi, (lk, j0, dg) in enumerate(visits):
                            e.wait_ge(s_mm.h, smm0 + vi + 1)
                            if vi >= 2 and sct0 + vi - 1 > 0:
                                e.wait_ge(s_ct.h, sct0 + vi - 1)
                            sps = [ps3, ps4][vi % 2]
                            c0 = j0 * 128
                            e.activation(psb[vi % 2][:, c0:OWN], sps[:, c0:OWN],
                                         AF.Exp, bias=b15[:], scale=SCALE).then_inc(s_ex.h, 1)
                    P.emit("scalar", f_exp)
                    s_ex.v += nvis

                    def f_msk(e, visits=visits, sex0=sex0):
                        for vi, (lk, j0, dg) in enumerate(visits):
                            if not dg:
                                continue
                            e.wait_ge(s_ex.h, sex0 + vi + 1)
                            c0 = j0 * 128
                            e.tensor_tensor(psb[vi % 2][:, c0:c0 + 128],
                                            psb[vi % 2][:, c0:c0 + 128],
                                            tri[:], ALU.mult).then_inc(s_mk.h, 1)
                    P.emit("vector", f_msk)
                    s_mk.v += nmask_thru[-1]

                    def f_den(e, ctv=s_ct.v):
                        e.wait_ge(s_ct.h, ctv)
                        e.copy(den_sb[:1, :], psd[:1, :]).then_inc(s_sq.h, 1)
                    P.emit("scalar", f_den)
                    s_sq.v += 1

                    def f_bc(e, sqv=s_sq.v):
                        e.wait_ge(s_sq.h, sqv)
                        e.matmul(ps3[:, 0:512], ones_f[:], den_sb[:1, 0:512],
                                 start=True, stop=True)
                        e.matmul(ps4[:, 0:512], ones_f[:], den_sb[:1, 512:1024],
                                 start=True, stop=True).then_inc(s_mm.h, 1)
                    P.emit("tensor", f_bc)
                    s_mm.v += 1

                    def f_ev(e, qh=qh, mmv=s_mm.v):
                        e.wait_ge(s_mm.h, mmv)
                        e.reciprocal(recf[0][:], ps3[:, 0:512])
                        e.reciprocal(recf[1][:], ps4[:, 0:512]).then_inc(s_ev.h, 1)
                        e.tensor_tensor(bufb[qh][:, 0:512], ps1[:], recf[0][:], ALU.mult)
                        e.tensor_tensor(bufb[qh][:, 512:1024], ps2[:], recf[1][:], ALU.mult).then_inc(s_ev.h, 2)
                    P.emit("vector", f_ev)
                    s_ev.v += 3

                kvh_ct[kvh] = s_ct.v

            vals = [(sem, sem.v) for sem in ALL_SEMS if sem is not s_ph]
            for e in ENGS:
                def f(eng, vals=vals):
                    drain_waits(eng, vals)
                    eng.nop().then_inc(s_ph.h, 1)
                P.emit(e, f)

        def p4():
            tgt = s_ph.v + 5
            for q in range(4):
                snap = [s.v for s in ALL_SEMS]
                P.if_open(q)
                p4_variant(q)
                P.if_close()
                for s, v0 in zip(ALL_SEMS, snap):
                    s.v = v0
            s_ph.v = tgt
            for e in ENGS:
                def f(eng, tgt=tgt):
                    eng.wait_ge(s_ph.h, tgt)
                P.emit(e, f)

        def wos(par, c):
            return wbig[:, 8192 + par * 4096 + c * 128: 8192 + par * 4096 + (c + 1) * 128]

        wt_end = {}

        def p_wo():
            wmm_end = {}
            for och in range(NH):
                par = och % 2

                def f_ld(e, och=och, par=par, mmv=wmm_end.get(och - 2, 0)):
                    if mmv > 0:
                        e.wait_ge(s_mm2.h, mmv)
                    for g in range(8):
                        src = wo.rearrange("(c p) o -> p c o", p=128)[
                            :, g * 4:(g + 1) * 4, och * 128:(och + 1) * 128]
                        dst = bass.AP(wbig[:].tensor, 8192 + par * 4096 + (g * 4) * 128,
                                      [[WBC, 128], [128, 4], [1, 128]])
                        e.dma_start(out=dst, in_=src).then_inc(s_wt2.h, 16)
                    e.dma_start(out=bufa[och],
                                in_=xT[och * 128:(och + 1) * 128, 0:OWN]).then_inc(s_wt2.h, 16)
                P.emit("gpsimd", f_ld)
                s_wt2.v += 16 * 9

                mm0 = s_mm2.v

                def f_mm(e, och=och, par=par, wtv=s_wt2.v, ev0=s_ev2.v):
                    e.wait_ge(s_wt2.h, wtv)
                    for hf in range(2):
                        tgt = ev0 + hf - 1
                        if tgt > 0:
                            e.wait_ge(s_ev2.h, tgt)
                        pspc = [ps1, ps2][hf]
                        for c in range(NC):
                            ins = e.matmul(pspc[:], wos(par, c),
                                           bufb[c][:, hf * 512:(hf + 1) * 512],
                                           start=(c == 0), stop=(c == NC - 1))
                        ins.then_inc(s_mm2.h, 1)
                P.emit("tensor", f_mm)
                s_mm2.v += 2
                wmm_end[och] = s_mm2.v

                def f_ev(e, och=och, mm0=mm0):
                    for hf in range(2):
                        pspc = [ps1, ps2][hf]
                        e.wait_ge(s_mm2.h, mm0 + hf + 1)
                        sl = bufa[och][:, hf * 512:(hf + 1) * 512]
                        e.tensor_tensor(sl, sl, pspc[:], ALU.add).then_inc(s_ev2.h, 1)
                P.emit("vector", f_ev)
                s_ev2.v += 2

                p6_och(och)
                if och == 1:
                    def f_ldw0(e, spkv=s_spk.v, wtv0=s_wt2.v):
                        e.wait_ge(s_spk.h, spkv)
                        for ic in range(NC):
                            dst = wbig[:, ic * 256:(ic + 1) * 256]
                            e.dma_start_transpose(dst,
                                                  w_scr[0:256, ic * 128:(ic + 1) * 128]).then_inc(s_wt2.h, 16)
                    P.emit("sync", f_ldw0)
                    s_wt2.v += 16 * NC
                    wt_end[0] = s_wt2.v

            bar()

        def p5():
            for hf in range(2):
                ln_stats(lambda c, hf=hf: bufa[c][:, hf * 512:(hf + 1) * 512],
                         s_ph, s_ph.v, s_mm2, s_sq2, s_st2)

                def f_h2(e, hf=hf, sqv=s_sq2.v):
                    e.wait_ge(s_sq2.h, sqv)
                    for c in range(NC):
                        sl_in = bufa[c][:, hf * 512:(hf + 1) * 512]
                        sl_out = bufb[c][:, hf * 512:(hf + 1) * 512]
                        e.tensor_tensor(s1f[:], sl_in, m_rep[:], ALU.subtract)
                        e.tensor_tensor(sl_out, s1f[:], r_rep[:], ALU.mult).then_inc(s_h2.h, 1)
                P.emit("vector", f_h2)
                s_h2.v += NC

            bar()

        NSUB = 8

        def p6_och(och):
            def f_lds(e, och=och):
                e.dma_start(out=sct[:], in_=sc[och * 128:(och + 1) * 128, :]).then_inc(s_x2.h, 16)
                e.dma_start(out=mkt[:], in_=mk[och * 128:(och + 1) * 128, :]).then_inc(s_x2.h, 16)
            P.emit("sync", f_lds)
            s_x2.v += 32

            def f_se(e, xv=s_x2.v, dqv=s_dq.v):
                e.wait_ge(s_x2.h, xv)
                if dqv > 0:
                    e.wait_ge(s_dq.h, dqv)
                e.tensor_scalar(s1f[:, 0:128], mkt[:], 0.5, 0.5, ALU.mult, ALU.add)
                e.tensor_tensor(seff[:], sct[:], s1f[:, 0:128], ALU.mult).then_inc(s_st2.h, 1)
            P.emit("vector", f_se)
            s_st2.v += 1
            se_done = s_st2.v

            for sub in range(NSUB):
                par = sub % 2
                i = och * NSUB + sub

                def f_ldp(e, och=och, sub=sub, par=par, i=i, dqv=s_dq.v):
                    if i >= 2:
                        e.wait_ge(s_dq.h, dqv - 1)
                    e.dma_start(out=pkt[par][:],
                                in_=pk[och * 128:(och + 1) * 128,
                                       sub * 256:(sub + 1) * 256]).then_inc(s_pk.h, 16)
                P.emit("sync", f_ldp)
                s_pk.v += 16

                def f_nib(e, par=par, xv=s_pk.v, dqv=s_dq.v):
                    e.wait_ge(s_pk.h, xv)
                    if dqv >= 2:
                        e.wait_ge(s_dq.h, dqv)
                    e.tensor_scalar(qi32[:, 0:512:2], pkt[par][:], 15, None, ALU.bitwise_and)
                    e.tensor_scalar(qi32[:, 1:512:2], pkt[par][:], 4, None,
                                    ALU.logical_shift_right).then_inc(s_dq.h, 1)
                P.emit("vector", f_nib)
                s_dq.v += 1

                def f_stt(e, sub=sub, par=par, i=i, dqv=s_dq.v,
                          se_done=se_done, spv=s_spk.v):
                    e.wait_ge(s_dq.h, dqv)
                    e.wait_ge(s_st2.h, se_done)
                    if i >= 2:
                        e.wait_ge(s_spk.h, spv - 16)
                    sl = bass.AP(seff[:].tensor, seff[:].offset + sub * 16,
                                 [[128, 128], [1, 16], [0, 32]])
                    qv = qi32[:].rearrange("p (g i) -> p g i", i=32)
                    wv_ = wdt[par][:].rearrange("p (g i) -> p g i", i=32)
                    e.scalar_tensor_tensor(wv_, qv, 8.0, sl,
                                           ALU.subtract, ALU.mult).then_inc(s_dq.h, 1)
                P.emit("vector", f_stt)
                s_dq.v += 1

                def f_st(e, och=och, sub=sub, par=par, dqv=s_dq.v):
                    e.wait_ge(s_dq.h, dqv)
                    e.dma_start(out=w_scr[och * 128:(och + 1) * 128,
                                          sub * 512:(sub + 1) * 512],
                                in_=wdt[par][:]).then_inc(s_spk.h, 16)
                P.emit("gpsimd", f_st)
                s_spk.v += 16

        def p7_pre():
            def f_ldw1(e, spkv=s_spk.v):
                e.wait_ge(s_spk.h, spkv)
                for ic in range(NC):
                    dst = wbig[:, 8192 + ic * 256: 8192 + (ic + 1) * 256]
                    e.dma_start_transpose(dst,
                                          w_scr[256:512, ic * 128:(ic + 1) * 128]).then_inc(s_wt2.h, 16)
            P.emit("sync", f_ldw1)
            s_wt2.v += 16 * NC
            wt_end[1] = s_wt2.v

        def p7():
            p7_ev0 = s_ev2.v
            ffn_end = {}
            for og in range(16):
                par = og % 2
                if og >= 2:
                    def f_ldw(e, og=og, par=par, mmv=ffn_end.get(og - 2, 0)):
                        if mmv > 0:
                            e.wait_ge(s_mm2.h, mmv)
                        for ic in range(NC):
                            dst = wbig[:, par * 8192 + ic * 256: par * 8192 + (ic + 1) * 256]
                            e.dma_start_transpose(dst,
                                                  w_scr[og * 256:(og + 1) * 256,
                                                        ic * 128:(ic + 1) * 128]).then_inc(s_wt2.h, 16)
                    P.emit("sync", f_ldw)
                    s_wt2.v += 16 * NC
                    wt_end[og] = s_wt2.v

                for oc2 in range(2):
                    och = og * 2 + oc2
                    mm0 = s_mm2.v

                    def f_mm(e, par=par, oc2=oc2, och=och, wtv=wt_end[og], p7_ev0=p7_ev0):
                        e.wait_ge(s_wt2.h, wtv)
                        for hf in range(2):
                            i = och * 2 + hf
                            tgt = p7_ev0 + i - 1
                            if i >= 2:
                                e.wait_ge(s_ev2.h, tgt)
                            pspc = [ps1, ps2][hf]
                            for ic in range(NC):
                                lhs = wbig[:, par * 8192 + ic * 256 + oc2 * 128:
                                           par * 8192 + ic * 256 + oc2 * 128 + 128]
                                ins = e.matmul(pspc[:], lhs,
                                               bufb[ic][:, hf * 512:(hf + 1) * 512],
                                               start=(ic == 0), stop=(ic == NC - 1))
                            ins.then_inc(s_mm2.h, 1)
                    P.emit("tensor", f_mm)
                    s_mm2.v += 2
                    ffn_end[og] = s_mm2.v

                    def f_ep(e, och=och, mm0=mm0, outv=s_out.v):
                        for hf in range(2):
                            pspc = [ps1, ps2][hf]
                            e.wait_ge(s_mm2.h, mm0 + hf + 1)
                            i = och * 2 + hf
                            if i >= 2:
                                e.wait_ge(s_out.h, 16 * (i - 1))
                            e.scalar_tensor_tensor(outt[hf][:], pspc[:],
                                                   bias_sb[:, och:och + 1],
                                                   bufa[och][:, hf * 512:(hf + 1) * 512],
                                                   ALU.add, ALU.add).then_inc(s_ev2.h, 1)
                    P.emit("vector", f_ep)
                    s_ev2.v += 2

                    def f_st(e, och=och, evv=s_ev2.v):
                        for hf in range(2):
                            e.wait_ge(s_ev2.h, evv - 1 + hf)
                            e.dma_start(out=out[och * 128:(och + 1) * 128,
                                                hf * 512:(hf + 1) * 512],
                                        in_=outt[hf][:]).then_inc(s_out.h, 16)
                    P.emit("sync", f_st)
                    s_out.v += 32

            def f_fin(e, ov=s_out.v):
                e.wait_ge(s_out.h, ov)
            P.emit("sync", f_fin)

        def dump_scr():
            if not DEBUG:
                return
            def f(e, xv=s_x2.v):
                e.dma_start(out=dbg["d_kscr"], in_=k_scr).then_inc(s_x2.h, 16)
                e.dma_start(out=dbg["d_vscr"], in_=v_scr).then_inc(s_x2.h, 16)
                e.dma_start(out=dbg["d_hscr"], in_=h_scr).then_inc(s_x2.h, 16)
                e.dma_start(out=dbg["d_wscr"], in_=w_scr).then_inc(s_x2.h, 16)
                e.wait_ge(s_x2.h, xv + 64)
            P.emit("gpsimd", f)
            s_x2.v += 64
            bar()

        def dump_sbuf(key, tiles):
            if not DEBUG:
                return
            def f(e, key=key, tiles=tiles, base=s_x2.v):
                for c in range(NC):
                    e.dma_start(out=dbg[key][c * 128:(c + 1) * 128, :],
                                in_=tiles[c]).then_inc(s_x2.h, 16)
                e.wait_ge(s_x2.h, base + 16 * NC)
            P.emit("gpsimd", f)
            s_x2.v += 16 * NC
            bar()

        NPH = int(os.environ.get("KPHASES", "10"))
        phases = [lambda: pass_a(0), lambda: pass_a(1), lambda: pass_b(0),
                  lambda: pass_b(1), p3, p4, p_wo, p7_pre, p5, p7]
        names = ["A1", "A2", "B1", "B2", "P3", "P4", "WO", "PRE", "P5", "P7"]
        for i in range(min(NPH, 10)):
            phases[i]()
            if DEBUG:
                if names[i] == "P3":
                    dump_sbuf("d_q", bufa)
                elif names[i] == "P4":
                    dump_sbuf("d_ctx", bufb)
                elif names[i] == "WO":
                    dump_sbuf("d_x2", bufa)
                elif names[i] == "P5":
                    dump_sbuf("d_h2", bufb)
        if DEBUG:
            dump_scr()
        if NPH < 10:
            def f_dummy(e):
                e.memset(outt[0][:], 0.0).then_inc(s_st2.h, 1)
            P.emit("vector", f_dummy)
            s_st2.v += 1

            def f_fin0(e, stv=s_st2.v, ov=s_out.v):
                e.wait_ge(s_st2.h, stv)
                e.dma_start(out=out[0:128, 0:512], in_=outt[0][:]).then_inc(s_out.h, 16)
                e.wait_ge(s_out.h, ov + 16)
            P.emit("sync", f_fin0)
            s_out.v += 16

        block = es.enter_context(nc.Block())
        block.sync(lambda e: P.replay("sync", e))
        block.gpsimd(lambda e: P.replay("gpsimd", e))
        block.vector(lambda e: P.replay("vector", e))
        block.scalar(lambda e: P.replay("scalar", e))
        block.tensor(lambda e: P.replay("tensor", e))

    return nc


def _perm_for(q):
    A, B = q, 7 - q
    own = list(range(A * 512, (A + 1) * 512)) + list(range(B * 512, (B + 1) * 512))
    own_set = set(own)
    others = [t for t in range(T) if t not in own_set]
    return np.array(own + others, dtype=np.int64)


def _rope_tables(perm):
    inv = 1.0 / (500000.0 ** (np.arange(0, HD, 2, dtype=np.float64) / HD))
    t = np.arange(T, dtype=np.float64)
    fr = np.outer(t, inv)
    cos = np.cos(fr).astype(np.float32)[perm]
    sin = np.sin(fr).astype(np.float32)[perm]
    t1 = np.concatenate([cos.T, cos.T], axis=0).astype(ml_dtypes.bfloat16)
    t2 = np.concatenate([-sin.T, sin.T], axis=0).astype(ml_dtypes.bfloat16)
    return np.ascontiguousarray(t1), np.ascontiguousarray(t2)


_NC_CACHE = {}


def kernel(L0_attn_x, L0_attn_Wq, L0_attn_Wk, L0_attn_Wv, L0_attn_Wo,
           L0_ffn_weight_packed, L0_ffn_weight_scale, L0_ffn_weight_var_mask,
           L0_ffn_bias):
    x = np.asarray(L0_attn_x, dtype=np.float32)
    wq = np.ascontiguousarray(np.asarray(L0_attn_Wq, dtype=np.float32))
    wk = np.ascontiguousarray(np.asarray(L0_attn_Wk, dtype=np.float32))
    wv = np.ascontiguousarray(np.asarray(L0_attn_Wv, dtype=np.float32))
    wo = np.ascontiguousarray(np.asarray(L0_attn_Wo, dtype=np.float32))
    pk_a = np.ascontiguousarray(np.asarray(L0_ffn_weight_packed, dtype=np.int32).reshape(4096, 2048))
    sc_a = np.ascontiguousarray(np.asarray(L0_ffn_weight_scale).reshape(4096, 128))
    mk_a = np.ascontiguousarray(np.asarray(L0_ffn_weight_var_mask).reshape(4096, 128).view(np.uint8))
    bias_pp = np.ascontiguousarray(np.asarray(L0_ffn_bias, dtype=np.float32).reshape(32, 128).T)

    tri = np.zeros((128, 128), dtype=ml_dtypes.bfloat16)
    for kk in range(128):
        tri[kk, kk:] = 1.0
    swpm = np.zeros((128, 128), dtype=ml_dtypes.bfloat16)
    for m in range(128):
        swpm[(m + 64) % 128, m] = 1.0

    if "nc" not in _NC_CACHE:
        _NC_CACHE["nc"] = build_nc()
    nc = _NC_CACHE["nc"]

    in_maps = []
    perms = []
    for c in range(8):
        b, q = c // 4, c % 4
        perm = _perm_for(q)
        perms.append(perm)
        xTc = np.ascontiguousarray(x[b][perm].T)
        t1, t2 = _rope_tables(perm)
        in_maps.append(dict(
            xT=xTc, wq=wq, wk=wk, wv=wv, wo=wo, pk=pk_a, sc=sc_a, mk=mk_a,
            bias_in=bias_pp, t1_in=t1, t2_in=t2, tri_in=tri, swp_in=swpm,
        ))

    res = run_bass_kernel_spmd(nc, in_maps, core_ids=list(range(8)))

    outa = np.empty((2, T, D), dtype=np.float32)
    for c in range(8):
        b = c // 4
        o = np.asarray(res.results[c]["o"])
        outa[b][perms[c][:OWN]] = o.T
    if DEBUG:
        kernel.last_res = res
        kernel.last_perms = perms
    return outa


# version 06
# speedup vs baseline: 1.0037x; 1.0037x over previous
import os
import math
import contextlib
import numpy as np
import ml_dtypes

import concourse.bass as bass
import concourse.mybir as mybir
from concourse.bass_utils import run_bass_kernel_spmd

F32 = mybir.dt.float32
BF16 = mybir.dt.bfloat16
I32 = mybir.dt.int32
F16 = mybir.dt.float16
U8 = mybir.dt.uint8
AF = mybir.ActivationFunctionType
ALU = mybir.AluOpType

D = 4096
T = 4096
NH = 32
NKV = 8
HD = 128
NC = 32
OWN = 1024
NBLK = 8
SCALE = 1.0 / math.sqrt(HD)
EXP_BIAS = -15.0
EPS = 1e-5

ENGS = ("sync", "gpsimd", "vector", "scalar", "tensor")

DEBUG = False


def _variant_tables(q):
    A, B = q, 7 - q
    own = list(range(4 * A, 4 * A + 4)) + list(range(4 * B, 4 * B + 4))
    others = [g for g in range(32) if g not in own]
    j0 = [sum(1 for o in own if o < g) for g in others]
    return own, others, j0


class Sem:
    def __init__(self, es, nc, name):
        self.h = es.enter_context(nc.semaphore(name))
        self.v = 0


class Prog:
    def __init__(self):
        self.ops = {e: [] for e in ENGS}

    def emit(self, eng, fn):
        self.ops[eng].append(("op", fn))

    def if_open(self, v):
        for e in ENGS:
            self.ops[e].append(("if", v))

    def if_close(self):
        for e in ENGS:
            self.ops[e].append(("endif", None))

    def replay(self, eng_name, eng):
        need_if = any(k == "if" for k, _ in self.ops[eng_name])
        vmod = None
        if need_if:
            pid = eng.to_reg(eng.partition_id())
            vmod = eng.alloc_register(f"vm_{eng_name}")
            eng.reg_mod(vmod, pid, 4)
        guards = []
        for kind, payload in self.ops[eng_name]:
            if kind == "op":
                payload(eng)
            elif kind == "if":
                g = eng.If_eq(vmod, payload)
                g.__enter__()
                guards.append(g)
            else:
                guards.pop().__exit__(None, None, None)
        eng.end_ifs()


def build_nc():
    nc = bass.Bass("TRN2", target_bir_lowering=False, debug=False)

    xT = nc.dram_tensor("xT", [D, T], F32, kind="ExternalInput").ap()
    wq = nc.dram_tensor("wq", [D, D], F32, kind="ExternalInput").ap()
    wk = nc.dram_tensor("wk", [D, 1024], F32, kind="ExternalInput").ap()
    wv = nc.dram_tensor("wv", [D, 1024], F32, kind="ExternalInput").ap()
    wo = nc.dram_tensor("wo", [D, D], F32, kind="ExternalInput").ap()
    pk = nc.dram_tensor("pk", [4096, 2048], I32, kind="ExternalInput").ap()
    sc = nc.dram_tensor("sc", [4096, 128], F16, kind="ExternalInput").ap()
    mk = nc.dram_tensor("mk", [4096, 128], U8, kind="ExternalInput").ap()
    bias_in = nc.dram_tensor("bias_in", [128, 32], F32, kind="ExternalInput").ap()
    t1_in = nc.dram_tensor("t1_in", [128, T], BF16, kind="ExternalInput").ap()
    t2_in = nc.dram_tensor("t2_in", [128, T], BF16, kind="ExternalInput").ap()
    tri_in = nc.dram_tensor("tri_in", [128, 128], BF16, kind="ExternalInput").ap()
    swp_in = nc.dram_tensor("swp_in", [128, 128], BF16, kind="ExternalInput").ap()

    out = nc.dram_tensor("o", [D, OWN], F32, kind="ExternalOutput").ap()

    k_scr = nc.dram_tensor("k_scr", [1024, T], BF16, kind="Internal").ap()
    v_scr = nc.dram_tensor("v_scr", [T, 1024], BF16, kind="Internal").ap()
    h_scr = nc.dram_tensor("h_scr", [D, T], BF16, kind="Internal").ap()
    w_scr = nc.dram_tensor("w_scr", [4096, 4096], BF16, kind="Internal").ap()

    dbg = {}
    if DEBUG:
        for nm, shp in [("d_kscr", [1024, T]), ("d_vscr", [T, 1024]),
                        ("d_hscr", [D, T]), ("d_wscr", [4096, 4096]),
                        ("d_q", [D, OWN]), ("d_ctx", [D, OWN]),
                        ("d_x2", [D, OWN]), ("d_h2", [D, OWN])]:
            dbg[nm] = nc.dram_tensor(nm, shp, BF16, kind="ExternalOutput").ap()

    es = contextlib.ExitStack()
    with es:
        def sbt(name, shape, dtype):
            return es.enter_context(nc.sbuf_tensor(name, shape, dtype))

        bufa_t = sbt("bufa", [128, NC * OWN], BF16)
        bufb_t = sbt("bufb", [128, NC * OWN], BF16)
        bufa = [bufa_t[:, i * OWN:(i + 1) * OWN] for i in range(NC)]
        bufb = [bufb_t[:, i * OWN:(i + 1) * OWN] for i in range(NC)]
        WBC = 16384
        wbig = sbt("wbig", [128, WBC], BF16)

        t1b = sbt("t1b", [128, OWN], BF16)
        t2b = sbt("t2b", [128, OWN], BF16)
        kout = [sbt(f"kout{i}", [128, 512], BF16) for i in range(2)]
        s1f = sbt("s1f", [128, 512], F32)
        x2t = [sbt(f"x2t{i}", [128, 512], BF16) for i in range(2)]
        m_rep = sbt("m_rep", [128, 512], F32)
        r_rep = sbt("r_rep", [128, 512], F32)
        mrow = sbt("mrow", [1, 512], F32)
        rrow = sbt("rrow", [1, 512], F32)
        p1s = sbt("p1s", [128, 512], BF16)
        p2s = sbt("p2s", [128, 512], BF16)
        psb = [sbt(f"psb{i}", [128, OWN], BF16) for i in range(2)]
        recf = [sbt(f"recf{i}", [128, 512], F32) for i in range(2)]
        outt = [sbt(f"outt{i}", [128, 512], F32) for i in range(2)]
        pkt = [sbt(f"pkt{i}", [128, 256], I32) for i in range(2)]
        qi32 = sbt("qi32", [128, 512], I32)
        wdt = [sbt(f"wdt{i}", [128, 512], BF16) for i in range(2)]
        sct = sbt("sct", [128, 128], F16)
        mkt = sbt("mkt", [128, 128], U8)
        seff = sbt("seff", [128, 128], F32)
        tri = sbt("tri_sb", [128, 128], BF16)
        swp = sbt("swp_sb", [128, 128], BF16)
        kraw = [sbt(f"kraw{i}", [128, 512], BF16) for i in range(2)]
        ones_bf = sbt("ones_bf", [128, 1], BF16)
        ones_f = sbt("ones_f", [1, 128], F32)
        b15 = sbt("b15", [128, 1], F32)
        beps = sbt("beps", [1, 1], F32)
        bias_sb = sbt("bias_sb", [128, 32], F32)
        den_sb = sbt("den_sb", [1, OWN], F32)

        ps1 = es.enter_context(nc.psum_tensor("ps1", [128, 512], F32))
        ps2 = es.enter_context(nc.psum_tensor("ps2", [128, 512], F32))
        ps3 = es.enter_context(nc.psum_tensor("ps3", [128, 1024], F32))
        ps4 = es.enter_context(nc.psum_tensor("ps4", [128, 1024], F32))
        psd = es.enter_context(nc.psum_tensor("psd", [1, 1024], F32))

        s_x = Sem(es, nc, "s_x")
        s_wt = Sem(es, nc, "s_wt")
        s_tb = Sem(es, nc, "s_tb")
        s_mm = Sem(es, nc, "s_mm")
        s_ct = Sem(es, nc, "s_ct")
        s_sq = Sem(es, nc, "s_sq")
        s_st = Sem(es, nc, "s_st")
        s_h1 = Sem(es, nc, "s_h1")
        s_ev = Sem(es, nc, "s_ev")
        s_ex = Sem(es, nc, "s_ex")
        s_mk = Sem(es, nc, "s_mk")
        s_spk = Sem(es, nc, "s_spk")
        s_sph = Sem(es, nc, "s_sph")
        s_kv = Sem(es, nc, "s_kv")
        s_dq = Sem(es, nc, "s_dq")
        s_out = Sem(es, nc, "s_out")
        s_ph = Sem(es, nc, "s_ph")
        s_wt2 = Sem(es, nc, "s_wt2")
        s_mm2 = Sem(es, nc, "s_mm2")
        s_ev2 = Sem(es, nc, "s_ev2")
        s_x2 = Sem(es, nc, "s_x2")
        s_sq2 = Sem(es, nc, "s_sq2")
        s_st2 = Sem(es, nc, "s_st2")
        s_h2 = Sem(es, nc, "s_h2")
        s_pk = Sem(es, nc, "s_pk")

        ALL_SEMS = [s_x, s_wt, s_tb, s_mm, s_ct, s_sq, s_st, s_h1, s_ev,
                    s_ex, s_mk, s_spk, s_sph, s_kv, s_dq, s_out, s_ph,
                    s_wt2, s_mm2, s_ev2, s_x2, s_sq2, s_st2, s_h2, s_pk]

        P = Prog()

        def drain_waits(eng, vals):
            for sem, v in vals:
                if v > 0:
                    eng.wait_ge(sem.h, v)

        def bar():
            tgt = s_ph.v + 5
            vals = [(sem, sem.v) for sem in ALL_SEMS if sem is not s_ph]
            for e in ENGS:
                def f(eng, tgt=tgt, vals=vals):
                    drain_waits(eng, vals)
                    eng.nop().then_inc(s_ph.h, 1)
                    eng.wait_ge(s_ph.h, tgt)
                P.emit(e, f)
            s_ph.v = tgt

        def xb(p, c):
            return bufa_t[:, c * OWN + p * 512: c * OWN + (p + 1) * 512]

        def xgrp_ap(p, g):
            return bass.AP(bufa_t[:].tensor, (g * 4) * OWN + p * 512,
                           [[NC * OWN, 128], [OWN, 4], [1, 512]])

        def f(e):
            e.dma_start(out=tri[:], in_=tri_in).then_inc(s_x.h, 16)
            e.dma_start(out=swp[:], in_=swp_in).then_inc(s_x.h, 16)
            e.dma_start(out=bias_sb[:], in_=bias_in).then_inc(s_x.h, 16)
            e.wait_ge(s_x.h, 48)
        P.emit("sync", f)
        s_x.v += 48

        def f(e):
            e.memset(ones_bf[:], 1.0)
            e.memset(ones_f[:], 1.0)
            e.memset(b15[:], EXP_BIAS)
            e.memset(beps[:], EPS).then_inc(s_st.h, 1)
        P.emit("vector", f)
        s_st.v += 1
        bar()

        def ln_stats(tiles, xsem, xval, smm, ssq, sst):
            sq0 = ssq.v
            mm0 = smm.v

            def f_sq(e, sq0=sq0, mm0=mm0):
                e.wait_ge(xsem.h, xval)
                for c in range(NC):
                    if c >= 2:
                        e.wait_ge(smm.h, mm0 + c - 1)
                    e.activation(x2t[c % 2][:], tiles(c), AF.Square).then_inc(ssq.h, 1)
            P.emit("scalar", f_sq)
            ssq.v += NC

            def f_mm(e, sq0=sq0, stv0=sst.v):
                e.wait_ge(xsem.h, xval)
                if stv0 > 0:
                    e.wait_ge(sst.h, stv0)
                for c in range(NC):
                    e.matmul(psd[:1, 0:512], ones_bf[:], tiles(c),
                             start=(c == 0), stop=(c == NC - 1))
                    e.wait_ge(ssq.h, sq0 + c + 1)
                    e.matmul(psd[:1, 512:1024], ones_bf[:], x2t[c % 2][:],
                             start=(c == 0), stop=(c == NC - 1)).then_inc(smm.h, 1)
            P.emit("tensor", f_mm)
            smm.v += NC

            def f_vs(e, mmv=smm.v):
                e.wait_ge(smm.h, mmv)
                e.tensor_scalar(mrow[:], psd[:1, 0:512], 1.0 / D, None, ALU.mult)
                e.tensor_scalar(rrow[:], psd[:1, 512:1024], 1.0 / D, None, ALU.mult)
                e.tensor_tensor(s1f[:1, :], mrow[:], mrow[:], ALU.mult)
                e.tensor_tensor(rrow[:], rrow[:], s1f[:1, :], ALU.subtract).then_inc(sst.h, 1)
            P.emit("vector", f_vs)
            sst.v += 1

            def f_sqrt(e, stv=sst.v):
                e.wait_ge(sst.h, stv)
                e.activation(rrow[:], rrow[:], AF.Sqrt, bias=beps[:]).then_inc(ssq.h, 1)
            P.emit("scalar", f_sqrt)
            ssq.v += 1

            def f_rec(e, sqv=ssq.v):
                e.wait_ge(ssq.h, sqv)
                e.reciprocal(rrow[:], rrow[:]).then_inc(sst.h, 1)
            P.emit("vector", f_rec)
            sst.v += 1

            def f_bc(e, stv=sst.v, evv=s_ev.v):
                e.wait_ge(sst.h, stv)
                if evv > 0:
                    e.wait_ge(s_ev.h, evv)
                e.matmul(ps1[:], ones_f[:], mrow[:], start=True, stop=True)
                e.matmul(ps2[:], ones_f[:], rrow[:], start=True, stop=True).then_inc(smm.h, 1)
            P.emit("tensor", f_bc)
            smm.v += 1

            def f_cp(e, mmv=smm.v):
                e.wait_ge(smm.h, mmv)
                e.copy(m_rep[:], ps1[:])
                e.copy(r_rep[:], ps2[:]).then_inc(ssq.h, 1)
            P.emit("scalar", f_cp)
            ssq.v += 1

        def wks(s, c):
            return wbig[:, s * 4096 + c * 128: s * 4096 + (c + 1) * 128]

        def wvs(c):
            return wbig[:, c * 512:(c + 1) * 512]

        def wqs(par, c):
            return wbig[:, par * 4096 + c * 128: par * 4096 + (c + 1) * 128]

        def pass_a(half):
            wt0 = s_wt.v

            def f_ldwk(e, half=half):
                for c in range(NC):
                    src = wk[c * 128:(c + 1) * 128,
                             half * 512:(half + 1) * 512].rearrange("p (s o) -> p s o", s=4)
                    dst = bass.AP(wbig[:].tensor, c * 128,
                                  [[WBC, 128], [4096, 4], [1, 128]])
                    e.dma_start(out=dst, in_=src).then_inc(s_wt.h, 16)
            P.emit("gpsimd", f_ldwk)
            s_wt.v += 16 * NC

            sph_end = {}
            for blk in range(NBLK):
                par = blk % 2

                if half == 0:
                    def f_ldx(e, blk=blk, par=par, spv=sph_end.get(blk - 2, 0)):
                        if spv > 0:
                            e.wait_ge(s_sph.h, spv)
                        for g in range(8):
                            src = xT.rearrange("(c p) t -> p c t", p=128)[
                                :, g * 4:(g + 1) * 4, blk * 512:(blk + 1) * 512]
                            e.dma_start(out=xgrp_ap(par, g), in_=src).then_inc(s_x.h, 16)
                    P.emit("gpsimd", f_ldx)
                    s_x.v += 16 * 8

                    ln_stats(lambda c, par=par: xb(par, c), s_x, s_x.v,
                             s_mm, s_sq, s_st)

                    def f_h1(e, par=par, sqv=s_sq.v):
                        e.wait_ge(s_sq.h, sqv)
                        for c in range(NC):
                            e.tensor_tensor(s1f[:], xb(par, c), m_rep[:], ALU.subtract)
                            e.tensor_tensor(xb(par, c), s1f[:], r_rep[:], ALU.mult).then_inc(s_h1.h, 1)
                    P.emit("vector", f_h1)
                    s_h1.v += NC

                    if blk < 2:
                        def f_own(e, par=par, blk=blk, h1v=s_h1.v):
                            for c in range(NC):
                                e.wait_ge(s_h1.h, h1v - NC + c + 1)
                                ins = e.copy(bufb[c][:, blk * 512:(blk + 1) * 512], xb(par, c))
                            ins.then_inc(s_sq.h, 1)
                        P.emit("scalar", f_own)
                        s_sq.v += 1
                else:
                    def f_ldh(e, blk=blk, par=par, mmv=s_mm.v, sphv=s_sph.v):
                        e.wait_ge(s_mm.h, mmv)
                        e.wait_ge(s_sph.h, sphv)
                        for g in range(8):
                            src = h_scr.rearrange("(c p) t -> p c t", p=128)[
                                :, g * 4:(g + 1) * 4, blk * 512:(blk + 1) * 512]
                            e.dma_start(out=xgrp_ap(par, g), in_=src).then_inc(s_x.h, 16)
                    P.emit("sync", f_ldh)
                    s_x.v += 16 * 8

                def f_ldt(e, blk=blk):
                    e.dma_start(out=t1b[:, 0:512],
                                in_=t1_in[:, blk * 512:(blk + 1) * 512]).then_inc(s_tb.h, 16)
                    e.dma_start(out=t2b[:, 0:512],
                                in_=t2_in[:, blk * 512:(blk + 1) * 512]).then_inc(s_tb.h, 16)
                P.emit("sync", f_ldt)
                s_tb.v += 32

                kmm0 = s_mm.v
                h1v = s_h1.v
                xv = s_x.v
                kev_base = s_ev.v
                ksq0 = s_sq.v

                def f_kmm(e, par=par, wt0=wt0, h1v=h1v, xv=xv,
                          ksq0=ksq0, half=half, kev_base=kev_base):
                    e.wait_ge(s_wt.h, wt0 + 16 * NC)
                    if half == 0:
                        e.wait_ge(s_h1.h, h1v)
                    else:
                        e.wait_ge(s_x.h, xv)
                    for s in range(4):
                        tgt = ksq0 + s - 1
                        if tgt > 0:
                            e.wait_ge(s_sq.h, tgt)
                        pspc = [ps3[:, 0:512], ps4[:, 0:512]][s % 2]
                        for c in range(NC):
                            ins = e.matmul(pspc, wks(s, c), xb(par, c),
                                           start=(c == 0), stop=(c == NC - 1))
                        ins.then_inc(s_mm.h, 1)
                    for s in range(4):
                        e.wait_ge(s_sq.h, ksq0 + s + 1)
                        if kev_base + s > 0:
                            e.wait_ge(s_ev.h, kev_base + s)
                        e.matmul(ps1[:], swp[:], kraw[s % 2][:],
                                 start=True, stop=True).then_inc(s_mm.h, 1)
                P.emit("tensor", f_kmm)
                s_mm.v += 8

                def f_kev(e, kmm0=kmm0, ksq0=ksq0, kev_base=kev_base):
                    for s in range(4):
                        pspc = [ps3[:, 0:512], ps4[:, 0:512]][s % 2]
                        e.wait_ge(s_mm.h, kmm0 + s + 1)
                        if kev_base + s - 1 > 0:
                            e.wait_ge(s_ev.h, kev_base + s - 1)
                        e.copy(kraw[s % 2][:], pspc).then_inc(s_sq.h, 1)
                P.emit("scalar", f_kev)
                s_sq.v += 4

                kev0 = s_ev.v

                def f_kro(e, kmm0=kmm0, tbv=s_tb.v, spk0=s_spk.v):
                    e.wait_ge(s_tb.h, tbv)
                    for s in range(4):
                        e.wait_ge(s_mm.h, kmm0 + 4 + s + 1)
                        tgt = spk0 + 16 * (s - 1)
                        if tgt > 0:
                            e.wait_ge(s_spk.h, tgt)
                        ko = kout[s % 2]
                        e.tensor_tensor(p1s[:], kraw[s % 2][:], t1b[:, 0:512], ALU.mult)
                        e.tensor_tensor(p2s[:], ps1[:], t2b[:, 0:512], ALU.mult)
                        e.tensor_tensor(ko[:], p1s[:], p2s[:], ALU.add).then_inc(s_ev.h, 1)
                P.emit("vector", f_kro)
                s_ev.v += 4

                def f_kst(e, kev0=kev0, blk=blk, half=half):
                    for s in range(4):
                        kvh = half * 4 + s
                        e.wait_ge(s_ev.h, kev0 + s + 1)
                        e.dma_start(out=k_scr[kvh * 128:(kvh + 1) * 128,
                                              blk * 512:(blk + 1) * 512],
                                    in_=kout[s % 2][:]).then_inc(s_spk.h, 16)
                P.emit("sync", f_kst)
                s_spk.v += 16 * 4

                if half == 0:
                    def f_sp(e, par=par, blk=blk, mmv=s_mm.v):
                        e.wait_ge(s_mm.h, mmv)
                        for g in range(8):
                            dst = h_scr.rearrange("(c p) t -> p c t", p=128)[
                                :, g * 4:(g + 1) * 4, blk * 512:(blk + 1) * 512]
                            e.dma_start(out=dst, in_=xgrp_ap(par, g)).then_inc(s_sph.h, 16)
                    P.emit("gpsimd", f_sp)
                    s_sph.v += 16 * 8
                    sph_end[blk] = s_sph.v

            bar()

        def pass_b(vh):
            def f_ldwv(e, vh=vh):
                for c in range(NC):
                    e.dma_start(out=wvs(c),
                                in_=wv[c * 128:(c + 1) * 128,
                                       vh * 512:(vh + 1) * 512]).then_inc(s_wt.h, 16)
            P.emit("gpsimd", f_ldwv)
            wvdone = s_wt.v + 16 * NC
            s_wt.v += 16 * NC

            for blk in range(NBLK):
                par = blk % 2

                def f_ldh(e, blk=blk, par=par, mmv=s_mm.v, sphv=s_sph.v):
                    e.wait_ge(s_mm.h, mmv)
                    e.wait_ge(s_sph.h, sphv)
                    for g in range(8):
                        src = h_scr.rearrange("(c p) t -> p c t", p=128)[
                            :, g * 4:(g + 1) * 4, blk * 512:(blk + 1) * 512]
                        e.dma_start(out=xgrp_ap(par, g), in_=src).then_inc(s_x.h, 16)
                P.emit("sync", f_ldh)
                s_x.v += 16 * 8

                vmm0 = s_mm.v

                def f_vmm(e, par=par, xv=s_x.v, wvdone=wvdone, ev0=s_ev.v):
                    e.wait_ge(s_wt.h, wvdone)
                    e.wait_ge(s_x.h, xv)
                    for tc in range(4):
                        tgt = ev0 + tc - 1
                        if tgt > 0:
                            e.wait_ge(s_ev.h, tgt)
                        pspc = [ps3[:, 0:512], ps4[:, 0:512]][tc % 2]
                        for c in range(NC):
                            ins = e.matmul(pspc, xb(par, c)[:, tc * 128:(tc + 1) * 128],
                                           wvs(c), start=(c == 0), stop=(c == NC - 1))
                        ins.then_inc(s_mm.h, 1)
                P.emit("tensor", f_vmm)
                s_mm.v += 4

                vev0 = s_ev.v

                def f_vev(e, vmm0=vmm0, spk0=s_spk.v):
                    for tc in range(4):
                        pspc = [ps3[:, 0:512], ps4[:, 0:512]][tc % 2]
                        e.wait_ge(s_mm.h, vmm0 + tc + 1)
                        tgt = spk0 + 16 * (tc - 1)
                        if tgt > 0:
                            e.wait_ge(s_spk.h, tgt)
                        e.tensor_copy(kout[tc % 2][:], pspc).then_inc(s_ev.h, 1)
                P.emit("vector", f_vev)
                s_ev.v += 4

                def f_vst(e, vev0=vev0, blk=blk, vh=vh):
                    for tc in range(4):
                        e.wait_ge(s_ev.h, vev0 + tc + 1)
                        e.dma_start(out=v_scr[blk * 512 + tc * 128: blk * 512 + (tc + 1) * 128,
                                              vh * 512:(vh + 1) * 512],
                                    in_=kout[tc % 2][:]).then_inc(s_spk.h, 16)
                P.emit("gpsimd", f_vst)
                s_spk.v += 16 * 4

            bar()

        def p3():
            def f_ldt(e):
                e.dma_start(out=t1b[:], in_=t1_in[:, 0:OWN]).then_inc(s_tb.h, 16)
                e.dma_start(out=t2b[:], in_=t2_in[:, 0:OWN]).then_inc(s_tb.h, 16)
            P.emit("sync", f_ldt)
            s_tb.v += 32

            qmm_end = {}
            for qh in range(NH):
                par = qh % 2

                def f_ldq(e, qh=qh, par=par, mmv=qmm_end.get(qh - 2, s_mm.v)):
                    e.wait_ge(s_mm.h, mmv)
                    for g in range(8):
                        src = wq.rearrange("(c p) o -> p c o", p=128)[
                            :, g * 4:(g + 1) * 4, qh * 128:(qh + 1) * 128]
                        dst = bass.AP(wbig[:].tensor, par * 4096 + (g * 4) * 128,
                                      [[WBC, 128], [128, 4], [1, 128]])
                        e.dma_start(out=dst, in_=src).then_inc(s_wt.h, 16)
                P.emit("gpsimd", f_ldq)
                s_wt.v += 16 * 8

                qmm0 = s_mm.v
                qsq0 = s_sq.v
                qev0 = s_ev.v

                def f_qmm(e, qh=qh, par=par, wtv=s_wt.v, qsq0=qsq0, qev0=qev0):
                    e.wait_ge(s_wt.h, wtv)
                    for hf in range(2):
                        tgt = qsq0 + hf - 1
                        if tgt > 0:
                            e.wait_ge(s_sq.h, tgt)
                        pspc = [ps3[:, 0:512], ps4[:, 0:512]][hf]
                        for c in range(NC):
                            ins = e.matmul(pspc, wqs(par, c),
                                           bufb[c][:, hf * 512:(hf + 1) * 512],
                                           start=(c == 0), stop=(c == NC - 1))
                        ins.then_inc(s_mm.h, 1)
                    for hf in range(2):
                        e.wait_ge(s_sq.h, qsq0 + hf + 1)
                        tgt = qev0 + hf
                        if tgt > 0:
                            e.wait_ge(s_ev.h, tgt)
                        e.matmul(ps1[:], swp[:], kraw[hf][:],
                                 start=True, stop=True).then_inc(s_mm.h, 1)
                P.emit("tensor", f_qmm)
                s_mm.v += 4
                qmm_end[qh] = s_mm.v

                def f_qev(e, qmm0=qmm0, qev0=qev0):
                    for hf in range(2):
                        pspc = [ps3[:, 0:512], ps4[:, 0:512]][hf]
                        e.wait_ge(s_mm.h, qmm0 + hf + 1)
                        tgt = qev0 + hf - 1
                        if tgt > 0:
                            e.wait_ge(s_ev.h, tgt)
                        e.copy(kraw[hf][:], pspc).then_inc(s_sq.h, 1)
                P.emit("scalar", f_qev)
                s_sq.v += 2

                def f_qro(e, qh=qh, qmm0=qmm0, tbv=s_tb.v):
                    e.wait_ge(s_tb.h, tbv)
                    for hf in range(2):
                        e.wait_ge(s_mm.h, qmm0 + 2 + hf + 1)
                        qt = bufa[qh][:, hf * 512:(hf + 1) * 512]
                        e.tensor_tensor(p1s[:], kraw[hf][:], t1b[:, hf * 512:(hf + 1) * 512], ALU.mult)
                        e.tensor_tensor(p2s[:], ps1[:], t2b[:, hf * 512:(hf + 1) * 512], ALU.mult)
                        e.tensor_tensor(qt[:], p1s[:], p2s[:], ALU.add).then_inc(s_ev.h, 1)
                P.emit("vector", f_qro)
                s_ev.v += 2

            bar()

        def kbuf(par):
            return wbig[:, par * 4096:(par + 1) * 4096]

        def vbuf(par):
            return wbig[:, 8192 + par * 4096: 8192 + (par + 1) * 4096]

        def p4_variant(q):
            own, others, j0s = _variant_tables(q)
            visits = [(l, l, True) for l in range(8)]
            visits += [(8 + t, j0, False) for t, j0 in enumerate(j0s) if j0 < 8]
            nvis = len(visits)
            lastA = max(vi for vi, (_, j0, _) in enumerate(visits) if j0 < 4)
            lastB = nvis - 1
            nmask_thru = []
            nm = 0
            for (_, _, dg) in visits:
                if dg:
                    nm += 1
                nmask_thru.append(nm)

            kvh_ct = {}
            for kvh in range(NKV):
                par = kvh % 2

                def f_ldkv(e, kvh=kvh, par=par, ctv=kvh_ct.get(kvh - 2, 0),
                           spkv=s_spk.v):
                    e.wait_ge(s_spk.h, spkv)
                    if kvh >= 2 and ctv > 0:
                        e.wait_ge(s_ct.h, ctv)
                    e.dma_start(out=kbuf(par),
                                in_=k_scr[kvh * 128:(kvh + 1) * 128, :]).then_inc(s_kv.h, 16)
                    for g in range(8):
                        src = v_scr.rearrange("(c p) o -> p c o", p=128)[
                            :, g * 4:(g + 1) * 4, kvh * 128:(kvh + 1) * 128]
                        dst = bass.AP(wbig[:].tensor, 8192 + par * 4096 + g * 512,
                                      [[WBC, 128], [128, 4], [1, 128]])
                        e.dma_start(out=dst, in_=src).then_inc(s_kv.h, 16)
                P.emit("sync", f_ldkv)
                s_kv.v += 16 * 9
                kv_now = s_kv.v

                for j2 in range(4):
                    qh = kvh * 4 + j2
                    smm0 = s_mm.v
                    sct0 = s_ct.v
                    sex0 = s_ex.v
                    smk0 = s_mk.v
                    sq_pre = s_sq.v
                    sev_pre = s_ev.v

                    def f_mm(e, par=par, qh=qh, visits=visits, smm0=smm0,
                             sct0=sct0, sex0=sex0, smk0=smk0,
                             nmask_thru=nmask_thru, lastA=lastA, lastB=lastB,
                             kv_now=kv_now, sq_pre=sq_pre, sev_pre=sev_pre,
                             nvis=nvis):
                        e.wait_ge(s_kv.h, kv_now)
                        if sq_pre > 0:
                            e.wait_ge(s_sq.h, sq_pre)
                        if sev_pre > 0:
                            e.wait_ge(s_ev.h, sev_pre)
                        started = {0: False, 512: False}

                        def ctx_of(vi):
                            lk_, j0_, dg_ = visits[vi]
                            e.wait_ge(s_ex.h, sex0 + vi + 1)
                            if dg_:
                                e.wait_ge(s_mk.h, smk0 + nmask_thru[vi])
                            vt = vbuf(par)[:, lk_ * 128:(lk_ + 1) * 128]
                            pt = psb[vi % 2]
                            c0 = j0_ * 128
                            ins = None
                            for cs in (0, 512):
                                cc = max(cs, c0)
                                if cc >= cs + 512:
                                    continue
                                bank = ps1 if cs == 0 else ps2
                                first = not started[cs]
                                lastv = lastA if cs == 0 else lastB
                                e.matmul(bank[:, cc - cs:512], vt, pt[:, cc:cs + 512],
                                         start=first, stop=(vi == lastv))
                                ins = e.matmul(psd[:1, cc:cs + 512], ones_bf[:],
                                               pt[:, cc:cs + 512],
                                               start=first, stop=(vi == lastv))
                                started[cs] = True
                            ins.then_inc(s_ct.h, 1)

                        for vi, (lk, j0, dg) in enumerate(visits):
                            if vi >= 2:
                                e.wait_ge(s_ex.h, sex0 + vi - 1)
                            sps = [ps3, ps4][vi % 2]
                            kt = kbuf(par)[:, lk * 128:(lk + 1) * 128]
                            c0 = j0 * 128
                            ins = None
                            for cs in (0, 512):
                                cc = max(cs, c0)
                                if cc >= cs + 512:
                                    continue
                                ins = e.matmul(sps[:, cc:cs + 512], kt,
                                               bufa[qh][:, cc:cs + 512],
                                               start=True, stop=True)
                            ins.then_inc(s_mm.h, 1)
                            if vi >= 1:
                                ctx_of(vi - 1)
                        ctx_of(nvis - 1)
                    P.emit("tensor", f_mm)
                    s_mm.v += nvis
                    s_ct.v += nvis

                    def f_exp(e, visits=visits, smm0=smm0, sct0=sct0, sex0=sex0):
                        for vi, (lk, j0, dg) in enumerate(visits):
                            e.wait_ge(s_mm.h, smm0 + vi + 1)
                            if vi >= 2 and sct0 + vi - 1 > 0:
                                e.wait_ge(s_ct.h, sct0 + vi - 1)
                            sps = [ps3, ps4][vi % 2]
                            c0 = j0 * 128
                            e.activation(psb[vi % 2][:, c0:OWN], sps[:, c0:OWN],
                                         AF.Exp, bias=b15[:], scale=SCALE).then_inc(s_ex.h, 1)
                    P.emit("scalar", f_exp)
                    s_ex.v += nvis

                    def f_msk(e, visits=visits, sex0=sex0):
                        for vi, (lk, j0, dg) in enumerate(visits):
                            if not dg:
                                continue
                            e.wait_ge(s_ex.h, sex0 + vi + 1)
                            c0 = j0 * 128
                            e.tensor_tensor(psb[vi % 2][:, c0:c0 + 128],
                                            psb[vi % 2][:, c0:c0 + 128],
                                            tri[:], ALU.mult).then_inc(s_mk.h, 1)
                    P.emit("vector", f_msk)
                    s_mk.v += nmask_thru[-1]

                    def f_den(e, ctv=s_ct.v):
                        e.wait_ge(s_ct.h, ctv)
                        e.copy(den_sb[:1, :], psd[:1, :]).then_inc(s_sq.h, 1)
                    P.emit("scalar", f_den)
                    s_sq.v += 1

                    def f_bc(e, sqv=s_sq.v):
                        e.wait_ge(s_sq.h, sqv)
                        e.matmul(ps3[:, 0:512], ones_f[:], den_sb[:1, 0:512],
                                 start=True, stop=True)
                        e.matmul(ps4[:, 0:512], ones_f[:], den_sb[:1, 512:1024],
                                 start=True, stop=True).then_inc(s_mm.h, 1)
                    P.emit("tensor", f_bc)
                    s_mm.v += 1

                    def f_ev(e, qh=qh, mmv=s_mm.v):
                        e.wait_ge(s_mm.h, mmv)
                        e.reciprocal(recf[0][:], ps3[:, 0:512])
                        e.reciprocal(recf[1][:], ps4[:, 0:512]).then_inc(s_ev.h, 1)
                        e.tensor_tensor(bufb[qh][:, 0:512], ps1[:], recf[0][:], ALU.mult)
                        e.tensor_tensor(bufb[qh][:, 512:1024], ps2[:], recf[1][:], ALU.mult).then_inc(s_ev.h, 2)
                    P.emit("vector", f_ev)
                    s_ev.v += 3

                kvh_ct[kvh] = s_ct.v

            vals = [(sem, sem.v) for sem in ALL_SEMS if sem is not s_ph]
            for e in ENGS:
                def f(eng, vals=vals):
                    drain_waits(eng, vals)
                    eng.nop().then_inc(s_ph.h, 1)
                P.emit(e, f)

        def p4():
            tgt = s_ph.v + 5
            for q in range(4):
                snap = [s.v for s in ALL_SEMS]
                P.if_open(q)
                p4_variant(q)
                P.if_close()
                for s, v0 in zip(ALL_SEMS, snap):
                    s.v = v0
            s_ph.v = tgt
            for e in ENGS:
                def f(eng, tgt=tgt):
                    eng.wait_ge(s_ph.h, tgt)
                P.emit(e, f)

        def wos(par, c):
            return wbig[:, 8192 + par * 4096 + c * 128: 8192 + par * 4096 + (c + 1) * 128]

        wt_end = {}

        def p_wo():
            wmm_end = {}
            for och in range(NH):
                par = och % 2

                def f_ld(e, och=och, par=par, mmv=wmm_end.get(och - 2, 0)):
                    if mmv > 0:
                        e.wait_ge(s_mm2.h, mmv)
                    for g in range(8):
                        src = wo.rearrange("(c p) o -> p c o", p=128)[
                            :, g * 4:(g + 1) * 4, och * 128:(och + 1) * 128]
                        dst = bass.AP(wbig[:].tensor, 8192 + par * 4096 + (g * 4) * 128,
                                      [[WBC, 128], [128, 4], [1, 128]])
                        e.dma_start(out=dst, in_=src).then_inc(s_wt2.h, 16)
                    e.dma_start(out=bufa[och],
                                in_=xT[och * 128:(och + 1) * 128, 0:OWN]).then_inc(s_wt2.h, 16)
                P.emit("gpsimd", f_ld)
                s_wt2.v += 16 * 9

                mm0 = s_mm2.v

                def f_mm(e, och=och, par=par, wtv=s_wt2.v, ev0=s_ev2.v):
                    e.wait_ge(s_wt2.h, wtv)
                    for hf in range(2):
                        i = och * 2 + hf
                        tgt = ev0 + hf - 3
                        if tgt > 0:
                            e.wait_ge(s_ev2.h, tgt)
                        pspc = [ps1[:], ps2[:], ps3[:, 0:512], ps4[:, 0:512]][i % 4]
                        for c in range(NC):
                            ins = e.matmul(pspc, wos(par, c),
                                           bufb[c][:, hf * 512:(hf + 1) * 512],
                                           start=(c == 0), stop=(c == NC - 1))
                        ins.then_inc(s_mm2.h, 1)
                P.emit("tensor", f_mm)
                s_mm2.v += 2
                wmm_end[och] = s_mm2.v

                def f_ev(e, och=och, mm0=mm0):
                    for hf in range(2):
                        i = och * 2 + hf
                        pspc = [ps1[:], ps2[:], ps3[:, 0:512], ps4[:, 0:512]][i % 4]
                        e.wait_ge(s_mm2.h, mm0 + hf + 1)
                        sl = bufa[och][:, hf * 512:(hf + 1) * 512]
                        e.tensor_tensor(sl, sl, pspc, ALU.add).then_inc(s_ev2.h, 1)
                P.emit("vector", f_ev)
                s_ev2.v += 2

                p6_och(och)
                if och == 1:
                    def f_ldw0(e, spkv=s_spk.v, wtv0=s_wt2.v):
                        e.wait_ge(s_spk.h, spkv)
                        for ic in range(NC):
                            dst = wbig[:, ic * 256:(ic + 1) * 256]
                            e.dma_start_transpose(dst,
                                                  w_scr[0:256, ic * 128:(ic + 1) * 128]).then_inc(s_wt2.h, 16)
                    P.emit("sync", f_ldw0)
                    s_wt2.v += 16 * NC
                    wt_end[0] = s_wt2.v

            bar()

        def p5():
            for hf in range(2):
                ln_stats(lambda c, hf=hf: bufa[c][:, hf * 512:(hf + 1) * 512],
                         s_ph, s_ph.v, s_mm2, s_sq2, s_st2)

                def f_h2(e, hf=hf, sqv=s_sq2.v):
                    e.wait_ge(s_sq2.h, sqv)
                    for c in range(NC):
                        sl_in = bufa[c][:, hf * 512:(hf + 1) * 512]
                        sl_out = bufb[c][:, hf * 512:(hf + 1) * 512]
                        e.tensor_tensor(s1f[:], sl_in, m_rep[:], ALU.subtract)
                        e.tensor_tensor(sl_out, s1f[:], r_rep[:], ALU.mult).then_inc(s_h2.h, 1)
                P.emit("vector", f_h2)
                s_h2.v += NC

            bar()

        NSUB = 8

        def p6_och(och):
            def f_lds(e, och=och):
                e.dma_start(out=sct[:], in_=sc[och * 128:(och + 1) * 128, :]).then_inc(s_x2.h, 16)
                e.dma_start(out=mkt[:], in_=mk[och * 128:(och + 1) * 128, :]).then_inc(s_x2.h, 16)
            P.emit("sync", f_lds)
            s_x2.v += 32

            def f_se(e, xv=s_x2.v, dqv=s_dq.v):
                e.wait_ge(s_x2.h, xv)
                if dqv > 0:
                    e.wait_ge(s_dq.h, dqv)
                e.tensor_scalar(s1f[:, 0:128], mkt[:], 0.5, 0.5, ALU.mult, ALU.add)
                e.tensor_tensor(seff[:], sct[:], s1f[:, 0:128], ALU.mult).then_inc(s_st2.h, 1)
            P.emit("vector", f_se)
            s_st2.v += 1
            se_done = s_st2.v

            for sub in range(NSUB):
                par = sub % 2
                i = och * NSUB + sub

                def f_ldp(e, och=och, sub=sub, par=par, i=i, dqv=s_dq.v):
                    if i >= 2:
                        e.wait_ge(s_dq.h, dqv - 1)
                    e.dma_start(out=pkt[par][:],
                                in_=pk[och * 128:(och + 1) * 128,
                                       sub * 256:(sub + 1) * 256]).then_inc(s_pk.h, 16)
                P.emit("sync", f_ldp)
                s_pk.v += 16

                def f_nib(e, par=par, xv=s_pk.v, dqv=s_dq.v):
                    e.wait_ge(s_pk.h, xv)
                    if dqv >= 2:
                        e.wait_ge(s_dq.h, dqv)
                    e.tensor_scalar(qi32[:, 0:512:2], pkt[par][:], 15, None, ALU.bitwise_and)
                    e.tensor_scalar(qi32[:, 1:512:2], pkt[par][:], 4, None,
                                    ALU.logical_shift_right).then_inc(s_dq.h, 1)
                P.emit("vector", f_nib)
                s_dq.v += 1

                def f_stt(e, sub=sub, par=par, i=i, dqv=s_dq.v,
                          se_done=se_done, spv=s_spk.v):
                    e.wait_ge(s_dq.h, dqv)
                    e.wait_ge(s_st2.h, se_done)
                    if i >= 2:
                        e.wait_ge(s_spk.h, spv - 16)
                    sl = bass.AP(seff[:].tensor, seff[:].offset + sub * 16,
                                 [[128, 128], [1, 16], [0, 32]])
                    qv = qi32[:].rearrange("p (g i) -> p g i", i=32)
                    wv_ = wdt[par][:].rearrange("p (g i) -> p g i", i=32)
                    e.scalar_tensor_tensor(wv_, qv, 8.0, sl,
                                           ALU.subtract, ALU.mult).then_inc(s_dq.h, 1)
                P.emit("vector", f_stt)
                s_dq.v += 1

                def f_st(e, och=och, sub=sub, par=par, dqv=s_dq.v):
                    e.wait_ge(s_dq.h, dqv)
                    e.dma_start(out=w_scr[och * 128:(och + 1) * 128,
                                          sub * 512:(sub + 1) * 512],
                                in_=wdt[par][:]).then_inc(s_spk.h, 16)
                P.emit("gpsimd", f_st)
                s_spk.v += 16

        def p7_pre():
            def f_ldw1(e, spkv=s_spk.v):
                e.wait_ge(s_spk.h, spkv)
                for ic in range(NC):
                    dst = wbig[:, 8192 + ic * 256: 8192 + (ic + 1) * 256]
                    e.dma_start_transpose(dst,
                                          w_scr[256:512, ic * 128:(ic + 1) * 128]).then_inc(s_wt2.h, 16)
            P.emit("sync", f_ldw1)
            s_wt2.v += 16 * NC
            wt_end[1] = s_wt2.v

        def p7():
            p7_ev0 = s_ev2.v
            ffn_end = {}
            for og in range(16):
                par = og % 2
                if og >= 2:
                    def f_ldw(e, og=og, par=par, mmv=ffn_end.get(og - 2, 0)):
                        if mmv > 0:
                            e.wait_ge(s_mm2.h, mmv)
                        for ic in range(NC):
                            dst = wbig[:, par * 8192 + ic * 256: par * 8192 + (ic + 1) * 256]
                            e.dma_start_transpose(dst,
                                                  w_scr[og * 256:(og + 1) * 256,
                                                        ic * 128:(ic + 1) * 128]).then_inc(s_wt2.h, 16)
                    P.emit("sync", f_ldw)
                    s_wt2.v += 16 * NC
                    wt_end[og] = s_wt2.v

                for oc2 in range(2):
                    och = og * 2 + oc2
                    mm0 = s_mm2.v

                    def f_mm(e, par=par, oc2=oc2, och=och, wtv=wt_end[og], p7_ev0=p7_ev0):
                        e.wait_ge(s_wt2.h, wtv)
                        for hf in range(2):
                            i = och * 2 + hf
                            tgt = p7_ev0 + i - 1
                            if i >= 2:
                                e.wait_ge(s_ev2.h, tgt)
                            pspc = [ps1, ps2][hf]
                            for ic in range(NC):
                                lhs = wbig[:, par * 8192 + ic * 256 + oc2 * 128:
                                           par * 8192 + ic * 256 + oc2 * 128 + 128]
                                ins = e.matmul(pspc[:], lhs,
                                               bufb[ic][:, hf * 512:(hf + 1) * 512],
                                               start=(ic == 0), stop=(ic == NC - 1))
                            ins.then_inc(s_mm2.h, 1)
                    P.emit("tensor", f_mm)
                    s_mm2.v += 2
                    ffn_end[og] = s_mm2.v

                    def f_ep(e, och=och, mm0=mm0, outv=s_out.v):
                        for hf in range(2):
                            pspc = [ps1, ps2][hf]
                            e.wait_ge(s_mm2.h, mm0 + hf + 1)
                            i = och * 2 + hf
                            if i >= 2:
                                e.wait_ge(s_out.h, 16 * (i - 1))
                            e.scalar_tensor_tensor(outt[hf][:], pspc[:],
                                                   bias_sb[:, och:och + 1],
                                                   bufa[och][:, hf * 512:(hf + 1) * 512],
                                                   ALU.add, ALU.add).then_inc(s_ev2.h, 1)
                    P.emit("vector", f_ep)
                    s_ev2.v += 2

                    def f_st(e, och=och, evv=s_ev2.v):
                        for hf in range(2):
                            e.wait_ge(s_ev2.h, evv - 1 + hf)
                            e.dma_start(out=out[och * 128:(och + 1) * 128,
                                                hf * 512:(hf + 1) * 512],
                                        in_=outt[hf][:]).then_inc(s_out.h, 16)
                    P.emit("sync", f_st)
                    s_out.v += 32

            def f_fin(e, ov=s_out.v):
                e.wait_ge(s_out.h, ov)
            P.emit("sync", f_fin)

        def dump_scr():
            if not DEBUG:
                return
            def f(e, xv=s_x2.v):
                e.dma_start(out=dbg["d_kscr"], in_=k_scr).then_inc(s_x2.h, 16)
                e.dma_start(out=dbg["d_vscr"], in_=v_scr).then_inc(s_x2.h, 16)
                e.dma_start(out=dbg["d_hscr"], in_=h_scr).then_inc(s_x2.h, 16)
                e.dma_start(out=dbg["d_wscr"], in_=w_scr).then_inc(s_x2.h, 16)
                e.wait_ge(s_x2.h, xv + 64)
            P.emit("gpsimd", f)
            s_x2.v += 64
            bar()

        def dump_sbuf(key, tiles):
            if not DEBUG:
                return
            def f(e, key=key, tiles=tiles, base=s_x2.v):
                for c in range(NC):
                    e.dma_start(out=dbg[key][c * 128:(c + 1) * 128, :],
                                in_=tiles[c]).then_inc(s_x2.h, 16)
                e.wait_ge(s_x2.h, base + 16 * NC)
            P.emit("gpsimd", f)
            s_x2.v += 16 * NC
            bar()

        NPH = int(os.environ.get("KPHASES", "10"))
        phases = [lambda: pass_a(0), lambda: pass_a(1), lambda: pass_b(0),
                  lambda: pass_b(1), p3, p4, p_wo, p7_pre, p5, p7]
        names = ["A1", "A2", "B1", "B2", "P3", "P4", "WO", "PRE", "P5", "P7"]
        for i in range(min(NPH, 10)):
            phases[i]()
            if DEBUG:
                if names[i] == "P3":
                    dump_sbuf("d_q", bufa)
                elif names[i] == "P4":
                    dump_sbuf("d_ctx", bufb)
                elif names[i] == "WO":
                    dump_sbuf("d_x2", bufa)
                elif names[i] == "P5":
                    dump_sbuf("d_h2", bufb)
        if DEBUG:
            dump_scr()
        if NPH < 10:
            def f_dummy(e):
                e.memset(outt[0][:], 0.0).then_inc(s_st2.h, 1)
            P.emit("vector", f_dummy)
            s_st2.v += 1

            def f_fin0(e, stv=s_st2.v, ov=s_out.v):
                e.wait_ge(s_st2.h, stv)
                e.dma_start(out=out[0:128, 0:512], in_=outt[0][:]).then_inc(s_out.h, 16)
                e.wait_ge(s_out.h, ov + 16)
            P.emit("sync", f_fin0)
            s_out.v += 16

        block = es.enter_context(nc.Block())
        block.sync(lambda e: P.replay("sync", e))
        block.gpsimd(lambda e: P.replay("gpsimd", e))
        block.vector(lambda e: P.replay("vector", e))
        block.scalar(lambda e: P.replay("scalar", e))
        block.tensor(lambda e: P.replay("tensor", e))

    return nc


def _perm_for(q):
    A, B = q, 7 - q
    own = list(range(A * 512, (A + 1) * 512)) + list(range(B * 512, (B + 1) * 512))
    own_set = set(own)
    others = [t for t in range(T) if t not in own_set]
    return np.array(own + others, dtype=np.int64)


def _rope_tables(perm):
    inv = 1.0 / (500000.0 ** (np.arange(0, HD, 2, dtype=np.float64) / HD))
    t = np.arange(T, dtype=np.float64)
    fr = np.outer(t, inv)
    cos = np.cos(fr).astype(np.float32)[perm]
    sin = np.sin(fr).astype(np.float32)[perm]
    t1 = np.concatenate([cos.T, cos.T], axis=0).astype(ml_dtypes.bfloat16)
    t2 = np.concatenate([-sin.T, sin.T], axis=0).astype(ml_dtypes.bfloat16)
    return np.ascontiguousarray(t1), np.ascontiguousarray(t2)


_NC_CACHE = {}


def kernel(L0_attn_x, L0_attn_Wq, L0_attn_Wk, L0_attn_Wv, L0_attn_Wo,
           L0_ffn_weight_packed, L0_ffn_weight_scale, L0_ffn_weight_var_mask,
           L0_ffn_bias):
    x = np.asarray(L0_attn_x, dtype=np.float32)
    wq = np.ascontiguousarray(np.asarray(L0_attn_Wq, dtype=np.float32))
    wk = np.ascontiguousarray(np.asarray(L0_attn_Wk, dtype=np.float32))
    wv = np.ascontiguousarray(np.asarray(L0_attn_Wv, dtype=np.float32))
    wo = np.ascontiguousarray(np.asarray(L0_attn_Wo, dtype=np.float32))
    pk_a = np.ascontiguousarray(np.asarray(L0_ffn_weight_packed, dtype=np.int32).reshape(4096, 2048))
    sc_a = np.ascontiguousarray(np.asarray(L0_ffn_weight_scale).reshape(4096, 128))
    mk_a = np.ascontiguousarray(np.asarray(L0_ffn_weight_var_mask).reshape(4096, 128).view(np.uint8))
    bias_pp = np.ascontiguousarray(np.asarray(L0_ffn_bias, dtype=np.float32).reshape(32, 128).T)

    tri = np.zeros((128, 128), dtype=ml_dtypes.bfloat16)
    for kk in range(128):
        tri[kk, kk:] = 1.0
    swpm = np.zeros((128, 128), dtype=ml_dtypes.bfloat16)
    for m in range(128):
        swpm[(m + 64) % 128, m] = 1.0

    if "nc" not in _NC_CACHE:
        _NC_CACHE["nc"] = build_nc()
    nc = _NC_CACHE["nc"]

    in_maps = []
    perms = []
    for c in range(8):
        b, q = c // 4, c % 4
        perm = _perm_for(q)
        perms.append(perm)
        xTc = np.ascontiguousarray(x[b][perm].T)
        t1, t2 = _rope_tables(perm)
        in_maps.append(dict(
            xT=xTc, wq=wq, wk=wk, wv=wv, wo=wo, pk=pk_a, sc=sc_a, mk=mk_a,
            bias_in=bias_pp, t1_in=t1, t2_in=t2, tri_in=tri, swp_in=swpm,
        ))

    res = run_bass_kernel_spmd(nc, in_maps, core_ids=list(range(8)))

    outa = np.empty((2, T, D), dtype=np.float32)
    for c in range(8):
        b = c // 4
        o = np.asarray(res.results[c]["o"])
        outa[b][perms[c][:OWN]] = o.T
    if DEBUG:
        kernel.last_res = res
        kernel.last_perms = perms
    return outa


# version 10
# speedup vs baseline: 1.0711x; 1.0671x over previous
import os
import math
import contextlib
import numpy as np
import ml_dtypes

import concourse.bass as bass
import concourse.mybir as mybir
from concourse.bass_utils import run_bass_kernel_spmd

F32 = mybir.dt.float32
BF16 = mybir.dt.bfloat16
I32 = mybir.dt.int32
F16 = mybir.dt.float16
U8 = mybir.dt.uint8
AF = mybir.ActivationFunctionType
ALU = mybir.AluOpType

D = 4096
T = 4096
NH = 32
NKV = 8
HD = 128
NC = 32
OWN = 1024
NBLK = 8
SCALE = 1.0 / math.sqrt(HD)
EXP_BIAS = -15.0
EPS = 1e-5

ENGS = ("sync", "gpsimd", "vector", "scalar", "tensor")

DEBUG = False


def _variant_tables(q):
    A, B = q, 7 - q
    own = list(range(4 * A, 4 * A + 4)) + list(range(4 * B, 4 * B + 4))
    others = [g for g in range(32) if g not in own]
    j0 = [sum(1 for o in own if o < g) for g in others]
    return own, others, j0


class Sem:
    def __init__(self, es, nc, name):
        self.h = es.enter_context(nc.semaphore(name))
        self.v = 0


class Prog:
    def __init__(self):
        self.ops = {e: [] for e in ENGS}

    def emit(self, eng, fn):
        self.ops[eng].append(("op", fn))

    def if_open(self, v):
        for e in ENGS:
            self.ops[e].append(("if", v))

    def if_close(self):
        for e in ENGS:
            self.ops[e].append(("endif", None))

    def replay(self, eng_name, eng):
        need_if = any(k == "if" for k, _ in self.ops[eng_name])
        vmod = None
        if need_if:
            pid = eng.to_reg(eng.partition_id())
            vmod = eng.alloc_register(f"vm_{eng_name}")
            eng.reg_mod(vmod, pid, 4)
        guards = []
        for kind, payload in self.ops[eng_name]:
            if kind == "op":
                payload(eng)
            elif kind == "if":
                g = eng.If_eq(vmod, payload)
                g.__enter__()
                guards.append(g)
            else:
                guards.pop().__exit__(None, None, None)
        eng.end_ifs()


def build_nc():
    nc = bass.Bass("TRN2", target_bir_lowering=False, debug=False)

    xT = nc.dram_tensor("xT", [D, T], F32, kind="ExternalInput").ap()
    wq = nc.dram_tensor("wq", [D, D], F32, kind="ExternalInput").ap()
    wk = nc.dram_tensor("wk", [D, 1024], F32, kind="ExternalInput").ap()
    wv = nc.dram_tensor("wv", [D, 1024], F32, kind="ExternalInput").ap()
    wo = nc.dram_tensor("wo", [D, D], F32, kind="ExternalInput").ap()
    pk = nc.dram_tensor("pk", [4096, 2048], I32, kind="ExternalInput").ap()
    sc = nc.dram_tensor("sc", [4096, 128], F16, kind="ExternalInput").ap()
    mk = nc.dram_tensor("mk", [4096, 128], U8, kind="ExternalInput").ap()
    bias_in = nc.dram_tensor("bias_in", [128, 32], F32, kind="ExternalInput").ap()
    t1_in = nc.dram_tensor("t1_in", [128, T], BF16, kind="ExternalInput").ap()
    t2_in = nc.dram_tensor("t2_in", [128, T], BF16, kind="ExternalInput").ap()
    tri_in = nc.dram_tensor("tri_in", [128, 128], BF16, kind="ExternalInput").ap()
    swp_in = nc.dram_tensor("swp_in", [128, 128], BF16, kind="ExternalInput").ap()

    out = nc.dram_tensor("o", [D, OWN], F32, kind="ExternalOutput").ap()

    k_scr = nc.dram_tensor("k_scr", [1024, T], BF16, kind="Internal").ap()
    v_scr = nc.dram_tensor("v_scr", [T, 1024], BF16, kind="Internal").ap()
    h_scr = nc.dram_tensor("h_scr", [D, T], BF16, kind="Internal").ap()
    w_scr = nc.dram_tensor("w_scr", [4096, 4096], BF16, kind="Internal").ap()

    dbg = {}
    if DEBUG:
        for nm, shp in [("d_kscr", [1024, T]), ("d_vscr", [T, 1024]),
                        ("d_hscr", [D, T]), ("d_wscr", [4096, 4096]),
                        ("d_q", [D, OWN]), ("d_ctx", [D, OWN]),
                        ("d_x2", [D, OWN]), ("d_h2", [D, OWN])]:
            dbg[nm] = nc.dram_tensor(nm, shp, BF16, kind="ExternalOutput").ap()

    es = contextlib.ExitStack()
    with es:
        def sbt(name, shape, dtype):
            return es.enter_context(nc.sbuf_tensor(name, shape, dtype))

        bufa_t = sbt("bufa", [128, NC * OWN], BF16)
        bufb_t = sbt("bufb", [128, NC * OWN], BF16)
        bufa = [bufa_t[:, i * OWN:(i + 1) * OWN] for i in range(NC)]
        bufb = [bufb_t[:, i * OWN:(i + 1) * OWN] for i in range(NC)]
        WBC = 16384
        wbig = sbt("wbig", [128, WBC], BF16)

        t1b = sbt("t1b", [128, OWN], BF16)
        t2b = sbt("t2b", [128, OWN], BF16)
        kout = [sbt(f"kout{i}", [128, 512], BF16) for i in range(2)]
        s1f = sbt("s1f", [128, 512], F32)
        x2t = [sbt(f"x2t{i}", [128, 512], BF16) for i in range(2)]
        m_rep = sbt("m_rep", [128, 512], F32)
        r_rep = sbt("r_rep", [128, 512], F32)
        mrow = sbt("mrow", [1, 512], F32)
        rrow = sbt("rrow", [1, 512], F32)
        p1s = sbt("p1s", [128, 512], BF16)
        p2s = sbt("p2s", [128, 512], BF16)
        psb = [sbt(f"psb{i}", [128, OWN], BF16) for i in range(2)]
        recf = [sbt(f"recf{i}", [128, 512], F32) for i in range(2)]
        outt = [sbt(f"outt{i}", [128, 512], F32) for i in range(2)]
        pkt = [sbt(f"pkt{i}", [128, 256], I32) for i in range(2)]
        qi32 = sbt("qi32", [128, 512], I32)
        wdt = [sbt(f"wdt{i}", [128, 512], BF16) for i in range(2)]
        sct = sbt("sct", [128, 128], F16)
        mkt = sbt("mkt", [128, 128], U8)
        seff = sbt("seff", [128, 128], F32)
        tri = sbt("tri_sb", [128, 128], BF16)
        swp = sbt("swp_sb", [128, 128], BF16)
        kraw = [sbt(f"kraw{i}", [128, 512], BF16) for i in range(2)]
        ones_bf = sbt("ones_bf", [128, 1], BF16)
        ones_f = sbt("ones_f", [1, 128], F32)
        b15 = sbt("b15", [128, 1], F32)
        beps = sbt("beps", [1, 1], F32)
        bias_sb = sbt("bias_sb", [128, 32], F32)
        den_sb = sbt("den_sb", [1, OWN], F32)

        ps1 = es.enter_context(nc.psum_tensor("ps1", [128, 512], F32))
        ps2 = es.enter_context(nc.psum_tensor("ps2", [128, 512], F32))
        ps3 = es.enter_context(nc.psum_tensor("ps3", [128, 1024], F32))
        ps4 = es.enter_context(nc.psum_tensor("ps4", [128, 1024], F32))
        psd = es.enter_context(nc.psum_tensor("psd", [1, 1024], F32))

        s_x = Sem(es, nc, "s_x")
        s_wt = Sem(es, nc, "s_wt")
        s_tb = Sem(es, nc, "s_tb")
        s_mm = Sem(es, nc, "s_mm")
        s_ct = Sem(es, nc, "s_ct")
        s_sq = Sem(es, nc, "s_sq")
        s_st = Sem(es, nc, "s_st")
        s_h1 = Sem(es, nc, "s_h1")
        s_ev = Sem(es, nc, "s_ev")
        s_ex = Sem(es, nc, "s_ex")
        s_mk = Sem(es, nc, "s_mk")
        s_spk = Sem(es, nc, "s_spk")
        s_sph = Sem(es, nc, "s_sph")
        s_kv = Sem(es, nc, "s_kv")
        s_dq = Sem(es, nc, "s_dq")
        s_out = Sem(es, nc, "s_out")
        s_ph = Sem(es, nc, "s_ph")
        s_wt2 = Sem(es, nc, "s_wt2")
        s_mm2 = Sem(es, nc, "s_mm2")
        s_ev2 = Sem(es, nc, "s_ev2")
        s_x2 = Sem(es, nc, "s_x2")
        s_sq2 = Sem(es, nc, "s_sq2")
        s_st2 = Sem(es, nc, "s_st2")
        s_h2 = Sem(es, nc, "s_h2")
        s_pk = Sem(es, nc, "s_pk")
        s_sw = Sem(es, nc, "s_sw")
        s_swp = [Sem(es, nc, f"s_swp{i}") for i in range(2)]
        s_spkp = [Sem(es, nc, f"s_spkp{i}") for i in range(2)]
        s_xp = [Sem(es, nc, f"s_xp{i}") for i in range(2)]
        s_sphp = [Sem(es, nc, f"s_sphp{i}") for i in range(2)]
        s_wqp = [Sem(es, nc, f"s_wqp{i}") for i in range(2)]
        s_wop = [Sem(es, nc, f"s_wop{i}") for i in range(2)]
        s_ftp = [Sem(es, nc, f"s_ftp{i}") for i in range(2)]
        s_kvp = [Sem(es, nc, f"s_kvp{i}") for i in range(2)]
        s_tbp = [Sem(es, nc, f"s_tbp{i}") for i in range(2)]
        s_pkp = [Sem(es, nc, f"s_pkp{i}") for i in range(2)]

        ALL_SEMS = [s_x, s_wt, s_tb, s_mm, s_ct, s_sq, s_st, s_h1, s_ev,
                    s_ex, s_mk, s_spk, s_sph, s_kv, s_dq, s_out, s_ph,
                    s_wt2, s_mm2, s_ev2, s_x2, s_sq2, s_st2, s_h2, s_pk, s_sw,
                    *s_swp, *s_spkp, *s_xp, *s_sphp, *s_wqp, *s_wop, *s_ftp, *s_kvp, *s_tbp, *s_pkp]

        P = Prog()

        def drain_waits(eng, vals):
            for sem, v in vals:
                if v > 0:
                    eng.wait_ge(sem.h, v)

        def bar():
            tgt = s_ph.v + 5
            vals = [(sem, sem.v) for sem in ALL_SEMS if sem is not s_ph]
            for e in ENGS:
                def f(eng, tgt=tgt, vals=vals):
                    drain_waits(eng, vals)
                    eng.nop().then_inc(s_ph.h, 1)
                    eng.wait_ge(s_ph.h, tgt)
                P.emit(e, f)
            s_ph.v = tgt

        def xb(p, c):
            return bufa_t[:, c * OWN + p * 512: c * OWN + (p + 1) * 512]

        def xgrp_ap(p, g):
            return bass.AP(bufa_t[:].tensor, (g * 4) * OWN + p * 512,
                           [[NC * OWN, 128], [OWN, 4], [1, 512]])

        def f(e):
            e.dma_start(out=tri[:], in_=tri_in).then_inc(s_x.h, 16)
            e.dma_start(out=swp[:], in_=swp_in).then_inc(s_x.h, 16)
            e.dma_start(out=bias_sb[:], in_=bias_in).then_inc(s_x.h, 16)
            e.wait_ge(s_x.h, 48)
        P.emit("sync", f)
        s_x.v += 48

        def f(e):
            e.memset(ones_bf[:], 1.0)
            e.memset(ones_f[:], 1.0)
            e.memset(b15[:], EXP_BIAS)
            e.memset(beps[:], EPS).then_inc(s_st.h, 1)
        P.emit("vector", f)
        s_st.v += 1
        bar()

        def ln_stats(tiles, xsem, xval, smm, ssq, sst):
            sq0 = ssq.v
            mm0 = smm.v

            def f_sq(e, sq0=sq0, mm0=mm0):
                e.wait_ge(xsem.h, xval)
                for c in range(NC):
                    if c >= 2:
                        e.wait_ge(smm.h, mm0 + c - 1)
                    e.activation(x2t[c % 2][:], tiles(c), AF.Square).then_inc(ssq.h, 1)
            P.emit("scalar", f_sq)
            ssq.v += NC

            def f_mm(e, sq0=sq0, stv0=sst.v):
                e.wait_ge(xsem.h, xval)
                if stv0 > 0:
                    e.wait_ge(sst.h, stv0)
                for c in range(NC):
                    e.matmul(psd[:1, 0:512], ones_bf[:], tiles(c),
                             start=(c == 0), stop=(c == NC - 1))
                    e.wait_ge(ssq.h, sq0 + c + 1)
                    e.matmul(psd[:1, 512:1024], ones_bf[:], x2t[c % 2][:],
                             start=(c == 0), stop=(c == NC - 1)).then_inc(smm.h, 1)
            P.emit("tensor", f_mm)
            smm.v += NC

            def f_vs(e, mmv=smm.v):
                e.wait_ge(smm.h, mmv)
                e.tensor_scalar(mrow[:], psd[:1, 0:512], 1.0 / D, None, ALU.mult)
                e.tensor_scalar(rrow[:], psd[:1, 512:1024], 1.0 / D, None, ALU.mult)
                e.tensor_tensor(s1f[:1, :], mrow[:], mrow[:], ALU.mult)
                e.tensor_tensor(rrow[:], rrow[:], s1f[:1, :], ALU.subtract).then_inc(sst.h, 1)
            P.emit("vector", f_vs)
            sst.v += 1

            def f_sqrt(e, stv=sst.v):
                e.wait_ge(sst.h, stv)
                e.activation(rrow[:], rrow[:], AF.Sqrt, bias=beps[:]).then_inc(ssq.h, 1)
            P.emit("scalar", f_sqrt)
            ssq.v += 1

            def f_rec(e, sqv=ssq.v):
                e.wait_ge(ssq.h, sqv)
                e.reciprocal(rrow[:], rrow[:]).then_inc(sst.h, 1)
            P.emit("vector", f_rec)
            sst.v += 1

            def f_bc(e, stv=sst.v, evv=s_ev.v):
                e.wait_ge(sst.h, stv)
                if evv > 0:
                    e.wait_ge(s_ev.h, evv)
                e.matmul(ps1[:], ones_f[:], mrow[:], start=True, stop=True)
                e.matmul(ps2[:], ones_f[:], rrow[:], start=True, stop=True).then_inc(smm.h, 1)
            P.emit("tensor", f_bc)
            smm.v += 1

            def f_cp(e, mmv=smm.v):
                e.wait_ge(smm.h, mmv)
                e.copy(m_rep[:], ps1[:])
                e.copy(r_rep[:], ps2[:]).then_inc(ssq.h, 1)
            P.emit("scalar", f_cp)
            ssq.v += 1

        def wks(s, c):
            return wbig[:, s * 4096 + c * 128: s * 4096 + (c + 1) * 128]

        def wvs(c):
            return wbig[:, c * 512:(c + 1) * 512]

        def wqs(par, c):
            return wbig[:, par * 4096 + c * 128: par * 4096 + (c + 1) * 128]

        NSUB = 8

        def p6_och(och):
            def f_lds(e, och=och, stv=s_st2.v):
                if stv > 0:
                    e.wait_ge(s_st2.h, stv)
                e.dma_start(out=sct[:], in_=sc[och * 128:(och + 1) * 128, :]).then_inc(s_x2.h, 16)
                e.dma_start(out=mkt[:], in_=mk[och * 128:(och + 1) * 128, :]).then_inc(s_x2.h, 16)
            P.emit("sync", f_lds)
            s_x2.v += 32

            def f_se(e, xv=s_x2.v, dqv=s_dq.v):
                e.wait_ge(s_x2.h, xv)
                if dqv > 0:
                    e.wait_ge(s_dq.h, dqv)
                e.tensor_scalar(s1f[:, 0:128], mkt[:], 0.5, 0.5, ALU.mult, ALU.add)
                e.tensor_tensor(seff[:], sct[:], s1f[:, 0:128], ALU.mult).then_inc(s_st2.h, 1)
            P.emit("vector", f_se)
            s_st2.v += 1
            se_done = s_st2.v

            for sub in range(NSUB):
                par = sub % 2
                i = och * NSUB + sub

                def f_ldp(e, och=och, sub=sub, par=par, i=i, dqv=s_dq.v):
                    if i >= 2:
                        e.wait_ge(s_dq.h, dqv - 1)
                    e.dma_start(out=pkt[par][:],
                                in_=pk[och * 128:(och + 1) * 128,
                                       sub * 256:(sub + 1) * 256]).then_inc(s_pkp[par].h, 16)
                P.emit("sync", f_ldp)
                s_pkp[par].v += 16

                def f_nib(e, par=par, xv=s_pkp[par].v, dqv=s_dq.v):
                    e.wait_ge(s_pkp[par].h, xv)
                    if dqv >= 2:
                        e.wait_ge(s_dq.h, dqv)
                    e.tensor_scalar(qi32[:, 0:512:2], pkt[par][:], 15, None, ALU.bitwise_and)
                    e.tensor_scalar(qi32[:, 1:512:2], pkt[par][:], 4, None,
                                    ALU.logical_shift_right).then_inc(s_dq.h, 1)
                P.emit("vector", f_nib)
                s_dq.v += 1

                def f_stt(e, sub=sub, par=par, i=i, dqv=s_dq.v,
                          se_done=se_done, spv=s_swp[par].v):
                    e.wait_ge(s_dq.h, dqv)
                    e.wait_ge(s_st2.h, se_done)
                    if spv > 0:
                        e.wait_ge(s_swp[par].h, spv)
                    sl = bass.AP(seff[:].tensor, seff[:].offset + sub * 16,
                                 [[128, 128], [1, 16], [0, 32]])
                    qv = qi32[:].rearrange("p (g i) -> p g i", i=32)
                    wv_ = wdt[par][:].rearrange("p (g i) -> p g i", i=32)
                    e.scalar_tensor_tensor(wv_, qv, 8.0, sl,
                                           ALU.subtract, ALU.mult).then_inc(s_dq.h, 1)
                P.emit("vector", f_stt)
                s_dq.v += 1

                def f_st(e, och=och, sub=sub, par=par, dqv=s_dq.v):
                    e.wait_ge(s_dq.h, dqv)
                    e.dma_start(out=w_scr[och * 128:(och + 1) * 128,
                                          sub * 512:(sub + 1) * 512],
                                in_=wdt[par][:]).then_inc(s_swp[par].h, 16)
                P.emit("gpsimd", f_st)
                s_swp[par].v += 16

        def pass_a(half):
            wt0 = s_wt.v

            def f_ldwk(e, half=half):
                for c in range(NC):
                    src = wk[c * 128:(c + 1) * 128,
                             half * 512:(half + 1) * 512].rearrange("p (s o) -> p s o", s=4)
                    dst = bass.AP(wbig[:].tensor, c * 128,
                                  [[WBC, 128], [4096, 4], [1, 128]])
                    e.dma_start(out=dst, in_=src).then_inc(s_wt.h, 16)
            P.emit("gpsimd", f_ldwk)
            s_wt.v += 16 * NC

            sph_end = {}
            for blk in range(NBLK):
                par = blk % 2

                if half == 0:
                    def f_ldx(e, blk=blk, par=par, spv=sph_end.get(blk - 2, 0)):
                        if spv > 0:
                            e.wait_ge(s_sphp[par].h, spv)
                        for g in range(8):
                            src = xT.rearrange("(c p) t -> p c t", p=128)[
                                :, g * 4:(g + 1) * 4, blk * 512:(blk + 1) * 512]
                            e.dma_start(out=xgrp_ap(par, g), in_=src).then_inc(s_xp[par].h, 16)
                    P.emit("gpsimd", f_ldx)
                    s_xp[par].v += 16 * 8

                    ln_stats(lambda c, par=par: xb(par, c), s_xp[par], s_xp[par].v,
                             s_mm, s_sq, s_st)

                    def f_h1(e, par=par, sqv=s_sq.v):
                        e.wait_ge(s_sq.h, sqv)
                        for c in range(NC):
                            e.tensor_tensor(s1f[:], xb(par, c), m_rep[:], ALU.subtract)
                            e.tensor_tensor(xb(par, c), s1f[:], r_rep[:], ALU.mult).then_inc(s_h1.h, 1)
                    P.emit("vector", f_h1)
                    s_h1.v += NC

                    if blk < 2:
                        def f_own(e, par=par, blk=blk, h1v=s_h1.v):
                            for c in range(NC):
                                e.wait_ge(s_h1.h, h1v - NC + c + 1)
                                ins = e.copy(bufb[c][:, blk * 512:(blk + 1) * 512], xb(par, c))
                            ins.then_inc(s_sq.h, 1)
                        P.emit("scalar", f_own)
                        s_sq.v += 1
                else:
                    def f_ldh(e, blk=blk, par=par, mmv=s_mm.v,
                              sp0=s_sphp[0].v, sp1=s_sphp[1].v):
                        e.wait_ge(s_mm.h, mmv)
                        if sp0 > 0:
                            e.wait_ge(s_sphp[0].h, sp0)
                        if sp1 > 0:
                            e.wait_ge(s_sphp[1].h, sp1)
                        for g in range(8):
                            src = h_scr.rearrange("(c p) t -> p c t", p=128)[
                                :, g * 4:(g + 1) * 4, blk * 512:(blk + 1) * 512]
                            e.dma_start(out=xgrp_ap(par, g), in_=src).then_inc(s_xp[par].h, 16)
                    P.emit("sync", f_ldh)
                    s_xp[par].v += 16 * 8

                def f_ldt(e, blk=blk, par=par, evv=max(0, s_ev.v - 4)):
                    if evv > 0:
                        e.wait_ge(s_ev.h, evv)
                    e.dma_start(out=t1b[:, par * 512:(par + 1) * 512],
                                in_=t1_in[:, blk * 512:(blk + 1) * 512]).then_inc(s_tbp[par].h, 16)
                    e.dma_start(out=t2b[:, par * 512:(par + 1) * 512],
                                in_=t2_in[:, blk * 512:(blk + 1) * 512]).then_inc(s_tbp[par].h, 16)
                P.emit("sync", f_ldt)
                s_tbp[par].v += 32

                kmm0 = s_mm.v
                h1v = s_h1.v
                xv = s_x.v
                kev_base = s_ev.v
                ksq0 = s_sq.v

                def f_kmm(e, par=par, wt0=wt0, h1v=h1v, xv=s_xp[par].v,
                          ksq0=ksq0, half=half, kev_base=kev_base):
                    e.wait_ge(s_wt.h, wt0 + 16 * NC)
                    if half == 0:
                        e.wait_ge(s_h1.h, h1v)
                    else:
                        e.wait_ge(s_xp[par].h, xv)
                    for s in range(4):
                        tgt = ksq0 + s - 1
                        if tgt > 0:
                            e.wait_ge(s_sq.h, tgt)
                        pspc = [ps3[:, 0:512], ps4[:, 0:512]][s % 2]
                        for c in range(NC):
                            ins = e.matmul(pspc, wks(s, c), xb(par, c),
                                           start=(c == 0), stop=(c == NC - 1))
                        ins.then_inc(s_mm.h, 1)
                    for s in range(4):
                        e.wait_ge(s_sq.h, ksq0 + s + 1)
                        if kev_base + s > 0:
                            e.wait_ge(s_ev.h, kev_base + s)
                        e.matmul(ps1[:], swp[:], kraw[s % 2][:],
                                 start=True, stop=True).then_inc(s_mm.h, 1)
                P.emit("tensor", f_kmm)
                s_mm.v += 8

                def f_kev(e, kmm0=kmm0, ksq0=ksq0, kev_base=kev_base):
                    for s in range(4):
                        pspc = [ps3[:, 0:512], ps4[:, 0:512]][s % 2]
                        e.wait_ge(s_mm.h, kmm0 + s + 1)
                        if kev_base + s - 1 > 0:
                            e.wait_ge(s_ev.h, kev_base + s - 1)
                        e.copy(kraw[s % 2][:], pspc).then_inc(s_sq.h, 1)
                P.emit("scalar", f_kev)
                s_sq.v += 4

                kev0 = s_ev.v

                def f_kro(e, par=par, kmm0=kmm0, tbv=s_tbp[par].v,
                          sp0=s_spkp[0].v, sp1=s_spkp[1].v):
                    e.wait_ge(s_tbp[par].h, tbv)
                    for s in range(4):
                        e.wait_ge(s_mm.h, kmm0 + 4 + s + 1)
                        tgt = [sp0, sp1][s % 2] + (16 if s >= 2 else 0)
                        if tgt > 0:
                            e.wait_ge(s_spkp[s % 2].h, tgt)
                        ko = kout[s % 2]
                        e.tensor_tensor(p1s[:], kraw[s % 2][:], t1b[:, par * 512:(par + 1) * 512], ALU.mult)
                        e.tensor_tensor(p2s[:], ps1[:], t2b[:, par * 512:(par + 1) * 512], ALU.mult)
                        e.tensor_tensor(ko[:], p1s[:], p2s[:], ALU.add).then_inc(s_ev.h, 1)
                P.emit("vector", f_kro)
                s_ev.v += 4

                def f_kst(e, kev0=kev0, blk=blk, half=half):
                    for s in range(4):
                        kvh = half * 4 + s
                        e.wait_ge(s_ev.h, kev0 + s + 1)
                        e.dma_start(out=k_scr[kvh * 128:(kvh + 1) * 128,
                                              blk * 512:(blk + 1) * 512],
                                    in_=kout[s % 2][:]).then_inc(s_spkp[s % 2].h, 16)
                P.emit("sync", f_kst)
                s_spkp[0].v += 16 * 2
                s_spkp[1].v += 16 * 2

                if half == 1:
                    for _j in range(4):
                        p6_och(blk * 4 + _j)

                if half == 0:
                    def f_sp(e, par=par, blk=blk, mmv=s_mm.v):
                        e.wait_ge(s_mm.h, mmv)
                        for g in range(8):
                            dst = h_scr.rearrange("(c p) t -> p c t", p=128)[
                                :, g * 4:(g + 1) * 4, blk * 512:(blk + 1) * 512]
                            e.dma_start(out=dst, in_=xgrp_ap(par, g)).then_inc(s_sphp[par].h, 16)
                    P.emit("gpsimd", f_sp)
                    s_sphp[par].v += 16 * 8
                    sph_end[blk] = s_sphp[par].v

            bar()

        def pass_b(vh):
            def f_ldwv(e, vh=vh):
                for c in range(NC):
                    e.dma_start(out=wvs(c),
                                in_=wv[c * 128:(c + 1) * 128,
                                       vh * 512:(vh + 1) * 512]).then_inc(s_wt.h, 16)
            P.emit("gpsimd", f_ldwv)
            wvdone = s_wt.v + 16 * NC
            s_wt.v += 16 * NC

            for blk in range(NBLK):
                par = blk % 2

                def f_ldh(e, blk=blk, par=par, mmv=s_mm.v,
                          sp0=s_sphp[0].v, sp1=s_sphp[1].v):
                    e.wait_ge(s_mm.h, mmv)
                    if sp0 > 0:
                        e.wait_ge(s_sphp[0].h, sp0)
                    if sp1 > 0:
                        e.wait_ge(s_sphp[1].h, sp1)
                    for g in range(8):
                        src = h_scr.rearrange("(c p) t -> p c t", p=128)[
                            :, g * 4:(g + 1) * 4, blk * 512:(blk + 1) * 512]
                        e.dma_start(out=xgrp_ap(par, g), in_=src).then_inc(s_xp[par].h, 16)
                P.emit("sync", f_ldh)
                s_xp[par].v += 16 * 8

                vmm0 = s_mm.v

                def f_vmm(e, par=par, xv=s_xp[par].v, wvdone=wvdone, ev0=s_ev.v):
                    e.wait_ge(s_wt.h, wvdone)
                    e.wait_ge(s_xp[par].h, xv)
                    for tc in range(4):
                        tgt = ev0 + tc - 1
                        if tgt > 0:
                            e.wait_ge(s_ev.h, tgt)
                        pspc = [ps3[:, 0:512], ps4[:, 0:512]][tc % 2]
                        for c in range(NC):
                            ins = e.matmul(pspc, xb(par, c)[:, tc * 128:(tc + 1) * 128],
                                           wvs(c), start=(c == 0), stop=(c == NC - 1))
                        ins.then_inc(s_mm.h, 1)
                P.emit("tensor", f_vmm)
                s_mm.v += 4

                vev0 = s_ev.v

                def f_vev(e, vmm0=vmm0, sp0=s_spkp[0].v, sp1=s_spkp[1].v):
                    for tc in range(4):
                        pspc = [ps3[:, 0:512], ps4[:, 0:512]][tc % 2]
                        e.wait_ge(s_mm.h, vmm0 + tc + 1)
                        tgt = [sp0, sp1][tc % 2] + (16 if tc >= 2 else 0)
                        if tgt > 0:
                            e.wait_ge(s_spkp[tc % 2].h, tgt)
                        e.tensor_copy(kout[tc % 2][:], pspc).then_inc(s_ev.h, 1)
                P.emit("vector", f_vev)
                s_ev.v += 4

                def f_vst(e, vev0=vev0, blk=blk, vh=vh):
                    for tc in range(4):
                        e.wait_ge(s_ev.h, vev0 + tc + 1)
                        e.dma_start(out=v_scr[blk * 512 + tc * 128: blk * 512 + (tc + 1) * 128,
                                              vh * 512:(vh + 1) * 512],
                                    in_=kout[tc % 2][:]).then_inc(s_spkp[tc % 2].h, 16)
                P.emit("gpsimd", f_vst)
                s_spkp[0].v += 16 * 2
                s_spkp[1].v += 16 * 2

            bar()

        def p3():
            def f_ldt(e):
                e.dma_start(out=t1b[:], in_=t1_in[:, 0:OWN]).then_inc(s_tb.h, 16)
                e.dma_start(out=t2b[:], in_=t2_in[:, 0:OWN]).then_inc(s_tb.h, 16)
            P.emit("sync", f_ldt)
            s_tb.v += 32

            qmm_end = {}
            for qh in range(NH):
                par = qh % 2

                def f_ldq(e, qh=qh, par=par, mmv=qmm_end.get(qh - 2, s_mm.v)):
                    e.wait_ge(s_mm.h, mmv)
                    for g in range(8):
                        src = wq.rearrange("(c p) o -> p c o", p=128)[
                            :, g * 4:(g + 1) * 4, qh * 128:(qh + 1) * 128]
                        dst = bass.AP(wbig[:].tensor, par * 4096 + (g * 4) * 128,
                                      [[WBC, 128], [128, 4], [1, 128]])
                        e.dma_start(out=dst, in_=src).then_inc(s_wqp[par].h, 16)
                P.emit("gpsimd", f_ldq)
                s_wqp[par].v += 16 * 8

                qmm0 = s_mm.v
                qsq0 = s_sq.v
                qev0 = s_ev.v

                def f_qmm(e, qh=qh, par=par, wtv=s_wqp[par].v, qsq0=qsq0, qev0=qev0):
                    e.wait_ge(s_wqp[par].h, wtv)
                    for hf in range(2):
                        tgt = qsq0 + hf - 1
                        if tgt > 0:
                            e.wait_ge(s_sq.h, tgt)
                        pspc = [ps3[:, 0:512], ps4[:, 0:512]][hf]
                        for c in range(NC):
                            ins = e.matmul(pspc, wqs(par, c),
                                           bufb[c][:, hf * 512:(hf + 1) * 512],
                                           start=(c == 0), stop=(c == NC - 1))
                        ins.then_inc(s_mm.h, 1)
                    for hf in range(2):
                        e.wait_ge(s_sq.h, qsq0 + hf + 1)
                        tgt = qev0 + hf
                        if tgt > 0:
                            e.wait_ge(s_ev.h, tgt)
                        e.matmul(ps1[:], swp[:], kraw[hf][:],
                                 start=True, stop=True).then_inc(s_mm.h, 1)
                P.emit("tensor", f_qmm)
                s_mm.v += 4
                qmm_end[qh] = s_mm.v

                def f_qev(e, qmm0=qmm0, qev0=qev0):
                    for hf in range(2):
                        pspc = [ps3[:, 0:512], ps4[:, 0:512]][hf]
                        e.wait_ge(s_mm.h, qmm0 + hf + 1)
                        tgt = qev0 + hf - 1
                        if tgt > 0:
                            e.wait_ge(s_ev.h, tgt)
                        e.copy(kraw[hf][:], pspc).then_inc(s_sq.h, 1)
                P.emit("scalar", f_qev)
                s_sq.v += 2

                def f_qro(e, qh=qh, qmm0=qmm0, tbv=s_tb.v):
                    e.wait_ge(s_tb.h, tbv)
                    for hf in range(2):
                        e.wait_ge(s_mm.h, qmm0 + 2 + hf + 1)
                        qt = bufa[qh][:, hf * 512:(hf + 1) * 512]
                        e.tensor_tensor(p1s[:], kraw[hf][:], t1b[:, hf * 512:(hf + 1) * 512], ALU.mult)
                        e.tensor_tensor(p2s[:], ps1[:], t2b[:, hf * 512:(hf + 1) * 512], ALU.mult)
                        e.tensor_tensor(qt[:], p1s[:], p2s[:], ALU.add).then_inc(s_ev.h, 1)
                P.emit("vector", f_qro)
                s_ev.v += 2

            bar()

        def kbuf(par):
            return wbig[:, par * 4096:(par + 1) * 4096]

        def vbuf(par):
            return wbig[:, 8192 + par * 4096: 8192 + (par + 1) * 4096]

        def p4_variant(q):
            own, others, j0s = _variant_tables(q)
            visits = [(l, l, True) for l in range(8)]
            visits += [(8 + t, j0, False) for t, j0 in enumerate(j0s) if j0 < 8]
            nvis = len(visits)
            lastA = max(vi for vi, (_, j0, _) in enumerate(visits) if j0 < 4)
            lastB = nvis - 1
            nmask_thru = []
            nm = 0
            for (_, _, dg) in visits:
                if dg:
                    nm += 1
                nmask_thru.append(nm)

            kvh_ct = {}
            for kvh in range(NKV):
                par = kvh % 2

                def f_ldkv(e, kvh=kvh, par=par, ctv=kvh_ct.get(kvh - 2, 0),
                           sk0=s_spkp[0].v, sk1=s_spkp[1].v):
                    e.wait_ge(s_spkp[0].h, sk0)
                    e.wait_ge(s_spkp[1].h, sk1)
                    if kvh >= 2 and ctv > 0:
                        e.wait_ge(s_ct.h, ctv)
                    e.dma_start(out=kbuf(par),
                                in_=k_scr[kvh * 128:(kvh + 1) * 128, :]).then_inc(s_kvp[par].h, 16)
                    for g in range(8):
                        src = v_scr.rearrange("(c p) o -> p c o", p=128)[
                            :, g * 4:(g + 1) * 4, kvh * 128:(kvh + 1) * 128]
                        dst = bass.AP(wbig[:].tensor, 8192 + par * 4096 + g * 512,
                                      [[WBC, 128], [128, 4], [1, 128]])
                        e.dma_start(out=dst, in_=src).then_inc(s_kvp[par].h, 16)
                P.emit("sync", f_ldkv)
                s_kvp[par].v += 16 * 9
                kv_now = s_kvp[par].v

                for j2 in range(4):
                    qh = kvh * 4 + j2
                    smm0 = s_mm.v
                    sct0 = s_ct.v
                    sex0 = s_ex.v
                    smk0 = s_mk.v
                    sq_pre = s_sq.v
                    sev_pre = s_ev.v

                    def f_mm(e, par=par, qh=qh, visits=visits, smm0=smm0,
                             sct0=sct0, sex0=sex0, smk0=smk0,
                             nmask_thru=nmask_thru, lastA=lastA, lastB=lastB,
                             kv_now=kv_now, sq_pre=sq_pre, sev_pre=sev_pre,
                             nvis=nvis, par2=par):
                        e.wait_ge(s_kvp[par2].h, kv_now)
                        if sq_pre > 0:
                            e.wait_ge(s_sq.h, sq_pre)
                        if sev_pre > 0:
                            e.wait_ge(s_ev.h, sev_pre)
                        started = {0: False, 512: False}

                        def ctx_of(vi):
                            lk_, j0_, dg_ = visits[vi]
                            e.wait_ge(s_ex.h, sex0 + vi + 1)
                            if dg_:
                                e.wait_ge(s_mk.h, smk0 + nmask_thru[vi])
                            vt = vbuf(par)[:, lk_ * 128:(lk_ + 1) * 128]
                            pt = psb[vi % 2]
                            c0 = j0_ * 128
                            ins = None
                            for cs in (0, 512):
                                cc = max(cs, c0)
                                if cc >= cs + 512:
                                    continue
                                bank = ps1 if cs == 0 else ps2
                                first = not started[cs]
                                lastv = lastA if cs == 0 else lastB
                                e.matmul(bank[:, cc - cs:512], vt, pt[:, cc:cs + 512],
                                         start=first, stop=(vi == lastv))
                                ins = e.matmul(psd[:1, cc:cs + 512], ones_bf[:],
                                               pt[:, cc:cs + 512],
                                               start=first, stop=(vi == lastv))
                                started[cs] = True
                            ins.then_inc(s_ct.h, 1)

                        for vi, (lk, j0, dg) in enumerate(visits):
                            if vi >= 2:
                                e.wait_ge(s_ex.h, sex0 + vi - 1)
                            sps = [ps3, ps4][vi % 2]
                            kt = kbuf(par)[:, lk * 128:(lk + 1) * 128]
                            c0 = j0 * 128
                            ins = None
                            for cs in (0, 512):
                                cc = max(cs, c0)
                                if cc >= cs + 512:
                                    continue
                                ins = e.matmul(sps[:, cc:cs + 512], kt,
                                               bufa[qh][:, cc:cs + 512],
                                               start=True, stop=True)
                            ins.then_inc(s_mm.h, 1)
                            if vi >= 1:
                                ctx_of(vi - 1)
                        ctx_of(nvis - 1)
                    P.emit("tensor", f_mm)
                    s_mm.v += nvis
                    s_ct.v += nvis

                    def f_exp(e, visits=visits, smm0=smm0, sct0=sct0, sex0=sex0):
                        for vi, (lk, j0, dg) in enumerate(visits):
                            e.wait_ge(s_mm.h, smm0 + vi + 1)
                            if vi >= 2 and sct0 + vi - 1 > 0:
                                e.wait_ge(s_ct.h, sct0 + vi - 1)
                            sps = [ps3, ps4][vi % 2]
                            c0 = j0 * 128
                            e.activation(psb[vi % 2][:, c0:OWN], sps[:, c0:OWN],
                                         AF.Exp, bias=b15[:], scale=SCALE).then_inc(s_ex.h, 1)
                    P.emit("scalar", f_exp)
                    s_ex.v += nvis

                    def f_msk(e, visits=visits, sex0=sex0):
                        for vi, (lk, j0, dg) in enumerate(visits):
                            if not dg:
                                continue
                            e.wait_ge(s_ex.h, sex0 + vi + 1)
                            c0 = j0 * 128
                            e.tensor_tensor(psb[vi % 2][:, c0:c0 + 128],
                                            psb[vi % 2][:, c0:c0 + 128],
                                            tri[:], ALU.mult).then_inc(s_mk.h, 1)
                    P.emit("vector", f_msk)
                    s_mk.v += nmask_thru[-1]

                    def f_den(e, ctv=s_ct.v):
                        e.wait_ge(s_ct.h, ctv)
                        e.copy(den_sb[:1, :], psd[:1, :]).then_inc(s_sq.h, 1)
                    P.emit("scalar", f_den)
                    s_sq.v += 1

                    def f_bc(e, sqv=s_sq.v):
                        e.wait_ge(s_sq.h, sqv)
                        e.matmul(ps3[:, 0:512], ones_f[:], den_sb[:1, 0:512],
                                 start=True, stop=True)
                        e.matmul(ps4[:, 0:512], ones_f[:], den_sb[:1, 512:1024],
                                 start=True, stop=True).then_inc(s_mm.h, 1)
                    P.emit("tensor", f_bc)
                    s_mm.v += 1

                    def f_ev(e, qh=qh, mmv=s_mm.v):
                        e.wait_ge(s_mm.h, mmv)
                        e.reciprocal(recf[0][:], ps3[:, 0:512])
                        e.reciprocal(recf[1][:], ps4[:, 0:512]).then_inc(s_ev.h, 1)
                        e.tensor_tensor(bufb[qh][:, 0:512], ps1[:], recf[0][:], ALU.mult)
                        e.tensor_tensor(bufb[qh][:, 512:1024], ps2[:], recf[1][:], ALU.mult).then_inc(s_ev.h, 2)
                    P.emit("vector", f_ev)
                    s_ev.v += 3

                kvh_ct[kvh] = s_ct.v

            vals = [(sem, sem.v) for sem in ALL_SEMS if sem is not s_ph]
            for e in ENGS:
                def f(eng, vals=vals):
                    drain_waits(eng, vals)
                    eng.nop().then_inc(s_ph.h, 1)
                P.emit(e, f)

        def p4():
            tgt = s_ph.v + 5
            for q in range(4):
                snap = [s.v for s in ALL_SEMS]
                P.if_open(q)
                p4_variant(q)
                P.if_close()
                for s, v0 in zip(ALL_SEMS, snap):
                    s.v = v0
            s_ph.v = tgt
            for e in ENGS:
                def f(eng, tgt=tgt):
                    eng.wait_ge(s_ph.h, tgt)
                P.emit(e, f)

        def wos(par, c):
            return wbig[:, 8192 + par * 4096 + c * 128: 8192 + par * 4096 + (c + 1) * 128]

        wt_end = {}

        def p_wo():
            wmm_end = {}
            for och in range(NH):
                par = och % 2

                def f_ld(e, och=och, par=par, mmv=wmm_end.get(och - 2, 0)):
                    if mmv > 0:
                        e.wait_ge(s_mm2.h, mmv)
                    for g in range(8):
                        src = wo.rearrange("(c p) o -> p c o", p=128)[
                            :, g * 4:(g + 1) * 4, och * 128:(och + 1) * 128]
                        dst = bass.AP(wbig[:].tensor, 8192 + par * 4096 + (g * 4) * 128,
                                      [[WBC, 128], [128, 4], [1, 128]])
                        e.dma_start(out=dst, in_=src).then_inc(s_wop[par].h, 16)
                    e.dma_start(out=bufa[och],
                                in_=xT[och * 128:(och + 1) * 128, 0:OWN]).then_inc(s_wop[par].h, 16)
                P.emit("gpsimd", f_ld)
                s_wop[par].v += 16 * 9

                mm0 = s_mm2.v

                def f_mm(e, och=och, par=par, wtv=s_wop[par].v, ev0=s_ev2.v):
                    e.wait_ge(s_wop[par].h, wtv)
                    for hf in range(2):
                        i = och * 2 + hf
                        tgt = ev0 + hf - 3
                        if tgt > 0:
                            e.wait_ge(s_ev2.h, tgt)
                        pspc = [ps1[:], ps2[:], ps3[:, 0:512], ps4[:, 0:512]][i % 4]
                        for c in range(NC):
                            ins = e.matmul(pspc, wos(par, c),
                                           bufb[c][:, hf * 512:(hf + 1) * 512],
                                           start=(c == 0), stop=(c == NC - 1))
                        ins.then_inc(s_mm2.h, 1)
                P.emit("tensor", f_mm)
                s_mm2.v += 2
                wmm_end[och] = s_mm2.v

                def f_ev(e, och=och, mm0=mm0):
                    for hf in range(2):
                        i = och * 2 + hf
                        pspc = [ps1[:], ps2[:], ps3[:, 0:512], ps4[:, 0:512]][i % 4]
                        e.wait_ge(s_mm2.h, mm0 + hf + 1)
                        sl = bufa[och][:, hf * 512:(hf + 1) * 512]
                        e.tensor_tensor(sl, sl, pspc, ALU.add).then_inc(s_ev2.h, 1)
                P.emit("vector", f_ev)
                s_ev2.v += 2

                if och == 1:
                    def f_ldw0(e, sw0=s_swp[0].v, sw1=s_swp[1].v):
                        if sw0 > 0:
                            e.wait_ge(s_swp[0].h, sw0)
                        if sw1 > 0:
                            e.wait_ge(s_swp[1].h, sw1)
                        for ic in range(NC):
                            dst = wbig[:, ic * 256:(ic + 1) * 256]
                            e.dma_start_transpose(dst,
                                                  w_scr[0:256, ic * 128:(ic + 1) * 128]).then_inc(s_ftp[0].h, 16)
                    P.emit("sync", f_ldw0)
                    s_ftp[0].v += 16 * NC
                    wt_end[0] = s_ftp[0].v

            bar()

        def p5():
            for hf in range(2):
                ln_stats(lambda c, hf=hf: bufa[c][:, hf * 512:(hf + 1) * 512],
                         s_ph, s_ph.v, s_mm2, s_sq2, s_st2)

                def f_h2(e, hf=hf, sqv=s_sq2.v):
                    e.wait_ge(s_sq2.h, sqv)
                    for c in range(NC):
                        sl_in = bufa[c][:, hf * 512:(hf + 1) * 512]
                        sl_out = bufb[c][:, hf * 512:(hf + 1) * 512]
                        e.tensor_tensor(s1f[:], sl_in, m_rep[:], ALU.subtract)
                        e.tensor_tensor(sl_out, s1f[:], r_rep[:], ALU.mult).then_inc(s_h2.h, 1)
                P.emit("vector", f_h2)
                s_h2.v += NC

            bar()

        def p7_pre():
            def f_ldw1(e, sw0=s_swp[0].v, sw1=s_swp[1].v):
                if sw0 > 0:
                    e.wait_ge(s_swp[0].h, sw0)
                if sw1 > 0:
                    e.wait_ge(s_swp[1].h, sw1)
                for ic in range(NC):
                    dst = wbig[:, 8192 + ic * 256: 8192 + (ic + 1) * 256]
                    e.dma_start_transpose(dst,
                                          w_scr[256:512, ic * 128:(ic + 1) * 128]).then_inc(s_ftp[1].h, 16)
            P.emit("sync", f_ldw1)
            s_ftp[1].v += 16 * NC
            wt_end[1] = s_ftp[1].v

        def p7():
            p7_ev0 = s_ev2.v
            ffn_end = {}
            for og in range(16):
                par = og % 2
                if og >= 2:
                    def f_ldw(e, og=og, par=par, mmv=ffn_end.get(og - 2, 0)):
                        if mmv > 0:
                            e.wait_ge(s_mm2.h, mmv)
                        for ic in range(NC):
                            dst = wbig[:, par * 8192 + ic * 256: par * 8192 + (ic + 1) * 256]
                            e.dma_start_transpose(dst,
                                                  w_scr[og * 256:(og + 1) * 256,
                                                        ic * 128:(ic + 1) * 128]).then_inc(s_ftp[og % 2].h, 16)
                    P.emit("sync", f_ldw)
                    s_ftp[og % 2].v += 16 * NC
                    wt_end[og] = s_ftp[og % 2].v

                for oc2 in range(2):
                    och = og * 2 + oc2
                    mm0 = s_mm2.v

                    def f_mm(e, par=par, oc2=oc2, och=och, wtv=wt_end[og], p7_ev0=p7_ev0):
                        e.wait_ge(s_ftp[par].h, wtv)
                        for hf in range(2):
                            i = och * 2 + hf
                            tgt = p7_ev0 + i - 1
                            if i >= 2:
                                e.wait_ge(s_ev2.h, tgt)
                            pspc = [ps1, ps2][hf]
                            for ic in range(NC):
                                lhs = wbig[:, par * 8192 + ic * 256 + oc2 * 128:
                                           par * 8192 + ic * 256 + oc2 * 128 + 128]
                                ins = e.matmul(pspc[:], lhs,
                                               bufb[ic][:, hf * 512:(hf + 1) * 512],
                                               start=(ic == 0), stop=(ic == NC - 1))
                            ins.then_inc(s_mm2.h, 1)
                    P.emit("tensor", f_mm)
                    s_mm2.v += 2
                    ffn_end[og] = s_mm2.v

                    def f_ep(e, och=och, mm0=mm0, outv=s_out.v):
                        for hf in range(2):
                            pspc = [ps1, ps2][hf]
                            e.wait_ge(s_mm2.h, mm0 + hf + 1)
                            i = och * 2 + hf
                            if i >= 2:
                                e.wait_ge(s_out.h, 16 * (i - 1))
                            e.scalar_tensor_tensor(outt[hf][:], pspc[:],
                                                   bias_sb[:, och:och + 1],
                                                   bufa[och][:, hf * 512:(hf + 1) * 512],
                                                   ALU.add, ALU.add).then_inc(s_ev2.h, 1)
                    P.emit("vector", f_ep)
                    s_ev2.v += 2

                    def f_st(e, och=och, evv=s_ev2.v):
                        for hf in range(2):
                            e.wait_ge(s_ev2.h, evv - 1 + hf)
                            e.dma_start(out=out[och * 128:(och + 1) * 128,
                                                hf * 512:(hf + 1) * 512],
                                        in_=outt[hf][:]).then_inc(s_out.h, 16)
                    P.emit("sync", f_st)
                    s_out.v += 32

            def f_fin(e, ov=s_out.v):
                e.wait_ge(s_out.h, ov)
            P.emit("sync", f_fin)

        def dump_scr():
            if not DEBUG:
                return
            def f(e, xv=s_x2.v):
                e.dma_start(out=dbg["d_kscr"], in_=k_scr).then_inc(s_x2.h, 16)
                e.dma_start(out=dbg["d_vscr"], in_=v_scr).then_inc(s_x2.h, 16)
                e.dma_start(out=dbg["d_hscr"], in_=h_scr).then_inc(s_x2.h, 16)
                e.dma_start(out=dbg["d_wscr"], in_=w_scr).then_inc(s_x2.h, 16)
                e.wait_ge(s_x2.h, xv + 64)
            P.emit("gpsimd", f)
            s_x2.v += 64
            bar()

        def dump_sbuf(key, tiles):
            if not DEBUG:
                return
            def f(e, key=key, tiles=tiles, base=s_x2.v):
                for c in range(NC):
                    e.dma_start(out=dbg[key][c * 128:(c + 1) * 128, :],
                                in_=tiles[c]).then_inc(s_x2.h, 16)
                e.wait_ge(s_x2.h, base + 16 * NC)
            P.emit("gpsimd", f)
            s_x2.v += 16 * NC
            bar()

        NPH = int(os.environ.get("KPHASES", "10"))
        phases = [lambda: pass_a(0), lambda: pass_a(1), lambda: pass_b(0),
                  lambda: pass_b(1), p3, p4, p_wo, p7_pre, p5, p7]
        names = ["A1", "A2", "B1", "B2", "P3", "P4", "WO", "PRE", "P5", "P7"]
        for i in range(min(NPH, 10)):
            phases[i]()
            if DEBUG:
                if names[i] == "P3":
                    dump_sbuf("d_q", bufa)
                elif names[i] == "P4":
                    dump_sbuf("d_ctx", bufb)
                elif names[i] == "WO":
                    dump_sbuf("d_x2", bufa)
                elif names[i] == "P5":
                    dump_sbuf("d_h2", bufb)
        if DEBUG:
            dump_scr()
        if NPH < 10:
            def f_dummy(e):
                e.memset(outt[0][:], 0.0).then_inc(s_st2.h, 1)
            P.emit("vector", f_dummy)
            s_st2.v += 1

            def f_fin0(e, stv=s_st2.v, ov=s_out.v):
                e.wait_ge(s_st2.h, stv)
                e.dma_start(out=out[0:128, 0:512], in_=outt[0][:]).then_inc(s_out.h, 16)
                e.wait_ge(s_out.h, ov + 16)
            P.emit("sync", f_fin0)
            s_out.v += 16

        block = es.enter_context(nc.Block())
        block.sync(lambda e: P.replay("sync", e))
        block.gpsimd(lambda e: P.replay("gpsimd", e))
        block.vector(lambda e: P.replay("vector", e))
        block.scalar(lambda e: P.replay("scalar", e))
        block.tensor(lambda e: P.replay("tensor", e))

    return nc


def _perm_for(q):
    A, B = q, 7 - q
    own = list(range(A * 512, (A + 1) * 512)) + list(range(B * 512, (B + 1) * 512))
    own_set = set(own)
    others = [t for t in range(T) if t not in own_set]
    return np.array(own + others, dtype=np.int64)


def _rope_tables(perm):
    inv = 1.0 / (500000.0 ** (np.arange(0, HD, 2, dtype=np.float64) / HD))
    t = np.arange(T, dtype=np.float64)
    fr = np.outer(t, inv)
    cos = np.cos(fr).astype(np.float32)[perm]
    sin = np.sin(fr).astype(np.float32)[perm]
    t1 = np.concatenate([cos.T, cos.T], axis=0).astype(ml_dtypes.bfloat16)
    t2 = np.concatenate([-sin.T, sin.T], axis=0).astype(ml_dtypes.bfloat16)
    return np.ascontiguousarray(t1), np.ascontiguousarray(t2)


_NC_CACHE = {}


def kernel(L0_attn_x, L0_attn_Wq, L0_attn_Wk, L0_attn_Wv, L0_attn_Wo,
           L0_ffn_weight_packed, L0_ffn_weight_scale, L0_ffn_weight_var_mask,
           L0_ffn_bias):
    x = np.asarray(L0_attn_x, dtype=np.float32)
    wq = np.ascontiguousarray(np.asarray(L0_attn_Wq, dtype=np.float32))
    wk = np.ascontiguousarray(np.asarray(L0_attn_Wk, dtype=np.float32))
    wv = np.ascontiguousarray(np.asarray(L0_attn_Wv, dtype=np.float32))
    wo = np.ascontiguousarray(np.asarray(L0_attn_Wo, dtype=np.float32))
    pk_a = np.ascontiguousarray(np.asarray(L0_ffn_weight_packed, dtype=np.int32).reshape(4096, 2048))
    sc_a = np.ascontiguousarray(np.asarray(L0_ffn_weight_scale).reshape(4096, 128))
    mk_a = np.ascontiguousarray(np.asarray(L0_ffn_weight_var_mask).reshape(4096, 128).view(np.uint8))
    bias_pp = np.ascontiguousarray(np.asarray(L0_ffn_bias, dtype=np.float32).reshape(32, 128).T)

    tri = np.zeros((128, 128), dtype=ml_dtypes.bfloat16)
    for kk in range(128):
        tri[kk, kk:] = 1.0
    swpm = np.zeros((128, 128), dtype=ml_dtypes.bfloat16)
    for m in range(128):
        swpm[(m + 64) % 128, m] = 1.0

    if "nc" not in _NC_CACHE:
        _NC_CACHE["nc"] = build_nc()
    nc = _NC_CACHE["nc"]

    in_maps = []
    perms = []
    for c in range(8):
        b, q = c // 4, c % 4
        perm = _perm_for(q)
        perms.append(perm)
        xTc = np.ascontiguousarray(x[b][perm].T)
        t1, t2 = _rope_tables(perm)
        in_maps.append(dict(
            xT=xTc, wq=wq, wk=wk, wv=wv, wo=wo, pk=pk_a, sc=sc_a, mk=mk_a,
            bias_in=bias_pp, t1_in=t1, t2_in=t2, tri_in=tri, swp_in=swpm,
        ))

    res = run_bass_kernel_spmd(nc, in_maps, core_ids=list(range(8)))

    outa = np.empty((2, T, D), dtype=np.float32)
    for c in range(8):
        b = c // 4
        o = np.asarray(res.results[c]["o"])
        outa[b][perms[c][:OWN]] = o.T
    if DEBUG:
        kernel.last_res = res
        kernel.last_perms = perms
    return outa


# version 12
# speedup vs baseline: 1.1664x; 1.0890x over previous
import os
import math
import contextlib
import numpy as np
import ml_dtypes

import concourse.bass as bass
import concourse.mybir as mybir
from concourse.bass_utils import run_bass_kernel_spmd

F32 = mybir.dt.float32
BF16 = mybir.dt.bfloat16
I32 = mybir.dt.int32
F16 = mybir.dt.float16
U8 = mybir.dt.uint8
AF = mybir.ActivationFunctionType
ALU = mybir.AluOpType

D = 4096
T = 4096
NH = 32
NKV = 8
HD = 128
NC = 32
OWN = 1024
NBLK = 8
SCALE = 1.0 / math.sqrt(HD)
EXP_BIAS = -15.0
EPS = 1e-5

ENGS = ("sync", "gpsimd", "vector", "scalar", "tensor")

DEBUG = False


def _variant_tables(q):
    A, B = q, 7 - q
    own = list(range(4 * A, 4 * A + 4)) + list(range(4 * B, 4 * B + 4))
    others = [g for g in range(32) if g not in own]
    j0 = [sum(1 for o in own if o < g) for g in others]
    return own, others, j0


class Sem:
    def __init__(self, es, nc, name):
        self.h = es.enter_context(nc.semaphore(name))
        self.v = 0


class Prog:
    def __init__(self):
        self.ops = {e: [] for e in ENGS}

    def emit(self, eng, fn):
        self.ops[eng].append(("op", fn))

    def if_open(self, v):
        for e in ENGS:
            self.ops[e].append(("if", v))

    def if_close(self):
        for e in ENGS:
            self.ops[e].append(("endif", None))

    def replay(self, eng_name, eng):
        need_if = any(k == "if" for k, _ in self.ops[eng_name])
        vmod = None
        if need_if:
            pid = eng.to_reg(eng.partition_id())
            vmod = eng.alloc_register(f"vm_{eng_name}")
            eng.reg_mod(vmod, pid, 4)
        guards = []
        for kind, payload in self.ops[eng_name]:
            if kind == "op":
                payload(eng)
            elif kind == "if":
                g = eng.If_eq(vmod, payload)
                g.__enter__()
                guards.append(g)
            else:
                guards.pop().__exit__(None, None, None)
        eng.end_ifs()


def build_nc():
    nc = bass.Bass("TRN2", target_bir_lowering=False, debug=False)

    xT = nc.dram_tensor("xT", [D, T], F32, kind="ExternalInput").ap()
    wq = nc.dram_tensor("wq", [D, D], F32, kind="ExternalInput").ap()
    wk = nc.dram_tensor("wk", [D, 1024], F32, kind="ExternalInput").ap()
    wv = nc.dram_tensor("wv", [D, 1024], F32, kind="ExternalInput").ap()
    wo = nc.dram_tensor("wo", [D, D], F32, kind="ExternalInput").ap()
    pk = nc.dram_tensor("pk", [4096, 2048], I32, kind="ExternalInput").ap()
    sc = nc.dram_tensor("sc", [4096, 128], F16, kind="ExternalInput").ap()
    mk = nc.dram_tensor("mk", [4096, 128], U8, kind="ExternalInput").ap()
    bias_in = nc.dram_tensor("bias_in", [128, 32], F32, kind="ExternalInput").ap()
    t1_in = nc.dram_tensor("t1_in", [128, T], BF16, kind="ExternalInput").ap()
    t2_in = nc.dram_tensor("t2_in", [128, T], BF16, kind="ExternalInput").ap()
    tri_in = nc.dram_tensor("tri_in", [128, 128], BF16, kind="ExternalInput").ap()
    swp_in = nc.dram_tensor("swp_in", [128, 128], BF16, kind="ExternalInput").ap()

    out = nc.dram_tensor("o", [D, OWN], F32, kind="ExternalOutput").ap()

    k_scr = nc.dram_tensor("k_scr", [1024, T], BF16, kind="Internal").ap()
    v_scr = nc.dram_tensor("v_scr", [T, 1024], BF16, kind="Internal").ap()
    h_scr = nc.dram_tensor("h_scr", [D, T], BF16, kind="Internal").ap()
    w_scr = nc.dram_tensor("w_scr", [4096, 4096], BF16, kind="Internal").ap()

    dbg = {}
    if DEBUG:
        for nm, shp in [("d_kscr", [1024, T]), ("d_vscr", [T, 1024]),
                        ("d_hscr", [D, T]), ("d_wscr", [4096, 4096]),
                        ("d_q", [D, OWN]), ("d_ctx", [D, OWN]),
                        ("d_x2", [D, OWN]), ("d_h2", [D, OWN])]:
            dbg[nm] = nc.dram_tensor(nm, shp, BF16, kind="ExternalOutput").ap()

    es = contextlib.ExitStack()
    with es:
        def sbt(name, shape, dtype):
            return es.enter_context(nc.sbuf_tensor(name, shape, dtype))

        bufa_t = sbt("bufa", [128, NC * OWN], BF16)
        bufb_t = sbt("bufb", [128, NC * OWN], BF16)
        bufa = [bufa_t[:, i * OWN:(i + 1) * OWN] for i in range(NC)]
        bufb = [bufb_t[:, i * OWN:(i + 1) * OWN] for i in range(NC)]
        WBC = 16384
        wbig = sbt("wbig", [128, WBC], BF16)

        t1b = sbt("t1b", [128, OWN], BF16)
        t2b = sbt("t2b", [128, OWN], BF16)
        kout = [sbt(f"kout{i}", [128, 512], BF16) for i in range(2)]
        s1f = sbt("s1f", [128, 512], F32)
        x2t = [sbt(f"x2t{i}", [128, 512], BF16) for i in range(2)]
        m_rep = sbt("m_rep", [128, 512], F32)
        r_rep = sbt("r_rep", [128, 512], F32)
        mrow = sbt("mrow", [1, 512], F32)
        rrow = sbt("rrow", [1, 512], F32)
        p1s = sbt("p1s", [128, 512], BF16)
        p2s = sbt("p2s", [128, 512], BF16)
        psb = [sbt(f"psb{i}", [128, OWN], BF16) for i in range(2)]
        recf = [sbt(f"recf{i}", [128, 512], F32) for i in range(2)]
        outt = [sbt(f"outt{i}", [128, 512], F32) for i in range(2)]
        pkt = [sbt(f"pkt{i}", [128, 256], I32) for i in range(2)]
        qi32 = sbt("qi32", [128, 512], I32)
        wdt = [sbt(f"wdt{i}", [128, 512], BF16) for i in range(2)]
        sct = sbt("sct", [128, 128], F16)
        mkt = sbt("mkt", [128, 128], U8)
        seff = sbt("seff", [128, 128], F32)
        tri = sbt("tri_sb", [128, 128], BF16)
        swp = sbt("swp_sb", [128, 128], BF16)
        kraw = [sbt(f"kraw{i}", [128, 512], BF16) for i in range(2)]
        ones_bf = sbt("ones_bf", [128, 1], BF16)
        ones_f = sbt("ones_f", [1, 128], F32)
        b15 = sbt("b15", [128, 1], F32)
        beps = sbt("beps", [1, 1], F32)
        bias_sb = sbt("bias_sb", [128, 32], F32)
        den_sb = sbt("den_sb", [1, OWN], F32)

        ps1 = es.enter_context(nc.psum_tensor("ps1", [128, 512], F32))
        ps2 = es.enter_context(nc.psum_tensor("ps2", [128, 512], F32))
        ps3 = es.enter_context(nc.psum_tensor("ps3", [128, 1024], F32))
        ps4 = es.enter_context(nc.psum_tensor("ps4", [128, 1024], F32))
        psd = es.enter_context(nc.psum_tensor("psd", [1, 1024], F32))

        s_x = Sem(es, nc, "s_x")
        s_wt = Sem(es, nc, "s_wt")
        s_tb = Sem(es, nc, "s_tb")
        s_mm = Sem(es, nc, "s_mm")
        s_ct = Sem(es, nc, "s_ct")
        s_sq = Sem(es, nc, "s_sq")
        s_st = Sem(es, nc, "s_st")
        s_h1 = Sem(es, nc, "s_h1")
        s_ev = Sem(es, nc, "s_ev")
        s_ex = Sem(es, nc, "s_ex")
        s_mk = Sem(es, nc, "s_mk")
        s_spk = Sem(es, nc, "s_spk")
        s_sph = Sem(es, nc, "s_sph")
        s_kv = Sem(es, nc, "s_kv")
        s_dq = Sem(es, nc, "s_dq")
        s_out = Sem(es, nc, "s_out")
        s_ph = Sem(es, nc, "s_ph")
        s_wt2 = Sem(es, nc, "s_wt2")
        s_mm2 = Sem(es, nc, "s_mm2")
        s_ev2 = Sem(es, nc, "s_ev2")
        s_x2 = Sem(es, nc, "s_x2")
        s_sq2 = Sem(es, nc, "s_sq2")
        s_st2 = Sem(es, nc, "s_st2")
        s_h2 = Sem(es, nc, "s_h2")
        s_pk = Sem(es, nc, "s_pk")
        s_sw = Sem(es, nc, "s_sw")
        s_swp = [Sem(es, nc, f"s_swp{i}") for i in range(2)]
        s_spkp = [Sem(es, nc, f"s_spkp{i}") for i in range(2)]
        s_xp = [Sem(es, nc, f"s_xp{i}") for i in range(2)]
        s_sphp = [Sem(es, nc, f"s_sphp{i}") for i in range(2)]
        s_wqp = [Sem(es, nc, f"s_wqp{i}") for i in range(2)]
        s_wop = [Sem(es, nc, f"s_wop{i}") for i in range(2)]
        s_ftp = [Sem(es, nc, f"s_ftp{i}") for i in range(2)]
        s_kvp = [Sem(es, nc, f"s_kvp{i}") for i in range(2)]
        s_tbp = [Sem(es, nc, f"s_tbp{i}") for i in range(2)]
        s_pkp = [Sem(es, nc, f"s_pkp{i}") for i in range(2)]

        ALL_SEMS = [s_x, s_wt, s_tb, s_mm, s_ct, s_sq, s_st, s_h1, s_ev,
                    s_ex, s_mk, s_spk, s_sph, s_kv, s_dq, s_out, s_ph,
                    s_wt2, s_mm2, s_ev2, s_x2, s_sq2, s_st2, s_h2, s_pk, s_sw,
                    *s_swp, *s_spkp, *s_xp, *s_sphp, *s_wqp, *s_wop, *s_ftp, *s_kvp, *s_tbp, *s_pkp]

        P = Prog()

        def drain_waits(eng, vals):
            for sem, v in vals:
                if v > 0:
                    eng.wait_ge(sem.h, v)

        def bar():
            tgt = s_ph.v + 5
            vals = [(sem, sem.v) for sem in ALL_SEMS if sem is not s_ph]
            for e in ENGS:
                def f(eng, tgt=tgt, vals=vals):
                    drain_waits(eng, vals)
                    eng.nop().then_inc(s_ph.h, 1)
                    eng.wait_ge(s_ph.h, tgt)
                P.emit(e, f)
            s_ph.v = tgt

        def xb(p, c):
            return bufa_t[:, c * OWN + p * 512: c * OWN + (p + 1) * 512]

        def xgrp_ap(p, g):
            return bass.AP(bufa_t[:].tensor, (g * 4) * OWN + p * 512,
                           [[NC * OWN, 128], [OWN, 4], [1, 512]])

        def f(e):
            e.dma_start(out=tri[:], in_=tri_in).then_inc(s_x.h, 16)
            e.dma_start(out=swp[:], in_=swp_in).then_inc(s_x.h, 16)
            e.dma_start(out=bias_sb[:], in_=bias_in).then_inc(s_x.h, 16)
            e.wait_ge(s_x.h, 48)
        P.emit("sync", f)
        s_x.v += 48

        def f(e):
            e.memset(ones_bf[:], 1.0)
            e.memset(ones_f[:], 1.0)
            e.memset(b15[:], EXP_BIAS)
            e.memset(beps[:], EPS).then_inc(s_st.h, 1)
        P.emit("vector", f)
        s_st.v += 1
        bar()

        def ln_stats(tiles, xsem, xval, smm, ssq, sst):
            sq0 = ssq.v
            mm0 = smm.v

            def f_sq(e, sq0=sq0, mm0=mm0):
                e.wait_ge(xsem.h, xval)
                for c in range(NC):
                    if c >= 2:
                        e.wait_ge(smm.h, mm0 + c - 1)
                    e.activation(x2t[c % 2][:], tiles(c), AF.Square).then_inc(ssq.h, 1)
            P.emit("scalar", f_sq)
            ssq.v += NC

            def f_mm(e, sq0=sq0, stv0=sst.v):
                e.wait_ge(xsem.h, xval)
                if stv0 > 0:
                    e.wait_ge(sst.h, stv0)
                for c in range(NC):
                    e.matmul(psd[:1, 0:512], ones_bf[:], tiles(c),
                             start=(c == 0), stop=(c == NC - 1))
                    e.wait_ge(ssq.h, sq0 + c + 1)
                    e.matmul(psd[:1, 512:1024], ones_bf[:], x2t[c % 2][:],
                             start=(c == 0), stop=(c == NC - 1)).then_inc(smm.h, 1)
            P.emit("tensor", f_mm)
            smm.v += NC

            def f_vs(e, mmv=smm.v):
                e.wait_ge(smm.h, mmv)
                e.tensor_scalar(mrow[:], psd[:1, 0:512], 1.0 / D, None, ALU.mult)
                e.tensor_scalar(rrow[:], psd[:1, 512:1024], 1.0 / D, None, ALU.mult)
                e.tensor_tensor(s1f[:1, :], mrow[:], mrow[:], ALU.mult)
                e.tensor_tensor(rrow[:], rrow[:], s1f[:1, :], ALU.subtract).then_inc(sst.h, 1)
            P.emit("vector", f_vs)
            sst.v += 1

            def f_sqrt(e, stv=sst.v):
                e.wait_ge(sst.h, stv)
                e.activation(rrow[:], rrow[:], AF.Sqrt, bias=beps[:]).then_inc(ssq.h, 1)
            P.emit("scalar", f_sqrt)
            ssq.v += 1

            def f_rec(e, sqv=ssq.v):
                e.wait_ge(ssq.h, sqv)
                e.reciprocal(rrow[:], rrow[:]).then_inc(sst.h, 1)
            P.emit("vector", f_rec)
            sst.v += 1

            def f_bc(e, stv=sst.v, evv=s_ev.v):
                e.wait_ge(sst.h, stv)
                if evv > 0:
                    e.wait_ge(s_ev.h, evv)
                e.matmul(ps1[:], ones_f[:], mrow[:], start=True, stop=True)
                e.matmul(ps2[:], ones_f[:], rrow[:], start=True, stop=True).then_inc(smm.h, 1)
            P.emit("tensor", f_bc)
            smm.v += 1

            def f_cp(e, mmv=smm.v):
                e.wait_ge(smm.h, mmv)
                e.copy(m_rep[:], ps1[:])
                e.copy(r_rep[:], ps2[:]).then_inc(ssq.h, 1)
            P.emit("scalar", f_cp)
            ssq.v += 1

        def wks(s, c):
            return wbig[:, s * 4096 + c * 128: s * 4096 + (c + 1) * 128]

        def wvs(c):
            return wbig[:, c * 512:(c + 1) * 512]

        def wqs(par, c):
            return wbig[:, par * 4096 + c * 128: par * 4096 + (c + 1) * 128]

        NSUB = 8

        def p6_och(och):
            def f_lds(e, och=och, stv=s_st2.v):
                if stv > 0:
                    e.wait_ge(s_st2.h, stv)
                e.dma_start(out=sct[:], in_=sc[och * 128:(och + 1) * 128, :]).then_inc(s_x2.h, 16)
                e.dma_start(out=mkt[:], in_=mk[och * 128:(och + 1) * 128, :]).then_inc(s_x2.h, 16)
            P.emit("sync", f_lds)
            s_x2.v += 32

            def f_se(e, xv=s_x2.v, dqv=s_dq.v):
                e.wait_ge(s_x2.h, xv)
                if dqv > 0:
                    e.wait_ge(s_dq.h, dqv)
                e.tensor_scalar(s1f[:, 0:128], mkt[:], 0.5, 0.5, ALU.mult, ALU.add)
                e.tensor_tensor(seff[:], sct[:], s1f[:, 0:128], ALU.mult).then_inc(s_st2.h, 1)
            P.emit("vector", f_se)
            s_st2.v += 1
            se_done = s_st2.v

            for sub in range(NSUB):
                par = sub % 2
                i = och * NSUB + sub

                def f_ldp(e, och=och, sub=sub, par=par, i=i, dqv=s_dq.v):
                    if i >= 2:
                        e.wait_ge(s_dq.h, dqv - 1)
                    e.dma_start(out=pkt[par][:],
                                in_=pk[och * 128:(och + 1) * 128,
                                       sub * 256:(sub + 1) * 256]).then_inc(s_pkp[par].h, 16)
                P.emit("sync", f_ldp)
                s_pkp[par].v += 16

                def f_nib(e, par=par, xv=s_pkp[par].v, dqv=s_dq.v):
                    e.wait_ge(s_pkp[par].h, xv)
                    if dqv >= 2:
                        e.wait_ge(s_dq.h, dqv)
                    e.tensor_scalar(qi32[:, 0:512:2], pkt[par][:], 15, None, ALU.bitwise_and)
                    e.tensor_scalar(qi32[:, 1:512:2], pkt[par][:], 4, None,
                                    ALU.logical_shift_right).then_inc(s_dq.h, 1)
                P.emit("vector", f_nib)
                s_dq.v += 1

                def f_stt(e, sub=sub, par=par, i=i, dqv=s_dq.v,
                          se_done=se_done, spv=s_swp[par].v):
                    e.wait_ge(s_dq.h, dqv)
                    e.wait_ge(s_st2.h, se_done)
                    if spv > 0:
                        e.wait_ge(s_swp[par].h, spv)
                    sl = bass.AP(seff[:].tensor, seff[:].offset + sub * 16,
                                 [[128, 128], [1, 16], [0, 32]])
                    qv = qi32[:].rearrange("p (g i) -> p g i", i=32)
                    wv_ = wdt[par][:].rearrange("p (g i) -> p g i", i=32)
                    e.scalar_tensor_tensor(wv_, qv, 8.0, sl,
                                           ALU.subtract, ALU.mult).then_inc(s_dq.h, 1)
                P.emit("vector", f_stt)
                s_dq.v += 1

                def f_st(e, och=och, sub=sub, par=par, dqv=s_dq.v):
                    e.wait_ge(s_dq.h, dqv)
                    e.dma_start(out=w_scr[och * 128:(och + 1) * 128,
                                          sub * 512:(sub + 1) * 512],
                                in_=wdt[par][:]).then_inc(s_swp[par].h, 16)
                P.emit("gpsimd", f_st)
                s_swp[par].v += 16

        def pass_a(half):
            wt0 = s_wt.v

            def f_ldwk(e, half=half):
                for c in range(NC):
                    src = wk[c * 128:(c + 1) * 128,
                             half * 512:(half + 1) * 512].rearrange("p (s o) -> p s o", s=4)
                    dst = bass.AP(wbig[:].tensor, c * 128,
                                  [[WBC, 128], [4096, 4], [1, 128]])
                    e.dma_start(out=dst, in_=src).then_inc(s_wt.h, 16)
            P.emit("gpsimd", f_ldwk)
            s_wt.v += 16 * NC

            sph_end = {}
            amm_end = {}
            for blk in range(NBLK):
                par = blk % 2

                if half == 0:
                    def f_ldx(e, blk=blk, par=par, spv=sph_end.get(blk - 2, 0)):
                        if spv > 0:
                            e.wait_ge(s_sphp[par].h, spv)
                        for g in range(8):
                            src = xT.rearrange("(c p) t -> p c t", p=128)[
                                :, g * 4:(g + 1) * 4, blk * 512:(blk + 1) * 512]
                            e.dma_start(out=xgrp_ap(par, g), in_=src).then_inc(s_xp[par].h, 16)
                    P.emit("gpsimd", f_ldx)
                    s_xp[par].v += 16 * 8

                    ln_stats(lambda c, par=par: xb(par, c), s_xp[par], s_xp[par].v,
                             s_mm, s_sq, s_st)

                    def f_h1(e, par=par, sqv=s_sq.v):
                        e.wait_ge(s_sq.h, sqv)
                        for c in range(NC):
                            e.tensor_tensor(s1f[:], xb(par, c), m_rep[:], ALU.subtract)
                            e.tensor_tensor(xb(par, c), s1f[:], r_rep[:], ALU.mult).then_inc(s_h1.h, 1)
                    P.emit("vector", f_h1)
                    s_h1.v += NC

                    if blk < 2:
                        def f_own(e, par=par, blk=blk, h1v=s_h1.v):
                            for c in range(NC):
                                e.wait_ge(s_h1.h, h1v - NC + c + 1)
                                ins = e.copy(bufb[c][:, blk * 512:(blk + 1) * 512], xb(par, c))
                            ins.then_inc(s_sq.h, 1)
                        P.emit("scalar", f_own)
                        s_sq.v += 1
                else:
                    def f_ldh(e, blk=blk, par=par, mmv=amm_end.get(blk - 2, 0),
                              sp0=s_sphp[0].v, sp1=s_sphp[1].v):
                        if mmv > 0:
                            e.wait_ge(s_mm.h, mmv)
                        if sp0 > 0:
                            e.wait_ge(s_sphp[0].h, sp0)
                        if sp1 > 0:
                            e.wait_ge(s_sphp[1].h, sp1)
                        for g in range(8):
                            src = h_scr.rearrange("(c p) t -> p c t", p=128)[
                                :, g * 4:(g + 1) * 4, blk * 512:(blk + 1) * 512]
                            e.dma_start(out=xgrp_ap(par, g), in_=src).then_inc(s_xp[par].h, 16)
                    P.emit("sync", f_ldh)
                    s_xp[par].v += 16 * 8

                def f_ldt(e, blk=blk, par=par, evv=max(0, s_ev.v - 4)):
                    if evv > 0:
                        e.wait_ge(s_ev.h, evv)
                    e.dma_start(out=t1b[:, par * 512:(par + 1) * 512],
                                in_=t1_in[:, blk * 512:(blk + 1) * 512]).then_inc(s_tbp[par].h, 16)
                    e.dma_start(out=t2b[:, par * 512:(par + 1) * 512],
                                in_=t2_in[:, blk * 512:(blk + 1) * 512]).then_inc(s_tbp[par].h, 16)
                P.emit("sync", f_ldt)
                s_tbp[par].v += 32

                kmm0 = s_mm.v
                h1v = s_h1.v
                xv = s_x.v
                kev_base = s_ev.v
                ksq0 = s_sq.v

                def f_kmm(e, par=par, wt0=wt0, h1v=h1v, xv=s_xp[par].v,
                          ksq0=ksq0, half=half, kev_base=kev_base):
                    e.wait_ge(s_wt.h, wt0 + 16 * NC)
                    if half == 0:
                        e.wait_ge(s_h1.h, h1v)
                    else:
                        e.wait_ge(s_xp[par].h, xv)
                    for s in range(4):
                        tgt = ksq0 + s - 1
                        if tgt > 0:
                            e.wait_ge(s_sq.h, tgt)
                        pspc = [ps3[:, 0:512], ps4[:, 0:512]][s % 2]
                        for c in range(NC):
                            ins = e.matmul(pspc, wks(s, c), xb(par, c),
                                           start=(c == 0), stop=(c == NC - 1))
                        ins.then_inc(s_mm.h, 1)
                    for s in range(4):
                        e.wait_ge(s_sq.h, ksq0 + s + 1)
                        if kev_base + s > 0:
                            e.wait_ge(s_ev.h, kev_base + s)
                        e.matmul(ps1[:], swp[:], kraw[s % 2][:],
                                 start=True, stop=True).then_inc(s_mm.h, 1)
                P.emit("tensor", f_kmm)
                s_mm.v += 8
                amm_end[blk] = s_mm.v

                def f_kev(e, kmm0=kmm0, ksq0=ksq0, kev_base=kev_base):
                    for s in range(4):
                        pspc = [ps3[:, 0:512], ps4[:, 0:512]][s % 2]
                        e.wait_ge(s_mm.h, kmm0 + s + 1)
                        if kev_base + s - 1 > 0:
                            e.wait_ge(s_ev.h, kev_base + s - 1)
                        e.copy(kraw[s % 2][:], pspc).then_inc(s_sq.h, 1)
                P.emit("scalar", f_kev)
                s_sq.v += 4

                kev0 = s_ev.v

                def f_kro(e, par=par, kmm0=kmm0, tbv=s_tbp[par].v,
                          sp0=s_spkp[0].v, sp1=s_spkp[1].v):
                    e.wait_ge(s_tbp[par].h, tbv)
                    for s in range(4):
                        e.wait_ge(s_mm.h, kmm0 + 4 + s + 1)
                        tgt = [sp0, sp1][s % 2] + (16 if s >= 2 else 0)
                        if tgt > 0:
                            e.wait_ge(s_spkp[s % 2].h, tgt)
                        ko = kout[s % 2]
                        e.tensor_tensor(p1s[:], kraw[s % 2][:], t1b[:, par * 512:(par + 1) * 512], ALU.mult)
                        e.tensor_tensor(p2s[:], ps1[:], t2b[:, par * 512:(par + 1) * 512], ALU.mult)
                        e.tensor_tensor(ko[:], p1s[:], p2s[:], ALU.add).then_inc(s_ev.h, 1)
                P.emit("vector", f_kro)
                s_ev.v += 4

                def f_kst(e, kev0=kev0, blk=blk, half=half):
                    for s in range(4):
                        kvh = half * 4 + s
                        e.wait_ge(s_ev.h, kev0 + s + 1)
                        e.dma_start(out=k_scr[kvh * 128:(kvh + 1) * 128,
                                              blk * 512:(blk + 1) * 512],
                                    in_=kout[s % 2][:]).then_inc(s_spkp[s % 2].h, 16)
                P.emit("sync", f_kst)
                s_spkp[0].v += 16 * 2
                s_spkp[1].v += 16 * 2

                if half == 1:
                    for _j in range(4):
                        p6_och(blk * 4 + _j)

                if half == 0:
                    def f_sp(e, par=par, blk=blk, mmv=s_mm.v):
                        e.wait_ge(s_mm.h, mmv)
                        for g in range(8):
                            dst = h_scr.rearrange("(c p) t -> p c t", p=128)[
                                :, g * 4:(g + 1) * 4, blk * 512:(blk + 1) * 512]
                            e.dma_start(out=dst, in_=xgrp_ap(par, g)).then_inc(s_sphp[par].h, 16)
                    P.emit("gpsimd", f_sp)
                    s_sphp[par].v += 16 * 8
                    sph_end[blk] = s_sphp[par].v

            bar()

        def pass_b(vh):
            def f_ldwv(e, vh=vh):
                for c in range(NC):
                    e.dma_start(out=wvs(c),
                                in_=wv[c * 128:(c + 1) * 128,
                                       vh * 512:(vh + 1) * 512]).then_inc(s_wt.h, 16)
            P.emit("gpsimd", f_ldwv)
            wvdone = s_wt.v + 16 * NC
            s_wt.v += 16 * NC

            bmm_end = {}
            for blk in range(NBLK):
                par = blk % 2

                def f_ldh(e, blk=blk, par=par, mmv=bmm_end.get(blk - 2, 0),
                          sp0=s_sphp[0].v, sp1=s_sphp[1].v):
                    if mmv > 0:
                        e.wait_ge(s_mm.h, mmv)
                    if sp0 > 0:
                        e.wait_ge(s_sphp[0].h, sp0)
                    if sp1 > 0:
                        e.wait_ge(s_sphp[1].h, sp1)
                    for g in range(8):
                        src = h_scr.rearrange("(c p) t -> p c t", p=128)[
                            :, g * 4:(g + 1) * 4, blk * 512:(blk + 1) * 512]
                        e.dma_start(out=xgrp_ap(par, g), in_=src).then_inc(s_xp[par].h, 16)
                P.emit("sync", f_ldh)
                s_xp[par].v += 16 * 8

                vmm0 = s_mm.v

                def f_vmm(e, par=par, xv=s_xp[par].v, wvdone=wvdone, ev0=s_ev.v):
                    e.wait_ge(s_wt.h, wvdone)
                    e.wait_ge(s_xp[par].h, xv)
                    for tc in range(4):
                        tgt = ev0 + tc - 1
                        if tgt > 0:
                            e.wait_ge(s_ev.h, tgt)
                        pspc = [ps3[:, 0:512], ps4[:, 0:512]][tc % 2]
                        for c in range(NC):
                            ins = e.matmul(pspc, xb(par, c)[:, tc * 128:(tc + 1) * 128],
                                           wvs(c), start=(c == 0), stop=(c == NC - 1))
                        ins.then_inc(s_mm.h, 1)
                P.emit("tensor", f_vmm)
                s_mm.v += 4
                bmm_end[blk] = s_mm.v

                vev0 = s_ev.v

                def f_vev(e, vmm0=vmm0, sp0=s_spkp[0].v, sp1=s_spkp[1].v):
                    for tc in range(4):
                        pspc = [ps3[:, 0:512], ps4[:, 0:512]][tc % 2]
                        e.wait_ge(s_mm.h, vmm0 + tc + 1)
                        tgt = [sp0, sp1][tc % 2] + (16 if tc >= 2 else 0)
                        if tgt > 0:
                            e.wait_ge(s_spkp[tc % 2].h, tgt)
                        e.tensor_copy(kout[tc % 2][:], pspc).then_inc(s_ev.h, 1)
                P.emit("vector", f_vev)
                s_ev.v += 4

                def f_vst(e, vev0=vev0, blk=blk, vh=vh):
                    for tc in range(4):
                        e.wait_ge(s_ev.h, vev0 + tc + 1)
                        e.dma_start(out=v_scr[blk * 512 + tc * 128: blk * 512 + (tc + 1) * 128,
                                              vh * 512:(vh + 1) * 512],
                                    in_=kout[tc % 2][:]).then_inc(s_spkp[tc % 2].h, 16)
                P.emit("gpsimd", f_vst)
                s_spkp[0].v += 16 * 2
                s_spkp[1].v += 16 * 2

            bar()

        def p3():
            def f_ldt(e):
                e.dma_start(out=t1b[:], in_=t1_in[:, 0:OWN]).then_inc(s_tb.h, 16)
                e.dma_start(out=t2b[:], in_=t2_in[:, 0:OWN]).then_inc(s_tb.h, 16)
            P.emit("sync", f_ldt)
            s_tb.v += 32

            qmm_end = {}
            for qh in range(NH):
                par = qh % 2

                def f_ldq(e, qh=qh, par=par, mmv=qmm_end.get(qh - 2, s_mm.v)):
                    e.wait_ge(s_mm.h, mmv)
                    for g in range(8):
                        src = wq.rearrange("(c p) o -> p c o", p=128)[
                            :, g * 4:(g + 1) * 4, qh * 128:(qh + 1) * 128]
                        dst = bass.AP(wbig[:].tensor, par * 4096 + (g * 4) * 128,
                                      [[WBC, 128], [128, 4], [1, 128]])
                        e.dma_start(out=dst, in_=src).then_inc(s_wqp[par].h, 16)
                P.emit("gpsimd", f_ldq)
                s_wqp[par].v += 16 * 8

                qmm0 = s_mm.v
                qsq0 = s_sq.v
                qev0 = s_ev.v

                def f_qmm(e, qh=qh, par=par, wtv=s_wqp[par].v, qsq0=qsq0, qev0=qev0):
                    e.wait_ge(s_wqp[par].h, wtv)
                    for hf in range(2):
                        tgt = qsq0 + hf - 1
                        if tgt > 0:
                            e.wait_ge(s_sq.h, tgt)
                        pspc = [ps3[:, 0:512], ps4[:, 0:512]][hf]
                        for c in range(NC):
                            ins = e.matmul(pspc, wqs(par, c),
                                           bufb[c][:, hf * 512:(hf + 1) * 512],
                                           start=(c == 0), stop=(c == NC - 1))
                        ins.then_inc(s_mm.h, 1)
                    for hf in range(2):
                        e.wait_ge(s_sq.h, qsq0 + hf + 1)
                        tgt = qev0 + hf
                        if tgt > 0:
                            e.wait_ge(s_ev.h, tgt)
                        e.matmul(ps1[:], swp[:], kraw[hf][:],
                                 start=True, stop=True).then_inc(s_mm.h, 1)
                P.emit("tensor", f_qmm)
                s_mm.v += 4
                qmm_end[qh] = s_mm.v

                def f_qev(e, qmm0=qmm0, qev0=qev0):
                    for hf in range(2):
                        pspc = [ps3[:, 0:512], ps4[:, 0:512]][hf]
                        e.wait_ge(s_mm.h, qmm0 + hf + 1)
                        tgt = qev0 + hf - 1
                        if tgt > 0:
                            e.wait_ge(s_ev.h, tgt)
                        e.copy(kraw[hf][:], pspc).then_inc(s_sq.h, 1)
                P.emit("scalar", f_qev)
                s_sq.v += 2

                def f_qro(e, qh=qh, qmm0=qmm0, tbv=s_tb.v):
                    e.wait_ge(s_tb.h, tbv)
                    for hf in range(2):
                        e.wait_ge(s_mm.h, qmm0 + 2 + hf + 1)
                        qt = bufa[qh][:, hf * 512:(hf + 1) * 512]
                        e.tensor_tensor(p1s[:], kraw[hf][:], t1b[:, hf * 512:(hf + 1) * 512], ALU.mult)
                        e.tensor_tensor(p2s[:], ps1[:], t2b[:, hf * 512:(hf + 1) * 512], ALU.mult)
                        e.tensor_tensor(qt[:], p1s[:], p2s[:], ALU.add).then_inc(s_ev.h, 1)
                P.emit("vector", f_qro)
                s_ev.v += 2

            bar()

        def kbuf(par):
            return wbig[:, par * 4096:(par + 1) * 4096]

        def vbuf(par):
            return wbig[:, 8192 + par * 4096: 8192 + (par + 1) * 4096]

        def p4_variant(q):
            own, others, j0s = _variant_tables(q)
            visits = [(l, l, True) for l in range(8)]
            visits += [(8 + t, j0, False) for t, j0 in enumerate(j0s) if j0 < 8]
            nvis = len(visits)
            lastA = max(vi for vi, (_, j0, _) in enumerate(visits) if j0 < 4)
            lastB = nvis - 1
            nmask_thru = []
            nm = 0
            for (_, _, dg) in visits:
                if dg:
                    nm += 1
                nmask_thru.append(nm)

            kvh_ct = {}
            for kvh in range(NKV):
                par = kvh % 2

                def f_ldkv(e, kvh=kvh, par=par, ctv=kvh_ct.get(kvh - 2, 0),
                           sk0=s_spkp[0].v, sk1=s_spkp[1].v):
                    e.wait_ge(s_spkp[0].h, sk0)
                    e.wait_ge(s_spkp[1].h, sk1)
                    if kvh >= 2 and ctv > 0:
                        e.wait_ge(s_ct.h, ctv)
                    e.dma_start(out=kbuf(par),
                                in_=k_scr[kvh * 128:(kvh + 1) * 128, :]).then_inc(s_kvp[par].h, 16)
                    for g in range(8):
                        src = v_scr.rearrange("(c p) o -> p c o", p=128)[
                            :, g * 4:(g + 1) * 4, kvh * 128:(kvh + 1) * 128]
                        dst = bass.AP(wbig[:].tensor, 8192 + par * 4096 + g * 512,
                                      [[WBC, 128], [128, 4], [1, 128]])
                        e.dma_start(out=dst, in_=src).then_inc(s_kvp[par].h, 16)
                P.emit("sync", f_ldkv)
                s_kvp[par].v += 16 * 9
                kv_now = s_kvp[par].v

                for j2 in range(4):
                    qh = kvh * 4 + j2
                    smm0 = s_mm.v
                    sct0 = s_ct.v
                    sex0 = s_ex.v
                    smk0 = s_mk.v
                    sq_pre = s_sq.v
                    sev_pre = s_ev.v

                    def f_mm(e, par=par, qh=qh, visits=visits, smm0=smm0,
                             sct0=sct0, sex0=sex0, smk0=smk0,
                             nmask_thru=nmask_thru, lastA=lastA, lastB=lastB,
                             kv_now=kv_now, sq_pre=sq_pre, sev_pre=sev_pre,
                             nvis=nvis, par2=par):
                        e.wait_ge(s_kvp[par2].h, kv_now)
                        if sq_pre > 0:
                            e.wait_ge(s_sq.h, sq_pre)
                        if sev_pre > 0:
                            e.wait_ge(s_ev.h, sev_pre)
                        started = {0: False, 512: False}

                        def ctx_of(vi):
                            lk_, j0_, dg_ = visits[vi]
                            e.wait_ge(s_ex.h, sex0 + vi + 1)
                            if dg_:
                                e.wait_ge(s_mk.h, smk0 + nmask_thru[vi])
                            vt = vbuf(par)[:, lk_ * 128:(lk_ + 1) * 128]
                            pt = psb[vi % 2]
                            c0 = j0_ * 128
                            ins = None
                            for cs in (0, 512):
                                cc = max(cs, c0)
                                if cc >= cs + 512:
                                    continue
                                bank = ps1 if cs == 0 else ps2
                                first = not started[cs]
                                lastv = lastA if cs == 0 else lastB
                                e.matmul(bank[:, cc - cs:512], vt, pt[:, cc:cs + 512],
                                         start=first, stop=(vi == lastv))
                                ins = e.matmul(psd[:1, cc:cs + 512], ones_bf[:],
                                               pt[:, cc:cs + 512],
                                               start=first, stop=(vi == lastv))
                                started[cs] = True
                            ins.then_inc(s_ct.h, 1)

                        for vi, (lk, j0, dg) in enumerate(visits):
                            if vi >= 2:
                                e.wait_ge(s_ex.h, sex0 + vi - 1)
                            sps = [ps3, ps4][vi % 2]
                            kt = kbuf(par)[:, lk * 128:(lk + 1) * 128]
                            c0 = j0 * 128
                            ins = None
                            for cs in (0, 512):
                                cc = max(cs, c0)
                                if cc >= cs + 512:
                                    continue
                                ins = e.matmul(sps[:, cc:cs + 512], kt,
                                               bufa[qh][:, cc:cs + 512],
                                               start=True, stop=True)
                            ins.then_inc(s_mm.h, 1)
                            if vi >= 1:
                                ctx_of(vi - 1)
                        ctx_of(nvis - 1)
                    P.emit("tensor", f_mm)
                    s_mm.v += nvis
                    s_ct.v += nvis

                    def f_exp(e, visits=visits, smm0=smm0, sct0=sct0, sex0=sex0):
                        for vi, (lk, j0, dg) in enumerate(visits):
                            e.wait_ge(s_mm.h, smm0 + vi + 1)
                            if vi >= 2 and sct0 + vi - 1 > 0:
                                e.wait_ge(s_ct.h, sct0 + vi - 1)
                            sps = [ps3, ps4][vi % 2]
                            c0 = j0 * 128
                            e.activation(psb[vi % 2][:, c0:OWN], sps[:, c0:OWN],
                                         AF.Exp, bias=b15[:], scale=SCALE).then_inc(s_ex.h, 1)
                    P.emit("scalar", f_exp)
                    s_ex.v += nvis

                    def f_msk(e, visits=visits, sex0=sex0):
                        for vi, (lk, j0, dg) in enumerate(visits):
                            if not dg:
                                continue
                            e.wait_ge(s_ex.h, sex0 + vi + 1)
                            c0 = j0 * 128
                            e.tensor_tensor(psb[vi % 2][:, c0:c0 + 128],
                                            psb[vi % 2][:, c0:c0 + 128],
                                            tri[:], ALU.mult).then_inc(s_mk.h, 1)
                    P.emit("vector", f_msk)
                    s_mk.v += nmask_thru[-1]

                    def f_den(e, ctv=s_ct.v):
                        e.wait_ge(s_ct.h, ctv)
                        e.copy(den_sb[:1, :], psd[:1, :]).then_inc(s_sq.h, 1)
                    P.emit("scalar", f_den)
                    s_sq.v += 1

                    def f_bc(e, sqv=s_sq.v):
                        e.wait_ge(s_sq.h, sqv)
                        e.matmul(ps3[:, 0:512], ones_f[:], den_sb[:1, 0:512],
                                 start=True, stop=True)
                        e.matmul(ps4[:, 0:512], ones_f[:], den_sb[:1, 512:1024],
                                 start=True, stop=True).then_inc(s_mm.h, 1)
                    P.emit("tensor", f_bc)
                    s_mm.v += 1

                    def f_ev(e, qh=qh, mmv=s_mm.v):
                        e.wait_ge(s_mm.h, mmv)
                        e.reciprocal(recf[0][:], ps3[:, 0:512])
                        e.reciprocal(recf[1][:], ps4[:, 0:512]).then_inc(s_ev.h, 1)
                        e.tensor_tensor(bufb[qh][:, 0:512], ps1[:], recf[0][:], ALU.mult)
                        e.tensor_tensor(bufb[qh][:, 512:1024], ps2[:], recf[1][:], ALU.mult).then_inc(s_ev.h, 2)
                    P.emit("vector", f_ev)
                    s_ev.v += 3

                kvh_ct[kvh] = s_ct.v

            vals = [(sem, sem.v) for sem in ALL_SEMS if sem is not s_ph]
            for e in ENGS:
                def f(eng, vals=vals):
                    drain_waits(eng, vals)
                    eng.nop().then_inc(s_ph.h, 1)
                P.emit(e, f)

        def p4():
            tgt = s_ph.v + 5
            for q in range(4):
                snap = [s.v for s in ALL_SEMS]
                P.if_open(q)
                p4_variant(q)
                P.if_close()
                for s, v0 in zip(ALL_SEMS, snap):
                    s.v = v0
            s_ph.v = tgt
            for e in ENGS:
                def f(eng, tgt=tgt):
                    eng.wait_ge(s_ph.h, tgt)
                P.emit(e, f)

        def wos(par, c):
            return wbig[:, 8192 + par * 4096 + c * 128: 8192 + par * 4096 + (c + 1) * 128]

        wt_end = {}

        def p_wo():
            wmm_end = {}
            for och in range(NH):
                par = och % 2

                def f_ld(e, och=och, par=par, mmv=wmm_end.get(och - 2, 0)):
                    if mmv > 0:
                        e.wait_ge(s_mm2.h, mmv)
                    for g in range(8):
                        src = wo.rearrange("(c p) o -> p c o", p=128)[
                            :, g * 4:(g + 1) * 4, och * 128:(och + 1) * 128]
                        dst = bass.AP(wbig[:].tensor, 8192 + par * 4096 + (g * 4) * 128,
                                      [[WBC, 128], [128, 4], [1, 128]])
                        e.dma_start(out=dst, in_=src).then_inc(s_wop[par].h, 16)
                    e.dma_start(out=bufa[och],
                                in_=xT[och * 128:(och + 1) * 128, 0:OWN]).then_inc(s_wop[par].h, 16)
                P.emit("gpsimd", f_ld)
                s_wop[par].v += 16 * 9

                mm0 = s_mm2.v

                def f_mm(e, och=och, par=par, wtv=s_wop[par].v, ev0=s_ev2.v):
                    e.wait_ge(s_wop[par].h, wtv)
                    for hf in range(2):
                        i = och * 2 + hf
                        tgt = ev0 + hf - 3
                        if tgt > 0:
                            e.wait_ge(s_ev2.h, tgt)
                        pspc = [ps1[:], ps2[:], ps3[:, 0:512], ps4[:, 0:512]][i % 4]
                        for c in range(NC):
                            ins = e.matmul(pspc, wos(par, c),
                                           bufb[c][:, hf * 512:(hf + 1) * 512],
                                           start=(c == 0), stop=(c == NC - 1))
                        ins.then_inc(s_mm2.h, 1)
                P.emit("tensor", f_mm)
                s_mm2.v += 2
                wmm_end[och] = s_mm2.v

                def f_ev(e, och=och, mm0=mm0):
                    for hf in range(2):
                        i = och * 2 + hf
                        pspc = [ps1[:], ps2[:], ps3[:, 0:512], ps4[:, 0:512]][i % 4]
                        e.wait_ge(s_mm2.h, mm0 + hf + 1)
                        sl = bufa[och][:, hf * 512:(hf + 1) * 512]
                        e.tensor_tensor(sl, sl, pspc, ALU.add).then_inc(s_ev2.h, 1)
                P.emit("vector", f_ev)
                s_ev2.v += 2

                if och == 1:
                    def f_ldw0(e, sw0=s_swp[0].v, sw1=s_swp[1].v):
                        if sw0 > 0:
                            e.wait_ge(s_swp[0].h, sw0)
                        if sw1 > 0:
                            e.wait_ge(s_swp[1].h, sw1)
                        for ic in range(NC):
                            dst = wbig[:, ic * 256:(ic + 1) * 256]
                            e.dma_start_transpose(dst,
                                                  w_scr[0:256, ic * 128:(ic + 1) * 128]).then_inc(s_ftp[0].h, 16)
                    P.emit("sync", f_ldw0)
                    s_ftp[0].v += 16 * NC
                    wt_end[0] = s_ftp[0].v

            bar()

        def p5():
            for hf in range(2):
                ln_stats(lambda c, hf=hf: bufa[c][:, hf * 512:(hf + 1) * 512],
                         s_ph, s_ph.v, s_mm2, s_sq2, s_st2)

                def f_h2(e, hf=hf, sqv=s_sq2.v):
                    e.wait_ge(s_sq2.h, sqv)
                    for c in range(NC):
                        sl_in = bufa[c][:, hf * 512:(hf + 1) * 512]
                        sl_out = bufb[c][:, hf * 512:(hf + 1) * 512]
                        e.tensor_tensor(s1f[:], sl_in, m_rep[:], ALU.subtract)
                        e.tensor_tensor(sl_out, s1f[:], r_rep[:], ALU.mult).then_inc(s_h2.h, 1)
                P.emit("vector", f_h2)
                s_h2.v += NC

            bar()

        def p7_pre():
            def f_ldw1(e, sw0=s_swp[0].v, sw1=s_swp[1].v):
                if sw0 > 0:
                    e.wait_ge(s_swp[0].h, sw0)
                if sw1 > 0:
                    e.wait_ge(s_swp[1].h, sw1)
                for ic in range(NC):
                    dst = wbig[:, 8192 + ic * 256: 8192 + (ic + 1) * 256]
                    e.dma_start_transpose(dst,
                                          w_scr[256:512, ic * 128:(ic + 1) * 128]).then_inc(s_ftp[1].h, 16)
            P.emit("sync", f_ldw1)
            s_ftp[1].v += 16 * NC
            wt_end[1] = s_ftp[1].v

        def p7():
            p7_ev0 = s_ev2.v
            ffn_end = {}
            for og in range(16):
                par = og % 2
                if og >= 2:
                    def f_ldw(e, og=og, par=par, mmv=ffn_end.get(og - 2, 0)):
                        if mmv > 0:
                            e.wait_ge(s_mm2.h, mmv)
                        for ic in range(NC):
                            dst = wbig[:, par * 8192 + ic * 256: par * 8192 + (ic + 1) * 256]
                            e.dma_start_transpose(dst,
                                                  w_scr[og * 256:(og + 1) * 256,
                                                        ic * 128:(ic + 1) * 128]).then_inc(s_ftp[og % 2].h, 16)
                    P.emit("sync", f_ldw)
                    s_ftp[og % 2].v += 16 * NC
                    wt_end[og] = s_ftp[og % 2].v

                for oc2 in range(2):
                    och = og * 2 + oc2
                    mm0 = s_mm2.v

                    def f_mm(e, par=par, oc2=oc2, och=och, wtv=wt_end[og], p7_ev0=p7_ev0):
                        e.wait_ge(s_ftp[par].h, wtv)
                        for hf in range(2):
                            i = och * 2 + hf
                            tgt = p7_ev0 + i - 1
                            if i >= 2:
                                e.wait_ge(s_ev2.h, tgt)
                            pspc = [ps1, ps2][hf]
                            for ic in range(NC):
                                lhs = wbig[:, par * 8192 + ic * 256 + oc2 * 128:
                                           par * 8192 + ic * 256 + oc2 * 128 + 128]
                                ins = e.matmul(pspc[:], lhs,
                                               bufb[ic][:, hf * 512:(hf + 1) * 512],
                                               start=(ic == 0), stop=(ic == NC - 1))
                            ins.then_inc(s_mm2.h, 1)
                    P.emit("tensor", f_mm)
                    s_mm2.v += 2
                    ffn_end[og] = s_mm2.v

                    def f_ep(e, och=och, mm0=mm0, outv=s_out.v):
                        for hf in range(2):
                            pspc = [ps1, ps2][hf]
                            e.wait_ge(s_mm2.h, mm0 + hf + 1)
                            i = och * 2 + hf
                            if i >= 2:
                                e.wait_ge(s_out.h, 16 * (i - 1))
                            e.scalar_tensor_tensor(outt[hf][:], pspc[:],
                                                   bias_sb[:, och:och + 1],
                                                   bufa[och][:, hf * 512:(hf + 1) * 512],
                                                   ALU.add, ALU.add).then_inc(s_ev2.h, 1)
                    P.emit("vector", f_ep)
                    s_ev2.v += 2

                    def f_st(e, och=och, evv=s_ev2.v):
                        for hf in range(2):
                            e.wait_ge(s_ev2.h, evv - 1 + hf)
                            e.dma_start(out=out[och * 128:(och + 1) * 128,
                                                hf * 512:(hf + 1) * 512],
                                        in_=outt[hf][:]).then_inc(s_out.h, 16)
                    P.emit("gpsimd", f_st)
                    s_out.v += 32

            def f_fin(e, ov=s_out.v):
                e.wait_ge(s_out.h, ov)
            P.emit("sync", f_fin)

        def dump_scr():
            if not DEBUG:
                return
            def f(e, xv=s_x2.v):
                e.dma_start(out=dbg["d_kscr"], in_=k_scr).then_inc(s_x2.h, 16)
                e.dma_start(out=dbg["d_vscr"], in_=v_scr).then_inc(s_x2.h, 16)
                e.dma_start(out=dbg["d_hscr"], in_=h_scr).then_inc(s_x2.h, 16)
                e.dma_start(out=dbg["d_wscr"], in_=w_scr).then_inc(s_x2.h, 16)
                e.wait_ge(s_x2.h, xv + 64)
            P.emit("gpsimd", f)
            s_x2.v += 64
            bar()

        def dump_sbuf(key, tiles):
            if not DEBUG:
                return
            def f(e, key=key, tiles=tiles, base=s_x2.v):
                for c in range(NC):
                    e.dma_start(out=dbg[key][c * 128:(c + 1) * 128, :],
                                in_=tiles[c]).then_inc(s_x2.h, 16)
                e.wait_ge(s_x2.h, base + 16 * NC)
            P.emit("gpsimd", f)
            s_x2.v += 16 * NC
            bar()

        NPH = int(os.environ.get("KPHASES", "10"))
        phases = [lambda: pass_a(0), lambda: pass_a(1), lambda: pass_b(0),
                  lambda: pass_b(1), p3, p4, p_wo, p7_pre, p5, p7]
        names = ["A1", "A2", "B1", "B2", "P3", "P4", "WO", "PRE", "P5", "P7"]
        for i in range(min(NPH, 10)):
            phases[i]()
            if DEBUG:
                if names[i] == "P3":
                    dump_sbuf("d_q", bufa)
                elif names[i] == "P4":
                    dump_sbuf("d_ctx", bufb)
                elif names[i] == "WO":
                    dump_sbuf("d_x2", bufa)
                elif names[i] == "P5":
                    dump_sbuf("d_h2", bufb)
        if DEBUG:
            dump_scr()
        if NPH < 10:
            def f_dummy(e):
                e.memset(outt[0][:], 0.0).then_inc(s_st2.h, 1)
            P.emit("vector", f_dummy)
            s_st2.v += 1

            def f_fin0(e, stv=s_st2.v, ov=s_out.v):
                e.wait_ge(s_st2.h, stv)
                e.dma_start(out=out[0:128, 0:512], in_=outt[0][:]).then_inc(s_out.h, 16)
                e.wait_ge(s_out.h, ov + 16)
            P.emit("sync", f_fin0)
            s_out.v += 16

        block = es.enter_context(nc.Block())
        block.sync(lambda e: P.replay("sync", e))
        block.gpsimd(lambda e: P.replay("gpsimd", e))
        block.vector(lambda e: P.replay("vector", e))
        block.scalar(lambda e: P.replay("scalar", e))
        block.tensor(lambda e: P.replay("tensor", e))

    return nc


def _perm_for(q):
    A, B = q, 7 - q
    own = list(range(A * 512, (A + 1) * 512)) + list(range(B * 512, (B + 1) * 512))
    own_set = set(own)
    others = [t for t in range(T) if t not in own_set]
    return np.array(own + others, dtype=np.int64)


def _rope_tables(perm):
    inv = 1.0 / (500000.0 ** (np.arange(0, HD, 2, dtype=np.float64) / HD))
    t = np.arange(T, dtype=np.float64)
    fr = np.outer(t, inv)
    cos = np.cos(fr).astype(np.float32)[perm]
    sin = np.sin(fr).astype(np.float32)[perm]
    t1 = np.concatenate([cos.T, cos.T], axis=0).astype(ml_dtypes.bfloat16)
    t2 = np.concatenate([-sin.T, sin.T], axis=0).astype(ml_dtypes.bfloat16)
    return np.ascontiguousarray(t1), np.ascontiguousarray(t2)


_NC_CACHE = {}


def kernel(L0_attn_x, L0_attn_Wq, L0_attn_Wk, L0_attn_Wv, L0_attn_Wo,
           L0_ffn_weight_packed, L0_ffn_weight_scale, L0_ffn_weight_var_mask,
           L0_ffn_bias):
    x = np.asarray(L0_attn_x, dtype=np.float32)
    wq = np.ascontiguousarray(np.asarray(L0_attn_Wq, dtype=np.float32))
    wk = np.ascontiguousarray(np.asarray(L0_attn_Wk, dtype=np.float32))
    wv = np.ascontiguousarray(np.asarray(L0_attn_Wv, dtype=np.float32))
    wo = np.ascontiguousarray(np.asarray(L0_attn_Wo, dtype=np.float32))
    pk_a = np.ascontiguousarray(np.asarray(L0_ffn_weight_packed, dtype=np.int32).reshape(4096, 2048))
    sc_a = np.ascontiguousarray(np.asarray(L0_ffn_weight_scale).reshape(4096, 128))
    mk_a = np.ascontiguousarray(np.asarray(L0_ffn_weight_var_mask).reshape(4096, 128).view(np.uint8))
    bias_pp = np.ascontiguousarray(np.asarray(L0_ffn_bias, dtype=np.float32).reshape(32, 128).T)

    tri = np.zeros((128, 128), dtype=ml_dtypes.bfloat16)
    for kk in range(128):
        tri[kk, kk:] = 1.0
    swpm = np.zeros((128, 128), dtype=ml_dtypes.bfloat16)
    for m in range(128):
        swpm[(m + 64) % 128, m] = 1.0

    if "nc" not in _NC_CACHE:
        _NC_CACHE["nc"] = build_nc()
    nc = _NC_CACHE["nc"]

    in_maps = []
    perms = []
    for c in range(8):
        b, q = c // 4, c % 4
        perm = _perm_for(q)
        perms.append(perm)
        xTc = np.ascontiguousarray(x[b][perm].T)
        t1, t2 = _rope_tables(perm)
        in_maps.append(dict(
            xT=xTc, wq=wq, wk=wk, wv=wv, wo=wo, pk=pk_a, sc=sc_a, mk=mk_a,
            bias_in=bias_pp, t1_in=t1, t2_in=t2, tri_in=tri, swp_in=swpm,
        ))

    res = run_bass_kernel_spmd(nc, in_maps, core_ids=list(range(8)))

    outa = np.empty((2, T, D), dtype=np.float32)
    for c in range(8):
        b = c // 4
        o = np.asarray(res.results[c]["o"])
        outa[b][perms[c][:OWN]] = o.T
    if DEBUG:
        kernel.last_res = res
        kernel.last_perms = perms
    return outa


# version 13 (final)
# speedup vs baseline: 1.1818x; 1.0132x over previous
import os
import math
import contextlib
import numpy as np
import ml_dtypes

import concourse.bass as bass
import concourse.mybir as mybir
from concourse.bass_utils import run_bass_kernel_spmd

F32 = mybir.dt.float32
BF16 = mybir.dt.bfloat16
I32 = mybir.dt.int32
F16 = mybir.dt.float16
U8 = mybir.dt.uint8
AF = mybir.ActivationFunctionType
ALU = mybir.AluOpType

D = 4096
T = 4096
NH = 32
NKV = 8
HD = 128
NC = 32
OWN = 1024
NBLK = 8
SCALE = 1.0 / math.sqrt(HD)
EXP_BIAS = -15.0
EPS = 1e-5

ENGS = ("sync", "gpsimd", "vector", "scalar", "tensor")

DEBUG = False


def _variant_tables(q):
    A, B = q, 7 - q
    own = list(range(4 * A, 4 * A + 4)) + list(range(4 * B, 4 * B + 4))
    others = [g for g in range(32) if g not in own]
    j0 = [sum(1 for o in own if o < g) for g in others]
    return own, others, j0


class Sem:
    def __init__(self, es, nc, name):
        self.h = es.enter_context(nc.semaphore(name))
        self.v = 0


class Prog:
    def __init__(self):
        self.ops = {e: [] for e in ENGS}

    def emit(self, eng, fn):
        self.ops[eng].append(("op", fn))

    def if_open(self, v):
        for e in ENGS:
            self.ops[e].append(("if", v))

    def if_close(self):
        for e in ENGS:
            self.ops[e].append(("endif", None))

    def replay(self, eng_name, eng):
        need_if = any(k == "if" for k, _ in self.ops[eng_name])
        vmod = None
        if need_if:
            pid = eng.to_reg(eng.partition_id())
            vmod = eng.alloc_register(f"vm_{eng_name}")
            eng.reg_mod(vmod, pid, 4)
        guards = []
        for kind, payload in self.ops[eng_name]:
            if kind == "op":
                payload(eng)
            elif kind == "if":
                g = eng.If_eq(vmod, payload)
                g.__enter__()
                guards.append(g)
            else:
                guards.pop().__exit__(None, None, None)
        eng.end_ifs()


def build_nc():
    nc = bass.Bass("TRN2", target_bir_lowering=False, debug=False)

    xT = nc.dram_tensor("xT", [D, T], F32, kind="ExternalInput").ap()
    wq = nc.dram_tensor("wq", [D, D], F32, kind="ExternalInput").ap()
    wk = nc.dram_tensor("wk", [D, 1024], F32, kind="ExternalInput").ap()
    wv = nc.dram_tensor("wv", [D, 1024], F32, kind="ExternalInput").ap()
    wo = nc.dram_tensor("wo", [D, D], F32, kind="ExternalInput").ap()
    pk = nc.dram_tensor("pk", [4096, 2048], I32, kind="ExternalInput").ap()
    sc = nc.dram_tensor("sc", [4096, 128], F16, kind="ExternalInput").ap()
    mk = nc.dram_tensor("mk", [4096, 128], U8, kind="ExternalInput").ap()
    bias_in = nc.dram_tensor("bias_in", [128, 32], F32, kind="ExternalInput").ap()
    t1_in = nc.dram_tensor("t1_in", [128, T], BF16, kind="ExternalInput").ap()
    t2_in = nc.dram_tensor("t2_in", [128, T], BF16, kind="ExternalInput").ap()
    tri_in = nc.dram_tensor("tri_in", [128, 128], BF16, kind="ExternalInput").ap()
    swp_in = nc.dram_tensor("swp_in", [128, 128], BF16, kind="ExternalInput").ap()

    out = nc.dram_tensor("o", [D, OWN], F32, kind="ExternalOutput").ap()

    k_scr = nc.dram_tensor("k_scr", [1024, T], BF16, kind="Internal").ap()
    v_scr = nc.dram_tensor("v_scr", [T, 1024], BF16, kind="Internal").ap()
    h_scr = nc.dram_tensor("h_scr", [D, T], BF16, kind="Internal").ap()
    w_scr = nc.dram_tensor("w_scr", [4096, 4096], BF16, kind="Internal").ap()

    dbg = {}
    if DEBUG:
        for nm, shp in [("d_kscr", [1024, T]), ("d_vscr", [T, 1024]),
                        ("d_hscr", [D, T]), ("d_wscr", [4096, 4096]),
                        ("d_q", [D, OWN]), ("d_ctx", [D, OWN]),
                        ("d_x2", [D, OWN]), ("d_h2", [D, OWN])]:
            dbg[nm] = nc.dram_tensor(nm, shp, BF16, kind="ExternalOutput").ap()

    es = contextlib.ExitStack()
    with es:
        def sbt(name, shape, dtype):
            return es.enter_context(nc.sbuf_tensor(name, shape, dtype))

        bufa_t = sbt("bufa", [128, NC * OWN], BF16)
        bufb_t = sbt("bufb", [128, NC * OWN], BF16)
        bufa = [bufa_t[:, i * OWN:(i + 1) * OWN] for i in range(NC)]
        bufb = [bufb_t[:, i * OWN:(i + 1) * OWN] for i in range(NC)]
        WBC = 16384
        wbig = sbt("wbig", [128, WBC], BF16)

        t1b = sbt("t1b", [128, OWN], BF16)
        t2b = sbt("t2b", [128, OWN], BF16)
        kout = [sbt(f"kout{i}", [128, 512], BF16) for i in range(2)]
        s1f = sbt("s1f", [128, 512], F32)
        x2t = [sbt(f"x2t{i}", [128, 512], BF16) for i in range(2)]
        m_rep = sbt("m_rep", [128, 512], F32)
        r_rep = sbt("r_rep", [128, 512], F32)
        mrow = sbt("mrow", [1, 512], F32)
        rrow = sbt("rrow", [1, 512], F32)
        p1s = sbt("p1s", [128, 512], BF16)
        p2s = sbt("p2s", [128, 512], BF16)
        psb = [sbt(f"psb{i}", [128, OWN], BF16) for i in range(2)]
        recf = [sbt(f"recf{i}", [128, 512], F32) for i in range(2)]
        outt = [sbt(f"outt{i}", [128, 512], F32) for i in range(2)]
        pkt = [sbt(f"pkt{i}", [128, 256], I32) for i in range(2)]
        qi32 = sbt("qi32", [128, 512], I32)
        wdt = [sbt(f"wdt{i}", [128, 512], BF16) for i in range(2)]
        sct = sbt("sct", [128, 128], F16)
        mkt = sbt("mkt", [128, 128], U8)
        seff = sbt("seff", [128, 128], F32)
        tri = sbt("tri_sb", [128, 128], BF16)
        swp = sbt("swp_sb", [128, 128], BF16)
        kraw = [sbt(f"kraw{i}", [128, 512], BF16) for i in range(2)]
        ones_bf = sbt("ones_bf", [128, 1], BF16)
        ones_f = sbt("ones_f", [1, 128], F32)
        b15 = sbt("b15", [128, 1], F32)
        beps = sbt("beps", [1, 1], F32)
        bias_sb = sbt("bias_sb", [128, 32], F32)
        den_sb = sbt("den_sb", [1, OWN], F32)

        ps1 = es.enter_context(nc.psum_tensor("ps1", [128, 512], F32))
        ps2 = es.enter_context(nc.psum_tensor("ps2", [128, 512], F32))
        ps3 = es.enter_context(nc.psum_tensor("ps3", [128, 1024], F32))
        ps4 = es.enter_context(nc.psum_tensor("ps4", [128, 1024], F32))
        psd = es.enter_context(nc.psum_tensor("psd", [1, 1024], F32))

        s_x = Sem(es, nc, "s_x")
        s_wt = Sem(es, nc, "s_wt")
        s_tb = Sem(es, nc, "s_tb")
        s_mm = Sem(es, nc, "s_mm")
        s_ct = Sem(es, nc, "s_ct")
        s_sq = Sem(es, nc, "s_sq")
        s_st = Sem(es, nc, "s_st")
        s_h1 = Sem(es, nc, "s_h1")
        s_ev = Sem(es, nc, "s_ev")
        s_ex = Sem(es, nc, "s_ex")
        s_mk = Sem(es, nc, "s_mk")
        s_spk = Sem(es, nc, "s_spk")
        s_sph = Sem(es, nc, "s_sph")
        s_kv = Sem(es, nc, "s_kv")
        s_dq = Sem(es, nc, "s_dq")
        s_out = Sem(es, nc, "s_out")
        s_ph = Sem(es, nc, "s_ph")
        s_wt2 = Sem(es, nc, "s_wt2")
        s_mm2 = Sem(es, nc, "s_mm2")
        s_ev2 = Sem(es, nc, "s_ev2")
        s_x2 = Sem(es, nc, "s_x2")
        s_sq2 = Sem(es, nc, "s_sq2")
        s_st2 = Sem(es, nc, "s_st2")
        s_h2 = Sem(es, nc, "s_h2")
        s_pk = Sem(es, nc, "s_pk")
        s_sw = Sem(es, nc, "s_sw")
        s_swp = [Sem(es, nc, f"s_swp{i}") for i in range(2)]
        s_spkp = [Sem(es, nc, f"s_spkp{i}") for i in range(2)]
        s_xp = [Sem(es, nc, f"s_xp{i}") for i in range(2)]
        s_sphp = [Sem(es, nc, f"s_sphp{i}") for i in range(2)]
        s_wqp = [Sem(es, nc, f"s_wqp{i}") for i in range(2)]
        s_wop = [Sem(es, nc, f"s_wop{i}") for i in range(2)]
        s_ftp = [Sem(es, nc, f"s_ftp{i}") for i in range(2)]
        s_kvp = [Sem(es, nc, f"s_kvp{i}") for i in range(2)]
        s_tbp = [Sem(es, nc, f"s_tbp{i}") for i in range(2)]
        s_pkp = [Sem(es, nc, f"s_pkp{i}") for i in range(2)]

        ALL_SEMS = [s_x, s_wt, s_tb, s_mm, s_ct, s_sq, s_st, s_h1, s_ev,
                    s_ex, s_mk, s_spk, s_sph, s_kv, s_dq, s_out, s_ph,
                    s_wt2, s_mm2, s_ev2, s_x2, s_sq2, s_st2, s_h2, s_pk, s_sw,
                    *s_swp, *s_spkp, *s_xp, *s_sphp, *s_wqp, *s_wop, *s_ftp, *s_kvp, *s_tbp, *s_pkp]

        P = Prog()

        def drain_waits(eng, vals):
            for sem, v in vals:
                if v > 0:
                    eng.wait_ge(sem.h, v)

        def bar():
            tgt = s_ph.v + 5
            vals = [(sem, sem.v) for sem in ALL_SEMS if sem is not s_ph]
            for e in ENGS:
                def f(eng, tgt=tgt, vals=vals):
                    drain_waits(eng, vals)
                    eng.nop().then_inc(s_ph.h, 1)
                    eng.wait_ge(s_ph.h, tgt)
                P.emit(e, f)
            s_ph.v = tgt

        def xb(p, c):
            return bufa_t[:, c * OWN + p * 512: c * OWN + (p + 1) * 512]

        def xgrp_ap(p, g):
            return bass.AP(bufa_t[:].tensor, (g * 4) * OWN + p * 512,
                           [[NC * OWN, 128], [OWN, 4], [1, 512]])

        def f(e):
            e.dma_start(out=tri[:], in_=tri_in).then_inc(s_x.h, 16)
            e.dma_start(out=swp[:], in_=swp_in).then_inc(s_x.h, 16)
            e.dma_start(out=bias_sb[:], in_=bias_in).then_inc(s_x.h, 16)
            e.wait_ge(s_x.h, 48)
        P.emit("sync", f)
        s_x.v += 48

        def f(e):
            e.memset(ones_bf[:], 1.0)
            e.memset(ones_f[:], 1.0)
            e.memset(b15[:], EXP_BIAS)
            e.memset(beps[:], EPS).then_inc(s_st.h, 1)
        P.emit("vector", f)
        s_st.v += 1
        bar()

        def ln_stats(tiles, xsem, xval, smm, ssq, sst):
            sq0 = ssq.v
            mm0 = smm.v

            def f_sq(e, sq0=sq0, mm0=mm0):
                e.wait_ge(xsem.h, xval)
                for c in range(NC):
                    if c >= 2:
                        e.wait_ge(smm.h, mm0 + c - 1)
                    e.activation(x2t[c % 2][:], tiles(c), AF.Square).then_inc(ssq.h, 1)
            P.emit("scalar", f_sq)
            ssq.v += NC

            def f_mm(e, sq0=sq0, stv0=sst.v):
                e.wait_ge(xsem.h, xval)
                if stv0 > 0:
                    e.wait_ge(sst.h, stv0)
                for c in range(NC):
                    e.matmul(psd[:1, 0:512], ones_bf[:], tiles(c),
                             start=(c == 0), stop=(c == NC - 1))
                    e.wait_ge(ssq.h, sq0 + c + 1)
                    e.matmul(psd[:1, 512:1024], ones_bf[:], x2t[c % 2][:],
                             start=(c == 0), stop=(c == NC - 1)).then_inc(smm.h, 1)
            P.emit("tensor", f_mm)
            smm.v += NC

            def f_vs(e, mmv=smm.v):
                e.wait_ge(smm.h, mmv)
                e.tensor_scalar(mrow[:], psd[:1, 0:512], 1.0 / D, None, ALU.mult)
                e.tensor_scalar(rrow[:], psd[:1, 512:1024], 1.0 / D, None, ALU.mult)
                e.tensor_tensor(s1f[:1, :], mrow[:], mrow[:], ALU.mult)
                e.tensor_tensor(rrow[:], rrow[:], s1f[:1, :], ALU.subtract).then_inc(sst.h, 1)
            P.emit("vector", f_vs)
            sst.v += 1

            def f_sqrt(e, stv=sst.v):
                e.wait_ge(sst.h, stv)
                e.activation(rrow[:], rrow[:], AF.Sqrt, bias=beps[:]).then_inc(ssq.h, 1)
            P.emit("scalar", f_sqrt)
            ssq.v += 1

            def f_rec(e, sqv=ssq.v):
                e.wait_ge(ssq.h, sqv)
                e.reciprocal(rrow[:], rrow[:]).then_inc(sst.h, 1)
            P.emit("vector", f_rec)
            sst.v += 1

            def f_bc(e, stv=sst.v, evv=s_ev.v):
                e.wait_ge(sst.h, stv)
                if evv > 0:
                    e.wait_ge(s_ev.h, evv)
                e.matmul(ps1[:], ones_f[:], mrow[:], start=True, stop=True)
                e.matmul(ps2[:], ones_f[:], rrow[:], start=True, stop=True).then_inc(smm.h, 1)
            P.emit("tensor", f_bc)
            smm.v += 1

            def f_cp(e, mmv=smm.v):
                e.wait_ge(smm.h, mmv)
                e.copy(m_rep[:], ps1[:])
                e.copy(r_rep[:], ps2[:]).then_inc(ssq.h, 1)
            P.emit("scalar", f_cp)
            ssq.v += 1

        def wks(s, c):
            return wbig[:, s * 4096 + c * 128: s * 4096 + (c + 1) * 128]

        def wvs(c):
            return wbig[:, c * 512:(c + 1) * 512]

        def wqs(par, c):
            return wbig[:, par * 4096 + c * 128: par * 4096 + (c + 1) * 128]

        NSUB = 8

        def p6_och(och):
            def f_lds(e, och=och, stv=s_st2.v):
                if stv > 0:
                    e.wait_ge(s_st2.h, stv)
                e.dma_start(out=sct[:], in_=sc[och * 128:(och + 1) * 128, :]).then_inc(s_x2.h, 16)
                e.dma_start(out=mkt[:], in_=mk[och * 128:(och + 1) * 128, :]).then_inc(s_x2.h, 16)
            P.emit("sync", f_lds)
            s_x2.v += 32

            def f_se(e, xv=s_x2.v, dqv=s_dq.v):
                e.wait_ge(s_x2.h, xv)
                if dqv > 0:
                    e.wait_ge(s_dq.h, dqv)
                e.tensor_scalar(s1f[:, 0:128], mkt[:], 0.5, 0.5, ALU.mult, ALU.add)
                e.tensor_tensor(seff[:], sct[:], s1f[:, 0:128], ALU.mult).then_inc(s_st2.h, 1)
            P.emit("vector", f_se)
            s_st2.v += 1
            se_done = s_st2.v

            for sub in range(NSUB):
                par = sub % 2
                i = och * NSUB + sub

                def f_ldp(e, och=och, sub=sub, par=par, i=i, dqv=s_dq.v):
                    if i >= 2:
                        e.wait_ge(s_dq.h, dqv - 1)
                    e.dma_start(out=pkt[par][:],
                                in_=pk[och * 128:(och + 1) * 128,
                                       sub * 256:(sub + 1) * 256]).then_inc(s_pkp[par].h, 16)
                P.emit("sync", f_ldp)
                s_pkp[par].v += 16

                def f_nib(e, par=par, xv=s_pkp[par].v, dqv=s_dq.v):
                    e.wait_ge(s_pkp[par].h, xv)
                    if dqv >= 2:
                        e.wait_ge(s_dq.h, dqv)
                    e.tensor_scalar(qi32[:, 0:512:2], pkt[par][:], 15, None, ALU.bitwise_and)
                    e.tensor_scalar(qi32[:, 1:512:2], pkt[par][:], 4, None,
                                    ALU.logical_shift_right).then_inc(s_dq.h, 1)
                P.emit("vector", f_nib)
                s_dq.v += 1

                def f_stt(e, sub=sub, par=par, i=i, dqv=s_dq.v,
                          se_done=se_done, spv=s_swp[par].v):
                    e.wait_ge(s_dq.h, dqv)
                    e.wait_ge(s_st2.h, se_done)
                    if spv > 0:
                        e.wait_ge(s_swp[par].h, spv)
                    sl = bass.AP(seff[:].tensor, seff[:].offset + sub * 16,
                                 [[128, 128], [1, 16], [0, 32]])
                    qv = qi32[:].rearrange("p (g i) -> p g i", i=32)
                    wv_ = wdt[par][:].rearrange("p (g i) -> p g i", i=32)
                    e.scalar_tensor_tensor(wv_, qv, 8.0, sl,
                                           ALU.subtract, ALU.mult).then_inc(s_dq.h, 1)
                P.emit("vector", f_stt)
                s_dq.v += 1

                def f_st(e, och=och, sub=sub, par=par, dqv=s_dq.v):
                    e.wait_ge(s_dq.h, dqv)
                    e.dma_start(out=w_scr[och * 128:(och + 1) * 128,
                                          sub * 512:(sub + 1) * 512],
                                in_=wdt[par][:]).then_inc(s_swp[par].h, 16)
                P.emit("gpsimd", f_st)
                s_swp[par].v += 16

        def pass_a(half):
            wt0 = s_wt.v

            def f_ldwk(e, half=half):
                for c in range(NC):
                    src = wk[c * 128:(c + 1) * 128,
                             half * 512:(half + 1) * 512].rearrange("p (s o) -> p s o", s=4)
                    dst = bass.AP(wbig[:].tensor, c * 128,
                                  [[WBC, 128], [4096, 4], [1, 128]])
                    e.dma_start(out=dst, in_=src).then_inc(s_wt.h, 16)
            P.emit("gpsimd", f_ldwk)
            s_wt.v += 16 * NC

            sph_end = {}
            amm_end = {}
            for blk in range(NBLK):
                par = blk % 2

                if half == 0:
                    def f_ldx(e, blk=blk, par=par, spv=sph_end.get(blk - 2, 0)):
                        if spv > 0:
                            e.wait_ge(s_sphp[par].h, spv)
                        for g in range(8):
                            src = xT.rearrange("(c p) t -> p c t", p=128)[
                                :, g * 4:(g + 1) * 4, blk * 512:(blk + 1) * 512]
                            e.dma_start(out=xgrp_ap(par, g), in_=src).then_inc(s_xp[par].h, 16)
                    P.emit("gpsimd", f_ldx)
                    s_xp[par].v += 16 * 8

                    ln_stats(lambda c, par=par: xb(par, c), s_xp[par], s_xp[par].v,
                             s_mm, s_sq, s_st)

                    def f_h1(e, par=par, sqv=s_sq.v):
                        e.wait_ge(s_sq.h, sqv)
                        for c in range(NC):
                            e.tensor_tensor(s1f[:], xb(par, c), m_rep[:], ALU.subtract)
                            e.tensor_tensor(xb(par, c), s1f[:], r_rep[:], ALU.mult).then_inc(s_h1.h, 1)
                    P.emit("vector", f_h1)
                    s_h1.v += NC

                    if blk < 2:
                        def f_own(e, par=par, blk=blk, h1v=s_h1.v):
                            for c in range(NC):
                                e.wait_ge(s_h1.h, h1v - NC + c + 1)
                                ins = e.copy(bufb[c][:, blk * 512:(blk + 1) * 512], xb(par, c))
                            ins.then_inc(s_sq.h, 1)
                        P.emit("scalar", f_own)
                        s_sq.v += 1
                else:
                    def f_ldh(e, blk=blk, par=par, mmv=amm_end.get(blk - 2, 0),
                              sp0=s_sphp[0].v, sp1=s_sphp[1].v):
                        if mmv > 0:
                            e.wait_ge(s_mm.h, mmv)
                        if sp0 > 0:
                            e.wait_ge(s_sphp[0].h, sp0)
                        if sp1 > 0:
                            e.wait_ge(s_sphp[1].h, sp1)
                        for g in range(8):
                            src = h_scr.rearrange("(c p) t -> p c t", p=128)[
                                :, g * 4:(g + 1) * 4, blk * 512:(blk + 1) * 512]
                            e.dma_start(out=xgrp_ap(par, g), in_=src).then_inc(s_xp[par].h, 16)
                    P.emit("sync", f_ldh)
                    s_xp[par].v += 16 * 8

                def f_ldt(e, blk=blk, par=par, evv=max(0, s_ev.v - 4)):
                    if evv > 0:
                        e.wait_ge(s_ev.h, evv)
                    e.dma_start(out=t1b[:, par * 512:(par + 1) * 512],
                                in_=t1_in[:, blk * 512:(blk + 1) * 512]).then_inc(s_tbp[par].h, 16)
                    e.dma_start(out=t2b[:, par * 512:(par + 1) * 512],
                                in_=t2_in[:, blk * 512:(blk + 1) * 512]).then_inc(s_tbp[par].h, 16)
                P.emit("sync", f_ldt)
                s_tbp[par].v += 32

                kmm0 = s_mm.v
                h1v = s_h1.v
                xv = s_x.v
                kev_base = s_ev.v
                ksq0 = s_sq.v

                def f_kmm(e, par=par, wt0=wt0, h1v=h1v, xv=s_xp[par].v,
                          ksq0=ksq0, half=half, kev_base=kev_base):
                    e.wait_ge(s_wt.h, wt0 + 16 * NC)
                    if half == 0:
                        e.wait_ge(s_h1.h, h1v)
                    else:
                        e.wait_ge(s_xp[par].h, xv)
                    for s in range(4):
                        tgt = ksq0 + s - 1
                        if tgt > 0:
                            e.wait_ge(s_sq.h, tgt)
                        pspc = [ps3[:, 0:512], ps4[:, 0:512]][s % 2]
                        for c in range(NC):
                            ins = e.matmul(pspc, wks(s, c), xb(par, c),
                                           start=(c == 0), stop=(c == NC - 1))
                        ins.then_inc(s_mm.h, 1)
                    for s in range(4):
                        e.wait_ge(s_sq.h, ksq0 + s + 1)
                        if kev_base + s > 0:
                            e.wait_ge(s_ev.h, kev_base + s)
                        e.matmul(ps1[:], swp[:], kraw[s % 2][:],
                                 start=True, stop=True).then_inc(s_mm.h, 1)
                P.emit("tensor", f_kmm)
                s_mm.v += 8
                amm_end[blk] = s_mm.v

                def f_kev(e, kmm0=kmm0, ksq0=ksq0, kev_base=kev_base):
                    for s in range(4):
                        pspc = [ps3[:, 0:512], ps4[:, 0:512]][s % 2]
                        e.wait_ge(s_mm.h, kmm0 + s + 1)
                        if kev_base + s - 1 > 0:
                            e.wait_ge(s_ev.h, kev_base + s - 1)
                        e.copy(kraw[s % 2][:], pspc).then_inc(s_sq.h, 1)
                P.emit("scalar", f_kev)
                s_sq.v += 4

                kev0 = s_ev.v

                def f_kro(e, par=par, kmm0=kmm0, tbv=s_tbp[par].v,
                          sp0=s_spkp[0].v, sp1=s_spkp[1].v):
                    e.wait_ge(s_tbp[par].h, tbv)
                    for s in range(4):
                        e.wait_ge(s_mm.h, kmm0 + 4 + s + 1)
                        tgt = [sp0, sp1][s % 2] + (16 if s >= 2 else 0)
                        if tgt > 0:
                            e.wait_ge(s_spkp[s % 2].h, tgt)
                        ko = kout[s % 2]
                        e.tensor_tensor(p1s[:], kraw[s % 2][:], t1b[:, par * 512:(par + 1) * 512], ALU.mult)
                        e.tensor_tensor(p2s[:], ps1[:], t2b[:, par * 512:(par + 1) * 512], ALU.mult)
                        e.tensor_tensor(ko[:], p1s[:], p2s[:], ALU.add).then_inc(s_ev.h, 1)
                P.emit("vector", f_kro)
                s_ev.v += 4

                def f_kst(e, kev0=kev0, blk=blk, half=half):
                    for s in range(4):
                        kvh = half * 4 + s
                        e.wait_ge(s_ev.h, kev0 + s + 1)
                        e.dma_start(out=k_scr[kvh * 128:(kvh + 1) * 128,
                                              blk * 512:(blk + 1) * 512],
                                    in_=kout[s % 2][:]).then_inc(s_spkp[s % 2].h, 16)
                P.emit("sync", f_kst)
                s_spkp[0].v += 16 * 2
                s_spkp[1].v += 16 * 2

                if half == 1:
                    for _j in range(4):
                        p6_och(blk * 4 + _j)

                if half == 0:
                    def f_sp(e, par=par, blk=blk, mmv=s_mm.v):
                        e.wait_ge(s_mm.h, mmv)
                        for g in range(8):
                            dst = h_scr.rearrange("(c p) t -> p c t", p=128)[
                                :, g * 4:(g + 1) * 4, blk * 512:(blk + 1) * 512]
                            e.dma_start(out=dst, in_=xgrp_ap(par, g)).then_inc(s_sphp[par].h, 16)
                    P.emit("gpsimd", f_sp)
                    s_sphp[par].v += 16 * 8
                    sph_end[blk] = s_sphp[par].v

            bar()

        def pass_b(vh):
            def f_ldwv(e, vh=vh):
                for c in range(NC):
                    e.dma_start(out=wvs(c),
                                in_=wv[c * 128:(c + 1) * 128,
                                       vh * 512:(vh + 1) * 512]).then_inc(s_wt.h, 16)
            P.emit("gpsimd", f_ldwv)
            wvdone = s_wt.v + 16 * NC
            s_wt.v += 16 * NC

            bmm_end = {}
            for blk in range(NBLK):
                par = blk % 2

                def f_ldh(e, blk=blk, par=par, mmv=bmm_end.get(blk - 2, 0),
                          sp0=s_sphp[0].v, sp1=s_sphp[1].v):
                    if mmv > 0:
                        e.wait_ge(s_mm.h, mmv)
                    if sp0 > 0:
                        e.wait_ge(s_sphp[0].h, sp0)
                    if sp1 > 0:
                        e.wait_ge(s_sphp[1].h, sp1)
                    for g in range(8):
                        src = h_scr.rearrange("(c p) t -> p c t", p=128)[
                            :, g * 4:(g + 1) * 4, blk * 512:(blk + 1) * 512]
                        e.dma_start(out=xgrp_ap(par, g), in_=src).then_inc(s_xp[par].h, 16)
                P.emit("sync", f_ldh)
                s_xp[par].v += 16 * 8

                vmm0 = s_mm.v

                def f_vmm(e, par=par, xv=s_xp[par].v, wvdone=wvdone, ev0=s_ev.v):
                    e.wait_ge(s_wt.h, wvdone)
                    e.wait_ge(s_xp[par].h, xv)
                    for tc in range(4):
                        tgt = ev0 + tc - 1
                        if tgt > 0:
                            e.wait_ge(s_ev.h, tgt)
                        pspc = [ps3[:, 0:512], ps4[:, 0:512]][tc % 2]
                        for c in range(NC):
                            ins = e.matmul(pspc, xb(par, c)[:, tc * 128:(tc + 1) * 128],
                                           wvs(c), start=(c == 0), stop=(c == NC - 1))
                        ins.then_inc(s_mm.h, 1)
                P.emit("tensor", f_vmm)
                s_mm.v += 4
                bmm_end[blk] = s_mm.v

                vev0 = s_ev.v

                def f_vev(e, vmm0=vmm0, sp0=s_spkp[0].v, sp1=s_spkp[1].v):
                    for tc in range(4):
                        pspc = [ps3[:, 0:512], ps4[:, 0:512]][tc % 2]
                        e.wait_ge(s_mm.h, vmm0 + tc + 1)
                        tgt = [sp0, sp1][tc % 2] + (16 if tc >= 2 else 0)
                        if tgt > 0:
                            e.wait_ge(s_spkp[tc % 2].h, tgt)
                        e.tensor_copy(kout[tc % 2][:], pspc).then_inc(s_ev.h, 1)
                P.emit("vector", f_vev)
                s_ev.v += 4

                def f_vst(e, vev0=vev0, blk=blk, vh=vh):
                    for tc in range(4):
                        e.wait_ge(s_ev.h, vev0 + tc + 1)
                        e.dma_start(out=v_scr[blk * 512 + tc * 128: blk * 512 + (tc + 1) * 128,
                                              vh * 512:(vh + 1) * 512],
                                    in_=kout[tc % 2][:]).then_inc(s_spkp[tc % 2].h, 16)
                P.emit("gpsimd", f_vst)
                s_spkp[0].v += 16 * 2
                s_spkp[1].v += 16 * 2

            bar()

        def p3():
            def f_ldt(e):
                e.dma_start(out=t1b[:], in_=t1_in[:, 0:OWN]).then_inc(s_tb.h, 16)
                e.dma_start(out=t2b[:], in_=t2_in[:, 0:OWN]).then_inc(s_tb.h, 16)
            P.emit("sync", f_ldt)
            s_tb.v += 32

            qmm_end = {}
            for qh in range(NH):
                par = qh % 2

                def f_ldq(e, qh=qh, par=par, mmv=qmm_end.get(qh - 2, s_mm.v)):
                    e.wait_ge(s_mm.h, mmv)
                    for g in range(8):
                        src = wq.rearrange("(c p) o -> p c o", p=128)[
                            :, g * 4:(g + 1) * 4, qh * 128:(qh + 1) * 128]
                        dst = bass.AP(wbig[:].tensor, par * 4096 + (g * 4) * 128,
                                      [[WBC, 128], [128, 4], [1, 128]])
                        e.dma_start(out=dst, in_=src).then_inc(s_wqp[par].h, 16)
                P.emit("gpsimd", f_ldq)
                s_wqp[par].v += 16 * 8

                qmm0 = s_mm.v
                qsq0 = s_sq.v
                qev0 = s_ev.v

                def f_qmm(e, qh=qh, par=par, wtv=s_wqp[par].v, qsq0=qsq0, qev0=qev0):
                    e.wait_ge(s_wqp[par].h, wtv)
                    for hf in range(2):
                        tgt = qsq0 + hf - 1
                        if tgt > 0:
                            e.wait_ge(s_sq.h, tgt)
                        pspc = [ps3[:, 0:512], ps4[:, 0:512]][hf]
                        for c in range(NC):
                            ins = e.matmul(pspc, wqs(par, c),
                                           bufb[c][:, hf * 512:(hf + 1) * 512],
                                           start=(c == 0), stop=(c == NC - 1))
                        ins.then_inc(s_mm.h, 1)
                    for hf in range(2):
                        e.wait_ge(s_sq.h, qsq0 + hf + 1)
                        tgt = qev0 + hf
                        if tgt > 0:
                            e.wait_ge(s_ev.h, tgt)
                        e.matmul(ps1[:], swp[:], kraw[hf][:],
                                 start=True, stop=True).then_inc(s_mm.h, 1)
                P.emit("tensor", f_qmm)
                s_mm.v += 4
                qmm_end[qh] = s_mm.v

                def f_qev(e, qmm0=qmm0, qev0=qev0):
                    for hf in range(2):
                        pspc = [ps3[:, 0:512], ps4[:, 0:512]][hf]
                        e.wait_ge(s_mm.h, qmm0 + hf + 1)
                        tgt = qev0 + hf - 1
                        if tgt > 0:
                            e.wait_ge(s_ev.h, tgt)
                        e.copy(kraw[hf][:], pspc).then_inc(s_sq.h, 1)
                P.emit("scalar", f_qev)
                s_sq.v += 2

                def f_qro(e, qh=qh, qmm0=qmm0, tbv=s_tb.v):
                    e.wait_ge(s_tb.h, tbv)
                    for hf in range(2):
                        e.wait_ge(s_mm.h, qmm0 + 2 + hf + 1)
                        qt = bufa[qh][:, hf * 512:(hf + 1) * 512]
                        e.tensor_tensor(p1s[:], kraw[hf][:], t1b[:, hf * 512:(hf + 1) * 512], ALU.mult)
                        e.tensor_tensor(p2s[:], ps1[:], t2b[:, hf * 512:(hf + 1) * 512], ALU.mult)
                        e.tensor_tensor(qt[:], p1s[:], p2s[:], ALU.add).then_inc(s_ev.h, 1)
                P.emit("vector", f_qro)
                s_ev.v += 2

            bar()

        def kbuf(par):
            return wbig[:, par * 4096:(par + 1) * 4096]

        def vbuf(par):
            return wbig[:, 8192 + par * 4096: 8192 + (par + 1) * 4096]

        def p4_variant(q):
            own, others, j0s = _variant_tables(q)
            visits = [(l, l, True) for l in range(8)]
            visits += [(8 + t, j0, False) for t, j0 in enumerate(j0s) if j0 < 8]
            nvis = len(visits)
            lastA = max(vi for vi, (_, j0, _) in enumerate(visits) if j0 < 4)
            lastB = nvis - 1
            nmask_thru = []
            nm = 0
            for (_, _, dg) in visits:
                if dg:
                    nm += 1
                nmask_thru.append(nm)

            kvh_ct = {}
            for kvh in range(NKV):
                par = kvh % 2

                def f_ldkv(e, kvh=kvh, par=par, ctv=kvh_ct.get(kvh - 2, 0),
                           sk0=s_spkp[0].v, sk1=s_spkp[1].v):
                    e.wait_ge(s_spkp[0].h, sk0)
                    e.wait_ge(s_spkp[1].h, sk1)
                    if kvh >= 2 and ctv > 0:
                        e.wait_ge(s_ct.h, ctv)
                    e.dma_start(out=kbuf(par),
                                in_=k_scr[kvh * 128:(kvh + 1) * 128, :]).then_inc(s_kvp[par].h, 16)
                    for g in range(8):
                        src = v_scr.rearrange("(c p) o -> p c o", p=128)[
                            :, g * 4:(g + 1) * 4, kvh * 128:(kvh + 1) * 128]
                        dst = bass.AP(wbig[:].tensor, 8192 + par * 4096 + g * 512,
                                      [[WBC, 128], [128, 4], [1, 128]])
                        e.dma_start(out=dst, in_=src).then_inc(s_kvp[par].h, 16)
                P.emit("sync", f_ldkv)
                s_kvp[par].v += 16 * 9
                kv_now = s_kvp[par].v

                for j2 in range(4):
                    qh = kvh * 4 + j2
                    smm0 = s_mm.v
                    sct0 = s_ct.v
                    sex0 = s_ex.v
                    smk0 = s_mk.v
                    sq_pre = s_sq.v
                    sev_pre = s_ev.v

                    def f_mm(e, par=par, qh=qh, visits=visits, smm0=smm0,
                             sct0=sct0, sex0=sex0, smk0=smk0,
                             nmask_thru=nmask_thru, lastA=lastA, lastB=lastB,
                             kv_now=kv_now, sq_pre=sq_pre, sev_pre=sev_pre,
                             nvis=nvis, par2=par):
                        e.wait_ge(s_kvp[par2].h, kv_now)
                        if sq_pre > 0:
                            e.wait_ge(s_sq.h, sq_pre)
                        if sev_pre > 0:
                            e.wait_ge(s_ev.h, sev_pre)
                        started = {0: False, 512: False}

                        def ctx_of(vi):
                            lk_, j0_, dg_ = visits[vi]
                            e.wait_ge(s_ex.h, sex0 + vi + 1)
                            if dg_:
                                e.wait_ge(s_mk.h, smk0 + nmask_thru[vi])
                            vt = vbuf(par)[:, lk_ * 128:(lk_ + 1) * 128]
                            pt = psb[vi % 2]
                            c0 = j0_ * 128
                            ins = None
                            for cs in (0, 512):
                                cc = max(cs, c0)
                                if cc >= cs + 512:
                                    continue
                                bank = ps1 if cs == 0 else ps2
                                first = not started[cs]
                                lastv = lastA if cs == 0 else lastB
                                e.matmul(bank[:, cc - cs:512], vt, pt[:, cc:cs + 512],
                                         start=first, stop=(vi == lastv))
                                ins = e.matmul(psd[:1, cc:cs + 512], ones_bf[:],
                                               pt[:, cc:cs + 512],
                                               start=first, stop=(vi == lastv))
                                started[cs] = True
                            ins.then_inc(s_ct.h, 1)

                        for vi, (lk, j0, dg) in enumerate(visits):
                            if vi >= 2:
                                e.wait_ge(s_ex.h, sex0 + vi - 1)
                            sps = [ps3, ps4][vi % 2]
                            kt = kbuf(par)[:, lk * 128:(lk + 1) * 128]
                            c0 = j0 * 128
                            ins = None
                            for cs in (0, 512):
                                cc = max(cs, c0)
                                if cc >= cs + 512:
                                    continue
                                ins = e.matmul(sps[:, cc:cs + 512], kt,
                                               bufa[qh][:, cc:cs + 512],
                                               start=True, stop=True)
                            ins.then_inc(s_mm.h, 1)
                            if vi >= 1:
                                ctx_of(vi - 1)
                        ctx_of(nvis - 1)
                    P.emit("tensor", f_mm)
                    s_mm.v += nvis
                    s_ct.v += nvis

                    def f_exp(e, visits=visits, smm0=smm0, sct0=sct0, sex0=sex0):
                        for vi, (lk, j0, dg) in enumerate(visits):
                            e.wait_ge(s_mm.h, smm0 + vi + 1)
                            if vi >= 2 and sct0 + vi - 1 > 0:
                                e.wait_ge(s_ct.h, sct0 + vi - 1)
                            sps = [ps3, ps4][vi % 2]
                            c0 = j0 * 128
                            e.activation(psb[vi % 2][:, c0:OWN], sps[:, c0:OWN],
                                         AF.Exp, bias=b15[:], scale=SCALE).then_inc(s_ex.h, 1)
                    P.emit("scalar", f_exp)
                    s_ex.v += nvis

                    def f_msk(e, visits=visits, sex0=sex0):
                        for vi, (lk, j0, dg) in enumerate(visits):
                            if not dg:
                                continue
                            e.wait_ge(s_ex.h, sex0 + vi + 1)
                            c0 = j0 * 128
                            e.tensor_tensor(psb[vi % 2][:, c0:c0 + 128],
                                            psb[vi % 2][:, c0:c0 + 128],
                                            tri[:], ALU.mult).then_inc(s_mk.h, 1)
                    P.emit("vector", f_msk)
                    s_mk.v += nmask_thru[-1]

                    def f_den(e, ctv=s_ct.v):
                        e.wait_ge(s_ct.h, ctv)
                        e.copy(den_sb[:1, :], psd[:1, :]).then_inc(s_sq.h, 1)
                    P.emit("scalar", f_den)
                    s_sq.v += 1

                    def f_bc(e, sqv=s_sq.v):
                        e.wait_ge(s_sq.h, sqv)
                        e.matmul(ps3[:, 0:512], ones_f[:], den_sb[:1, 0:512],
                                 start=True, stop=True)
                        e.matmul(ps4[:, 0:512], ones_f[:], den_sb[:1, 512:1024],
                                 start=True, stop=True).then_inc(s_mm.h, 1)
                    P.emit("tensor", f_bc)
                    s_mm.v += 1

                    def f_ev(e, qh=qh, mmv=s_mm.v):
                        e.wait_ge(s_mm.h, mmv)
                        e.reciprocal(recf[0][:], ps3[:, 0:512])
                        e.reciprocal(recf[1][:], ps4[:, 0:512]).then_inc(s_ev.h, 1)
                        e.tensor_tensor(bufb[qh][:, 0:512], ps1[:], recf[0][:], ALU.mult)
                        e.tensor_tensor(bufb[qh][:, 512:1024], ps2[:], recf[1][:], ALU.mult).then_inc(s_ev.h, 2)
                    P.emit("vector", f_ev)
                    s_ev.v += 3

                kvh_ct[kvh] = s_ct.v

            vals = [(sem, sem.v) for sem in ALL_SEMS if sem is not s_ph]
            for e in ENGS:
                def f(eng, vals=vals):
                    drain_waits(eng, vals)
                    eng.nop().then_inc(s_ph.h, 1)
                P.emit(e, f)

        def p4():
            tgt = s_ph.v + 5
            for q in range(4):
                snap = [s.v for s in ALL_SEMS]
                P.if_open(q)
                p4_variant(q)
                P.if_close()
                for s, v0 in zip(ALL_SEMS, snap):
                    s.v = v0
            s_ph.v = tgt
            for e in ENGS:
                def f(eng, tgt=tgt):
                    eng.wait_ge(s_ph.h, tgt)
                P.emit(e, f)

        def wos(par, c):
            return wbig[:, 8192 + par * 4096 + c * 128: 8192 + par * 4096 + (c + 1) * 128]

        wt_end = {}

        def p_wo():
            wmm_end = {}
            for och in range(NH):
                par = och % 2

                def f_ld(e, och=och, par=par, mmv=wmm_end.get(och - 2, 0)):
                    if mmv > 0:
                        e.wait_ge(s_mm2.h, mmv)
                    for g in range(8):
                        src = wo.rearrange("(c p) o -> p c o", p=128)[
                            :, g * 4:(g + 1) * 4, och * 128:(och + 1) * 128]
                        dst = bass.AP(wbig[:].tensor, 8192 + par * 4096 + (g * 4) * 128,
                                      [[WBC, 128], [128, 4], [1, 128]])
                        e.dma_start(out=dst, in_=src).then_inc(s_wop[par].h, 16)
                    e.dma_start(out=bufa[och],
                                in_=xT[och * 128:(och + 1) * 128, 0:OWN]).then_inc(s_wop[par].h, 16)
                P.emit("gpsimd", f_ld)
                s_wop[par].v += 16 * 9

                mm0 = s_mm2.v

                def f_mm(e, och=och, par=par, wtv=s_wop[par].v, ev0=s_ev2.v):
                    e.wait_ge(s_wop[par].h, wtv)
                    for hf in range(2):
                        i = och * 2 + hf
                        tgt = ev0 + hf - 3
                        if tgt > 0:
                            e.wait_ge(s_ev2.h, tgt)
                        pspc = [ps1[:], ps2[:], ps3[:, 0:512], ps4[:, 0:512]][i % 4]
                        for c in range(NC):
                            ins = e.matmul(pspc, wos(par, c),
                                           bufb[c][:, hf * 512:(hf + 1) * 512],
                                           start=(c == 0), stop=(c == NC - 1))
                        ins.then_inc(s_mm2.h, 1)
                P.emit("tensor", f_mm)
                s_mm2.v += 2
                wmm_end[och] = s_mm2.v

                def f_ev(e, och=och, mm0=mm0):
                    for hf in range(2):
                        i = och * 2 + hf
                        pspc = [ps1[:], ps2[:], ps3[:, 0:512], ps4[:, 0:512]][i % 4]
                        e.wait_ge(s_mm2.h, mm0 + hf + 1)
                        sl = bufa[och][:, hf * 512:(hf + 1) * 512]
                        e.tensor_tensor(sl, sl, pspc, ALU.add).then_inc(s_ev2.h, 1)
                P.emit("vector", f_ev)
                s_ev2.v += 2

                if och == 1:
                    def f_ldw0(e, sw0=s_swp[0].v, sw1=s_swp[1].v):
                        if sw0 > 0:
                            e.wait_ge(s_swp[0].h, sw0)
                        if sw1 > 0:
                            e.wait_ge(s_swp[1].h, sw1)
                        for ic in range(NC):
                            dst = wbig[:, ic * 256:(ic + 1) * 256]
                            e.dma_start_transpose(dst,
                                                  w_scr[0:256, ic * 128:(ic + 1) * 128]).then_inc(s_ftp[0].h, 16)
                    P.emit("sync", f_ldw0)
                    s_ftp[0].v += 16 * NC
                    wt_end[0] = s_ftp[0].v

            bar()

        def p5():
            for hf in range(2):
                ln_stats(lambda c, hf=hf: bufa[c][:, hf * 512:(hf + 1) * 512],
                         s_ph, s_ph.v, s_mm2, s_sq2, s_st2)

                def f_h2(e, hf=hf, sqv=s_sq2.v):
                    e.wait_ge(s_sq2.h, sqv)
                    for c in range(NC):
                        sl_in = bufa[c][:, hf * 512:(hf + 1) * 512]
                        sl_out = bufb[c][:, hf * 512:(hf + 1) * 512]
                        e.tensor_tensor(s1f[:], sl_in, m_rep[:], ALU.subtract)
                        e.tensor_tensor(sl_out, s1f[:], r_rep[:], ALU.mult).then_inc(s_h2.h, 1)
                P.emit("vector", f_h2)
                s_h2.v += NC

            bar()

        def p7_pre():
            def f_ldw1(e, sw0=s_swp[0].v, sw1=s_swp[1].v):
                if sw0 > 0:
                    e.wait_ge(s_swp[0].h, sw0)
                if sw1 > 0:
                    e.wait_ge(s_swp[1].h, sw1)
                for ic in range(NC):
                    dst = wbig[:, 8192 + ic * 256: 8192 + (ic + 1) * 256]
                    e.dma_start_transpose(dst,
                                          w_scr[256:512, ic * 128:(ic + 1) * 128]).then_inc(s_ftp[1].h, 16)
            P.emit("sync", f_ldw1)
            s_ftp[1].v += 16 * NC
            wt_end[1] = s_ftp[1].v

        def p7():
            p7_ev0 = s_ev2.v
            ffn_end = {}
            for og in range(16):
                par = og % 2
                if og >= 2:
                    def f_ldw(e, og=og, par=par, mmv=ffn_end.get(og - 2, 0)):
                        if mmv > 0:
                            e.wait_ge(s_mm2.h, mmv)
                        for ic in range(NC):
                            dst = wbig[:, par * 8192 + ic * 256: par * 8192 + (ic + 1) * 256]
                            e.dma_start_transpose(dst,
                                                  w_scr[og * 256:(og + 1) * 256,
                                                        ic * 128:(ic + 1) * 128]).then_inc(s_ftp[og % 2].h, 16)
                    P.emit("sync", f_ldw)
                    s_ftp[og % 2].v += 16 * NC
                    wt_end[og] = s_ftp[og % 2].v

                for oc2 in range(2):
                    och = og * 2 + oc2
                    mm0 = s_mm2.v

                    def f_mm(e, par=par, oc2=oc2, och=och, wtv=wt_end[og], p7_ev0=p7_ev0):
                        e.wait_ge(s_ftp[par].h, wtv)
                        for hf in range(2):
                            i = och * 2 + hf
                            tgt = p7_ev0 + i - 3
                            if i >= 4:
                                e.wait_ge(s_ev2.h, tgt)
                            pspc = [ps1[:], ps2[:], ps3[:, 0:512], ps4[:, 0:512]][i % 4]
                            for ic in range(NC):
                                lhs = wbig[:, par * 8192 + ic * 256 + oc2 * 128:
                                           par * 8192 + ic * 256 + oc2 * 128 + 128]
                                ins = e.matmul(pspc, lhs,
                                               bufb[ic][:, hf * 512:(hf + 1) * 512],
                                               start=(ic == 0), stop=(ic == NC - 1))
                            ins.then_inc(s_mm2.h, 1)
                    P.emit("tensor", f_mm)
                    s_mm2.v += 2
                    ffn_end[og] = s_mm2.v

                    def f_ep(e, och=och, mm0=mm0, outv=s_out.v):
                        for hf in range(2):
                            i = och * 2 + hf
                            pspc = [ps1[:], ps2[:], ps3[:, 0:512], ps4[:, 0:512]][i % 4]
                            e.wait_ge(s_mm2.h, mm0 + hf + 1)
                            if i >= 2:
                                e.wait_ge(s_out.h, 16 * (i - 1))
                            e.scalar_tensor_tensor(outt[hf][:], pspc,
                                                   bias_sb[:, och:och + 1],
                                                   bufa[och][:, hf * 512:(hf + 1) * 512],
                                                   ALU.add, ALU.add).then_inc(s_ev2.h, 1)
                    P.emit("vector", f_ep)
                    s_ev2.v += 2

                    def f_st(e, och=och, evv=s_ev2.v):
                        for hf in range(2):
                            e.wait_ge(s_ev2.h, evv - 1 + hf)
                            e.dma_start(out=out[och * 128:(och + 1) * 128,
                                                hf * 512:(hf + 1) * 512],
                                        in_=outt[hf][:]).then_inc(s_out.h, 16)
                    P.emit("gpsimd", f_st)
                    s_out.v += 32

            def f_fin(e, ov=s_out.v):
                e.wait_ge(s_out.h, ov)
            P.emit("sync", f_fin)

        def dump_scr():
            if not DEBUG:
                return
            def f(e, xv=s_x2.v):
                e.dma_start(out=dbg["d_kscr"], in_=k_scr).then_inc(s_x2.h, 16)
                e.dma_start(out=dbg["d_vscr"], in_=v_scr).then_inc(s_x2.h, 16)
                e.dma_start(out=dbg["d_hscr"], in_=h_scr).then_inc(s_x2.h, 16)
                e.dma_start(out=dbg["d_wscr"], in_=w_scr).then_inc(s_x2.h, 16)
                e.wait_ge(s_x2.h, xv + 64)
            P.emit("gpsimd", f)
            s_x2.v += 64
            bar()

        def dump_sbuf(key, tiles):
            if not DEBUG:
                return
            def f(e, key=key, tiles=tiles, base=s_x2.v):
                for c in range(NC):
                    e.dma_start(out=dbg[key][c * 128:(c + 1) * 128, :],
                                in_=tiles[c]).then_inc(s_x2.h, 16)
                e.wait_ge(s_x2.h, base + 16 * NC)
            P.emit("gpsimd", f)
            s_x2.v += 16 * NC
            bar()

        NPH = int(os.environ.get("KPHASES", "10"))
        phases = [lambda: pass_a(0), lambda: pass_a(1), lambda: pass_b(0),
                  lambda: pass_b(1), p3, p4, p_wo, p7_pre, p5, p7]
        names = ["A1", "A2", "B1", "B2", "P3", "P4", "WO", "PRE", "P5", "P7"]
        for i in range(min(NPH, 10)):
            phases[i]()
            if DEBUG:
                if names[i] == "P3":
                    dump_sbuf("d_q", bufa)
                elif names[i] == "P4":
                    dump_sbuf("d_ctx", bufb)
                elif names[i] == "WO":
                    dump_sbuf("d_x2", bufa)
                elif names[i] == "P5":
                    dump_sbuf("d_h2", bufb)
        if DEBUG:
            dump_scr()
        if NPH < 10:
            def f_dummy(e):
                e.memset(outt[0][:], 0.0).then_inc(s_st2.h, 1)
            P.emit("vector", f_dummy)
            s_st2.v += 1

            def f_fin0(e, stv=s_st2.v, ov=s_out.v):
                e.wait_ge(s_st2.h, stv)
                e.dma_start(out=out[0:128, 0:512], in_=outt[0][:]).then_inc(s_out.h, 16)
                e.wait_ge(s_out.h, ov + 16)
            P.emit("sync", f_fin0)
            s_out.v += 16

        block = es.enter_context(nc.Block())
        block.sync(lambda e: P.replay("sync", e))
        block.gpsimd(lambda e: P.replay("gpsimd", e))
        block.vector(lambda e: P.replay("vector", e))
        block.scalar(lambda e: P.replay("scalar", e))
        block.tensor(lambda e: P.replay("tensor", e))

    return nc


def _perm_for(q):
    A, B = q, 7 - q
    own = list(range(A * 512, (A + 1) * 512)) + list(range(B * 512, (B + 1) * 512))
    own_set = set(own)
    others = [t for t in range(T) if t not in own_set]
    return np.array(own + others, dtype=np.int64)


def _rope_tables(perm):
    inv = 1.0 / (500000.0 ** (np.arange(0, HD, 2, dtype=np.float64) / HD))
    t = np.arange(T, dtype=np.float64)
    fr = np.outer(t, inv)
    cos = np.cos(fr).astype(np.float32)[perm]
    sin = np.sin(fr).astype(np.float32)[perm]
    t1 = np.concatenate([cos.T, cos.T], axis=0).astype(ml_dtypes.bfloat16)
    t2 = np.concatenate([-sin.T, sin.T], axis=0).astype(ml_dtypes.bfloat16)
    return np.ascontiguousarray(t1), np.ascontiguousarray(t2)


_NC_CACHE = {}


def kernel(L0_attn_x, L0_attn_Wq, L0_attn_Wk, L0_attn_Wv, L0_attn_Wo,
           L0_ffn_weight_packed, L0_ffn_weight_scale, L0_ffn_weight_var_mask,
           L0_ffn_bias):
    x = np.asarray(L0_attn_x, dtype=np.float32)
    wq = np.ascontiguousarray(np.asarray(L0_attn_Wq, dtype=np.float32))
    wk = np.ascontiguousarray(np.asarray(L0_attn_Wk, dtype=np.float32))
    wv = np.ascontiguousarray(np.asarray(L0_attn_Wv, dtype=np.float32))
    wo = np.ascontiguousarray(np.asarray(L0_attn_Wo, dtype=np.float32))
    pk_a = np.ascontiguousarray(np.asarray(L0_ffn_weight_packed, dtype=np.int32).reshape(4096, 2048))
    sc_a = np.ascontiguousarray(np.asarray(L0_ffn_weight_scale).reshape(4096, 128))
    mk_a = np.ascontiguousarray(np.asarray(L0_ffn_weight_var_mask).reshape(4096, 128).view(np.uint8))
    bias_pp = np.ascontiguousarray(np.asarray(L0_ffn_bias, dtype=np.float32).reshape(32, 128).T)

    tri = np.zeros((128, 128), dtype=ml_dtypes.bfloat16)
    for kk in range(128):
        tri[kk, kk:] = 1.0
    swpm = np.zeros((128, 128), dtype=ml_dtypes.bfloat16)
    for m in range(128):
        swpm[(m + 64) % 128, m] = 1.0

    if "nc" not in _NC_CACHE:
        _NC_CACHE["nc"] = build_nc()
    nc = _NC_CACHE["nc"]

    in_maps = []
    perms = []
    for c in range(8):
        b, q = c // 4, c % 4
        perm = _perm_for(q)
        perms.append(perm)
        xTc = np.ascontiguousarray(x[b][perm].T)
        t1, t2 = _rope_tables(perm)
        in_maps.append(dict(
            xT=xTc, wq=wq, wk=wk, wv=wv, wo=wo, pk=pk_a, sc=sc_a, mk=mk_a,
            bias_in=bias_pp, t1_in=t1, t2_in=t2, tri_in=tri, swp_in=swpm,
        ))

    res = run_bass_kernel_spmd(nc, in_maps, core_ids=list(range(8)))

    outa = np.empty((2, T, D), dtype=np.float32)
    for c in range(8):
        b = c // 4
        o = np.asarray(res.results[c]["o"])
        outa[b][perms[c][:OWN]] = o.T
    if DEBUG:
        kernel.last_res = res
        kernel.last_perms = perms
    return outa
